# version 31
# baseline (speedup 1.0000x reference)
import numpy as np
import concourse.bass as bass
import concourse.mybir as mybir
import bass_rust
from concourse.bass_utils import run_bass_kernel_spmd

F32 = mybir.dt.float32
BF16 = mybir.dt.bfloat16
AF = mybir.ActivationFunctionType
ALU = mybir.AluOpType
AX = mybir.AxisListType.X
MAGIC = 1.5 * 2.0 ** 23
REDMAX = bass_rust.ReduceOp.max

C = 1024
T = 2048
HD = 64
SPAN = 256
NSPAN = T // SPAN
CHUNK = 64
CPS = SPAN // CHUNK
NPT = C // 128
PS = NPT * SPAN
EPS_LN = 1e-5
EPS_GN = 0.00064
EPS_FFN = 1e-8

ENGINES = ("tensor", "vector", "scalar", "gpsimd", "sync")


class Prog:
    NDMASLOT = 4

    def __init__(self):
        self.ops = {e: [] for e in ENGINES}
        self.rcount = {}
        self.riders = {}
        self.tags = {}
        self.waited = {}
        self.dslot = {e: 0 for e in ENGINES}

    def _cover(self, peng, pdom, pidx):
        lst = self.riders.setdefault(pdom, [])
        for i, v in lst:
            if i >= pidx:
                return v
        isdma = pdom.startswith("dma_")
        ops = self.ops[peng]
        j = len(ops) - 1
        while ops[j]["dma"] != isdma or ops[j]["rv"] is not None:
            j -= 1
        op = ops[j]
        assert op["idx"] >= pidx, (peng, pdom, pidx)
        inc = 16 if pdom.startswith("dma_") else 1
        val = self.rcount.get(pdom, 0) + inc
        self.rcount[pdom] = val
        op["rv"] = val
        op["rdom"] = pdom
        op["rinc"] = inc
        lst.append((op["idx"], val))
        return val

    def emit(self, eng, fn, reads=(), writes=(), accum=False, dma=False):
        if dma:
            slot = self.dslot[eng]
            self.dslot[eng] = (slot + 1) % self.NDMASLOT
            dom = f"dma_{eng}{slot}"
        else:
            dom = eng
        ops = self.ops[eng]
        op = {"fn": fn, "waits": [], "rv": None, "rdom": None, "rinc": None,
              "idx": len(ops), "dma": dma}
        needs = {}

        def need(rec):
            k = (rec[0], rec[1])
            if needs.get(k, -1) < rec[2]:
                needs[k] = rec[2]

        for t in reads:
            rec = self.tags.get(t)
            if rec:
                for w in rec["w"]:
                    need(w)
        for t in writes:
            rec = self.tags.get(t)
            if rec:
                for w in rec["w"]:
                    if accum and w[0] == "tensor" and eng == "tensor":
                        continue
                    need(w)
                for r in rec["r"].values():
                    need(r)
        for (peng, pdom), pidx in sorted(needs.items(), key=lambda kv: (kv[0][0], kv[0][1])):
            if peng == eng and pdom == dom and pidx == op["idx"]:
                continue
            val = self._cover(peng, pdom, pidx)
            key = (eng, pdom)
            if self.waited.get(key, 0) < val:
                self.waited[key] = val
                op["waits"].append((pdom, val))
        me = (eng, dom, op["idx"])
        for t in writes:
            rec = self.tags.get(t)
            if rec is None or not accum:
                self.tags[t] = {"w": [me], "r": {}}
            else:
                rec["w"].append(me)
                rec["r"] = {}
        for t in reads:
            rec = self.tags.setdefault(t, {"w": [], "r": {}})
            rec["r"][(eng, dom)] = me
        ops.append(op)
        if dma:
            prev = self.rcount.get(dom, 0)
            key = (eng, dom)
            if prev > 0 and self.waited.get(key, 0) < prev:
                self.waited[key] = prev
                op["waits"].append((dom, prev))
            val = prev + 16
            self.rcount[dom] = val
            op["rv"] = val
            op["rdom"] = dom
            op["rinc"] = 16
            self.riders.setdefault(dom, []).append((op["idx"], val))
        return op

    def finalize(self, nc):
        import contextlib
        with contextlib.ExitStack() as stack:
            doms = sorted(set(self.rcount.keys()) | set(ENGINES))
            sems = {d: stack.enter_context(nc.semaphore("sem_" + d)) for d in doms}
            block = stack.enter_context(nc.Block())
            for eng in ENGINES:
                ops = self.ops[eng]
                if not ops:
                    continue

                def make_body(ops):
                    def body(e):
                        for op in ops:
                            for (pdom, val) in op["waits"]:
                                e.wait_ge(sems[pdom], val)
                            ins = op["fn"](e)
                            if op["rv"] is not None:
                                ins.then_inc(sems[op["rdom"]], op["rinc"])
                    return body
                getattr(block, eng)(make_body(ops))


def build_program():
    nc = bass.Bass("TRN2", target_bir_lowering=False)
    p = Prog()

    dx = nc.dram_tensor("x", [NSPAN, 128, PS], F32, kind="ExternalInput").ap()
    dws = {}
    for nm, shp in (("wr", [128, NPT * C]), ("wk", [128, NPT * C]),
                    ("wv", [128, NPT * C]), ("wo", [128, NPT * C]),
                    ("wq", [128, NPT * C]), ("w1", [128, NPT * 64]), ("w2", [64, C]),
                    ("a1", [128, NPT * 64]), ("a2", [64, C]), ("g1", [128, NPT * 128]),
                    ("g2", [128, C]), ("wmod", [128, 48 * C]), ("cmod", [C, 1])):
        dws[nm] = nc.dram_tensor(nm, shp, F32, kind="ExternalInput").ap()
    NV = 13
    dvecs = nc.dram_tensor("vecs", [NV, C], F32, kind="ExternalInput").ap()
    dout = nc.dram_tensor("out", [NSPAN, 128, PS], F32, kind="ExternalOutput").ap()

    def sb(name, cols, dt=BF16):
        return nc.alloc_sbuf_tensor(name, [128, cols], dt).ap()

    wr_b = sb("wr_b", NPT * C)
    wk_b = sb("wk_b", NPT * C)
    wv_b = sb("wv_b", NPT * C)
    wo_b = sb("wo_b", NPT * C)
    wq_b = sb("wq_b", NPT * C)
    w1_b = sb("w1_b", NPT * 64)
    a1_b = sb("a1_b", NPT * 64)
    g1_b = sb("g1_b", NPT * 128)
    w2_b = sb("w2_b", C)
    a2_b = w2_b
    g2_b = sb("g2_b", C)
    stage = sb("stage", C, F32)

    vecs = sb("vecs_s", NV * NPT, F32)
    silcb = sb("silcb", NPT)
    silcf = sb("silcf", NPT, F32)
    modv = sb("modv", 6 * NPT, F32)
    ca1 = sb("ca1", NPT, F32)
    cb1 = sb("cb1", NPT, F32)
    gav = sb("gav", NPT, F32)
    gbv = sb("gbv", NPT, F32)
    w0n = sb("w0n", NPT, F32)
    fvec = sb("fvec", 8, F32)
    wsc = sb("wsc", 4, F32)

    fvb = sb("fvb", 4)
    maskTP = sb("maskTP", 128)
    maskLO = sb("maskLO", 64)
    identb = sb("identb", 128)
    identf = sb("identf", 128, F32)
    onescol = sb("onescol", 1)
    bones16 = sb("bones16", NPT * 16)
    bcast1 = sb("bcast1", 128)
    bsel = sb("bsel", NPT * 128)
    zer64 = sb("zer64", 64, F32)
    zerb = sb("zerb", 1)

    stS = sb("stS", 8 * HD, F32)
    stSb = sb("stSb", 8 * HD)

    x_a = sb("x_a", PS, F32)
    slog = sb("slog", PS, F32)
    expw = sb("expw", PS, F32)
    tmpf = sb("tmpf", SPAN, F32)
    tmpf2 = sb("tmpf2", SPAN, F32)
    tmpb = sb("tmpb", SPAN)

    h_sp = sb("h_sp", NPT * (SPAN + 1))
    hcarry = sb("hcarry", NPT)
    xx_sp = sb("xx_sp", PS)
    mixA = sb("mixA", PS)
    mixB = sb("mixB", PS)
    xb_sp = sb("xb_sp", PS)
    xsq_sp = sb("xsq_sp", PS)
    ar_sp = sb("ar_sp", 2 * PS)
    bh_sp = sb("bh_sp", PS)
    kh_sp = sb("kh_sp", PS)
    kk_sp = sb("kk_sp", PS)
    kkg_sp = sb("kkg_sp", PS)
    kf_sp = sb("kf_sp", PS)
    asig_sp = sb("asig_sp", PS)
    g_sp = sb("g_sp", PS)
    c_sp = sb("c_sp", PS)
    ic_sp = sb("ic_sp", PS)
    cp_sp = sb("cp_sp", PS)
    esuf_sp = sb("esuf_sp", PS)
    lw_t = sb("lw_t", SPAN)
    lg_t = None
    rows = sb("rows", 768, F32)
    rowsb = sb("rowsb", 512)
    ascb = None

    btF_sp = kk_sp
    ktF_sp = xx_sp
    vF_sp = asig_sp
    vT_sp = xb_sp[:, 0:2 * C]
    o_sp = ic_sp
    y_sp = cp_sp
    xs_a = cp_sp[:, 0:1024]
    xs_b = cp_sp[:, 1024:2048]
    tap_sb = h_sp[:, 0:1024]
    pkk_sb = h_sp[:, 1024:2048]
    tn_sb = kkg_sp[:, 0:512]
    btT_cp = kkg_sp[:, 512:1024]
    ktT_cp = kkg_sp[:, 1024:1536]
    gG_cp = kkg_sp[:, 1536:1792]
    rEf_cp = kkg_sp[:, 1792:2048]
    ascb = tmpf2
    lg_t = lw_t
    slogb = slog.bitcast(BF16)
    expwb = expw.bitcast(BF16)
    clrep = stage.bitcast(BF16)[:, 0:CPS * 512]
    wmb = rows.bitcast(BF16)[:, 0:C]

    def psum(name, cols, dt=F32):
        return nc.alloc_psum_tensor(name, [128, cols], dt).ap()

    pb = [psum(f"pb{i}", 512) for i in range(8)]
    pbtag = [f"pb{i}" for i in range(8)]
    _pbidx = {f"pb{i}": i for i in range(8)}

    granges = [dict() for _ in range(8)]
    grrg = [dict() for _ in range(8)]

    def newgrp(*banks):
        for b in banks:
            for cell in granges[b].values():
                cell[0] = True
            granges[b].clear()

    _orig_emit = p.emit

    def _emit_hook(eng, fn, reads=(), writes=(), accum=False, dma=False):
        if eng in ("vector", "scalar"):
            for t in reads:
                if t in _pbidx and granges[_pbidx[t]]:
                    newgrp(_pbidx[t])
        return _orig_emit(eng, fn, reads=reads, writes=writes, accum=accum, dma=dma)
    p.emit = _emit_hook

    def t_mm(bi, out, lhsT, rhs, r, w_extra=(), tp=None):
        rng = (int(out.base_partition()), int(out.partition_size()))
        for (b0, sz) in granges[bi]:
            assert (b0, sz) == rng or b0 + sz <= rng[0] or rng[0] + rng[1] <= b0, \
                (bi, rng, (b0, sz))
        rg = int(lhsT.base_partition()) if tp is None else tp[0]
        first = rng not in granges[bi]
        if not first:
            assert grrg[bi][rng] == rg, (bi, rng, grrg[bi][rng], rg)
        cell = [False]
        granges[bi][rng] = cell
        grrg[bi][rng] = rg
        import traceback
        op = p.emit("tensor",
               lambda e, o=out, l=lhsT, rr=rhs, s1=first, t=tp, c=cell:
               e.matmul(o, l, rr, start=s1, stop=c[0], tile_position=t,
                        skip_group_check=True),
               reads=r, writes=[pbtag[bi]] + list(w_extra), accum=not first)
        op["dbg"] = (bi, rng, first, cell,
                     traceback.extract_stack(limit=3)[0].lineno)

    def v_tt(out, in0, in1, op, r, w, eng="vector"):
        p.emit(eng, lambda e, o=out, a=in0, b=in1, op=op:
               e.tensor_tensor(out=o, in0=a, in1=b, op=op), reads=r, writes=w)

    def v_ts(out, in0, s1, s2, op0, op1, r, w, eng="vector"):
        if s2 is None:
            p.emit(eng, lambda e, o=out, a=in0, s1=s1, op0=op0:
                   e.tensor_scalar(out=o, in0=a, scalar1=s1, scalar2=None, op0=op0),
                   reads=r, writes=w)
        else:
            p.emit(eng, lambda e, o=out, a=in0, s1=s1, s2=s2, op0=op0, op1=op1:
                   e.tensor_scalar(out=o, in0=a, scalar1=s1, scalar2=s2, op0=op0, op1=op1),
                   reads=r, writes=w)

    def v_stt(out, in0, scalar, in1, op0, op1, r, w, eng="vector"):
        p.emit(eng, lambda e, o=out, a=in0, s=scalar, b=in1, op0=op0, op1=op1:
               e.scalar_tensor_tensor(out=o, in0=a, scalar=s, in1=b, op0=op0, op1=op1),
               reads=r, writes=w)

    def v_copy(out, in_, r, w, eng="vector"):
        p.emit(eng, lambda e, o=out, a=in_: e.tensor_copy(out=o, in_=a), reads=r, writes=w)

    def s_copy(out, in_, r, w):
        p.emit("scalar", lambda e, o=out, a=in_: e.copy(out=o, in_=a), reads=r, writes=w)

    def s_act(out, in_, func, r, w, bias=0.0, scale=1.0):
        p.emit("scalar", lambda e, o=out, a=in_, f=func, bi=bias, sc=scale:
               e.activation(out=o, in_=a, func=f, bias=bi, scale=sc), reads=r, writes=w)

    def g_memset(ap, val, w, r=(), accum=False):
        p.emit("gpsimd", lambda e, a=ap, v=val: e.memset(a, v), reads=r, writes=w, accum=accum)

    def dma(eng, out, in_, r, w, slow=False):
        p.emit(eng, lambda e, o=out, a=in_, sl=slow: e.dma_start(
            out=o, in_=a, allow_slow_non_contiguous=sl),
            reads=r, writes=w, dma=True)

    g_memset(maskTP[:, :], 1.0, ["maskTP"])
    for half in (0, 64):
        p.emit("gpsimd", lambda e, hh=half: e.affine_select(
            out=maskTP[hh:hh + 64, 0:64], in_=maskTP[hh:hh + 64, 0:64],
            compare_op=ALU.is_gt, fill=0.0, base=0, pattern=[[1, 64]], channel_multiplier=-1),
            reads=["maskTP"], writes=["maskTP"])
        p.emit("gpsimd", lambda e, hh=half: e.affine_select(
            out=maskTP[hh:hh + 64, 64:128], in_=maskTP[hh:hh + 64, 64:128],
            compare_op=ALU.is_ge, fill=0.0, base=0, pattern=[[1, 64]], channel_multiplier=-1),
            reads=["maskTP"], writes=["maskTP"])
    g_memset(maskLO[:, :], 1.0, ["maskLO"])
    for half in (0, 64):
        p.emit("gpsimd", lambda e, hh=half: e.affine_select(
            out=maskLO[hh:hh + 64, 0:64], in_=maskLO[hh:hh + 64, 0:64],
            compare_op=ALU.is_gt, fill=0.0, base=0, pattern=[[-1, 64]], channel_multiplier=1),
            reads=["maskLO"], writes=["maskLO"])
    g_memset(identb[:, :], 1.0, ["identb"])
    g_memset(identf[:, :], 1.0, ["identf"])
    p.emit("gpsimd", lambda e: e.affine_select(
        out=identf[:, :], in_=identf[:, :], compare_op=ALU.is_ge, fill=0.0,
        base=0, pattern=[[-1, 128]], channel_multiplier=1), reads=["identf"], writes=["identf"])
    p.emit("gpsimd", lambda e: e.affine_select(
        out=identf[:, :], in_=identf[:, :], compare_op=ALU.is_ge, fill=0.0,
        base=0, pattern=[[1, 128]], channel_multiplier=-1), reads=["identf"], writes=["identf"])
    p.emit("gpsimd", lambda e: e.affine_select(
        out=identb[:, :], in_=identb[:, :], compare_op=ALU.is_ge, fill=0.0,
        base=0, pattern=[[-1, 128]], channel_multiplier=1), reads=["identb"], writes=["identb"])
    p.emit("gpsimd", lambda e: e.affine_select(
        out=identb[:, :], in_=identb[:, :], compare_op=ALU.is_ge, fill=0.0,
        base=0, pattern=[[1, 128]], channel_multiplier=-1), reads=["identb"], writes=["identb"])
    g_memset(onescol[:, :], 1.0, ["onescol"])
    g_memset(bones16[:, :], 0.0, ["bones16"])
    for q in range(NPT):
        g_memset(bones16[0:64, q * 16 + 2 * q: q * 16 + 2 * q + 1], 1.0,
                 ["bones16"], ["bones16"], accum=True)
        g_memset(bones16[64:128, q * 16 + 2 * q + 1: q * 16 + 2 * q + 2], 1.0,
                 ["bones16"], ["bones16"], accum=True)
    g_memset(bcast1[:, :], 0.0, ["bcast1"])
    g_memset(bcast1[0:1, :], 1.0, ["bcast1"], ["bcast1"], accum=True)
    g_memset(bsel[:, :], 1.0, ["bsel"])
    p.emit("gpsimd", lambda e: e.affine_select(
        out=bsel[0:16, :], in_=bsel[0:16, :], compare_op=ALU.is_ge, fill=0.0,
        base=0, pattern=[[1, NPT * 128]], channel_multiplier=-64),
        reads=["bsel"], writes=["bsel"])
    p.emit("gpsimd", lambda e: e.affine_select(
        out=bsel[0:16, :], in_=bsel[0:16, :], compare_op=ALU.is_ge, fill=0.0,
        base=63, pattern=[[-1, NPT * 128]], channel_multiplier=64),
        reads=["bsel"], writes=["bsel"])
    g_memset(zer64[:, :], 0.0, ["zer64"])
    g_memset(zerb[:, :], 0.0, ["zerb"])
    g_memset(fvec[:, 4:5], EPS_LN, ["cvec"])
    g_memset(fvec[:, 5:6], EPS_GN, ["cvec"])
    g_memset(fvec[:, 6:7], -0.5, ["cvec"])
    g_memset(stS[:, :], 0.0, ["stS"])
    g_memset(stSb[:, :], 0.0, ["stSb"])
    g_memset(h_sp[:, :], 0.0, ["h_sp"])

    for v in range(NV):
        dma("sync", vecs[:, v * NPT: (v + 1) * NPT],
            dvecs[v, :].rearrange("(q pp) -> pp q", pp=128), r=["vecs"], w=["vecs"],
            slow=True)
    for q in range(NPT):
        v_ts(w0n[:, q:q + 1], vecs[:, 6 * NPT + q: 6 * NPT + q + 1], -1.0, None,
             ALU.mult, None, r=["vecs"], w=["w0n"])

    dma("sync", silcf[:, :], dws["cmod"][:, 0].rearrange("(k pp) -> pp k", pp=128),
        r=["silcf"], w=["silcf"], slow=True)
    s_act(silcb[:, :], silcf[:, :], AF.Silu, r=["silcf"], w=["silcb"])

    wstg = [(stage[:, 0:C], "stage"), (expw[:, 0:C], "expwA"),
            (expw[:, C:2 * C], "expwB")]
    _wsi = [0]

    def wbuf():
        b = wstg[_wsi[0] % 3]
        _wsi[0] += 1
        return b

    def load_big(dsrc, dst_b, wtag):
        for k in range(NPT):
            buf, btag = wbuf()
            dma("gpsimd", buf, dsrc[:, k * C:(k + 1) * C], r=[btag], w=[btag])
            v_copy(dst_b[:, k * C:(k + 1) * C], buf, r=[btag], w=[wtag])

    load_big(dws["wr"], wr_b, "wr_b")
    load_big(dws["wk"], wk_b, "wk_b")
    load_big(dws["wv"], wv_b, "wv_b")
    load_big(dws["wo"], wo_b, "wo_b")
    for dsrc, dst_b, wtag, psl in (
            (dws["w1"], w1_b, "w1_b", slice(0, 512)),
            (dws["a1"], a1_b, "a1_b", slice(0, 512)),
            (dws["g1"], g1_b, "g1_b", slice(0, 1024)),
            (dws["g2"], g2_b, "g2_b", slice(0, 1024))):
        buf, btag = wbuf()
        dma("gpsimd", buf[:, psl], dsrc[:, :], r=[btag], w=[btag])
        v_copy(dst_b[:, :], buf[:, psl], r=[btag], w=[wtag])
    buf, btag = wbuf()
    dma("gpsimd", buf[0:64, 0:C], dws["w2"][:, :], r=[btag], w=[btag])
    v_copy(w2_b[0:64, :], buf[0:64, 0:C], r=[btag], w=["w2_b"])
    buf, btag = wbuf()
    dma("gpsimd", buf[64:128, 0:C], dws["a2"][:, :], r=[btag], w=[btag])
    v_copy(a2_b[64:128, :], buf[64:128, 0:C], r=[btag], w=["w2_b"])

    for k in range(NPT):
        buf, btag = wbuf()
        dma("gpsimd", buf, dws["wq"][:, k * C:(k + 1) * C], r=[btag], w=[btag])
        p.emit("vector", lambda e, k=k, buf=buf: e.reduce_sum(
            out=rows[0:128, 512 + k:513 + k], in_=buf, axis=AX,
            apply_absolute_value=True), reads=[btag], writes=["rows"], accum=(k > 0))
    p.emit("vector", lambda e: e.reduce_sum(out=rows[0:128, 520:521],
                                            in_=rows[0:128, 512:520], axis=AX),
           reads=["rows"], writes=["rows"], accum=True)
    v_copy(rowsb[0:128, 0:1], rows[0:128, 520:521], r=["rows"], w=["rowsb"])
    newgrp(6)
    t_mm(6, pb[6][0:1, 0:1], onescol[:, 0:1], rowsb[0:128, 0:1], r=["onescol", "rowsb"])
    v_ts(rows[0:1, 522:523], pb[6][0:1, 0:1], 1.0 / (C * C), EPS_FFN, ALU.mult, ALU.add,
         r=["pb6"], w=["rows"])
    s_act(rows[0:1, 523:524], rows[0:1, 522:523], AF.Ln, r=["rows"], w=["rows"])
    s_act(rows[0:1, 524:525], rows[0:1, 523:524], AF.Exp, r=["rows"], w=["rows"], scale=-1.0)
    dma("gpsimd", wsc[:, 0:1], rows[0:1, 522:523].unsqueeze(1).broadcast_to([1, 128, 1]),
        r=["rows", "wsc"], w=["wsc"])
    dma("gpsimd", wsc[:, 1:2], rows[0:1, 524:525].unsqueeze(1).broadcast_to([1, 128, 1]),
        r=["rows", "wsc"], w=["wsc"])
    for k in range(NPT):
        buf, btag = wbuf()
        dma("gpsimd", buf, dws["wq"][:, k * C:(k + 1) * C], r=[btag], w=[btag])
        v_ts(buf, buf, wsc[:, 1:2], None,
             ALU.mult, None, r=[btag, "wsc"], w=[btag])
        v_ts(buf, buf, MAGIC, MAGIC, ALU.add, ALU.subtract,
             r=[btag], w=[btag])
        v_ts(wq_b[:, k * C:(k + 1) * C], buf,
             -1.0, 1.0, ALU.max, ALU.min, r=[btag], w=["wq_b"])

    mstg = [(slog[:, 0:C], "slogA"), (slog[:, C:2 * C], "slogB")]
    mcst = [(wmb, "rows"), (xsq_sp.bitcast(BF16)[:, 0:C] if False else
             kk_sp[:, 0:C], "kk_sp")]
    for fm in range(48):
        buf, btag = mstg[fm % 2]
        cst, ctag = mcst[fm % 2]
        dma("sync", buf, dws["wmod"][:, fm * C:(fm + 1) * C], r=[btag], w=[btag])
        v_copy(cst, buf, r=[btag], w=[ctag])
        newgrp(6)
        for k in range(NPT):
            t_mm(6, pb[6][0:128, 4:5], cst[:, k * 128:(k + 1) * 128], silcb[:, k:k + 1],
                 r=[ctag, "silcb"])
        vi, q = fm // 8, fm % 8
        v_copy(modv[:, vi * NPT + q: vi * NPT + q + 1], pb[6][0:128, 4:5],
               r=["pb6"], w=["modv"])

    for q in range(NPT):
        v_ts(ca1[:, q:q + 1], modv[:, 1 * NPT + q: 1 * NPT + q + 1], 1.0, None,
             ALU.add, None, r=["modv"], w=["ca1"])
        v_ts(cb1[:, q:q + 1], modv[:, 4 * NPT + q: 4 * NPT + q + 1], 1.0, None,
             ALU.add, None, r=["modv"], w=["cb1"])
    for (vi, dst, dtag) in ((2, gav, "gav"), (5, gbv, "gbv")):
        for q in range(NPT):
            src = modv[:, vi * NPT + q: vi * NPT + q + 1]
            v_ts(fvec[:, 0:1], src, 0.0, None, ALU.is_ge, None, r=["modv"], w=["fvec"])
            v_ts(fvec[:, 1:2], src, 0.25, None, ALU.max, None, r=["modv"], w=["fvec"])
            v_ts(fvec[:, 2:3], src, -0.25, None, ALU.min, None, r=["modv"], w=["fvec"])
            v_tt(fvec[:, 3:4], fvec[:, 1:2], fvec[:, 2:3], ALU.subtract, r=["fvec"], w=["fvec"])
            v_tt(fvec[:, 3:4], fvec[:, 0:1], fvec[:, 3:4], ALU.mult, r=["fvec"], w=["fvec"])
            v_tt(dst[:, q:q + 1], fvec[:, 2:3], fvec[:, 3:4], ALU.add, r=["fvec"], w=[dtag])
    for q in range(NPT):
        v_ts(gbv[:, q:q + 1], gbv[:, q:q + 1], wsc[:, 0:1], 1.0 / 127.0,
             ALU.mult, ALU.mult, r=["gbv", "wsc"], w=["gbv"])

    g_memset(zerb[:, :], 0.0, ["zerb", "slog", "expw", "kk_sp"],
             ["slogA", "slogB", "expwA", "expwB"])

    def sp(q):
        return slice(q * SPAN, (q + 1) * SPAN)

    def hcol(q, t0):
        return q * (SPAN + 1) + 1 + t0

    for s in range(NSPAN):
        if s == 0:
            dma("sync", x_a[:, :], dx[0], r=["x_a"], w=["x_a"])

        v_copy(xb_sp[:, :], x_a[:, :], r=["x_a"], w=["xb_sp"])
        s_act(xsq_sp[:, :], x_a[:, :], AF.Square, r=["x_a"], w=["xsq_sp"])
        newgrp(7)
        for q in range(NPT):
            t_mm(7, pb[7][0:1, 0:256], onescol[:, 0:1], xb_sp[:, sp(q)],
                 r=["onescol", "xb_sp"])
        for q in range(NPT):
            t_mm(7, pb[7][0:1, 256:512], onescol[:, 0:1], xsq_sp[:, sp(q)],
                 r=["onescol", "xsq_sp"])

        def ln_rows(nparts, scale, epscol, rtag):
            v_ts(rows[0:nparts, 0:256], pb[7][0:nparts, 0:256], scale, None, ALU.mult,
                 None, r=[rtag], w=["rows"])
            v_ts(rows[0:nparts, 256:512], pb[7][0:nparts, 256:512], scale, None, ALU.mult,
                 None, r=[rtag], w=["rows"], eng="vector")
            v_tt(rows[0:nparts, 512:768], rows[0:nparts, 0:256], rows[0:nparts, 0:256],
                 ALU.mult, r=["rows"], w=["rows"])
            v_tt(rows[0:nparts, 512:768], rows[0:nparts, 256:512], rows[0:nparts, 512:768],
                 ALU.subtract, r=["rows"], w=["rows"])
            p.emit("scalar", lambda e, nparts=nparts, epscol=epscol: e.activation(
                out=rows[0:nparts, 512:768], in_=rows[0:nparts, 512:768], func=AF.Ln,
                bias=fvec[0:nparts, epscol:epscol + 1], scale=1.0),
                reads=["rows", "cvec"], writes=["rows"])
            s_act(rows[0:nparts, 512:768], rows[0:nparts, 512:768], AF.Exp,
                  r=["rows"], w=["rows"], scale=-0.5)
            v_tt(rows[0:nparts, 256:512], rows[0:nparts, 0:256], rows[0:nparts, 512:768],
                 ALU.mult, r=["rows"], w=["rows"])
            v_ts(rows[0:nparts, 256:512], rows[0:nparts, 256:512], -1.0, None, ALU.mult,
                 None, r=["rows"], w=["rows"])
            v_copy(rowsb[0:nparts, 0:256], rows[0:nparts, 512:768], r=["rows"], w=["rowsb"])
            v_copy(rowsb[0:nparts, 256:512], rows[0:nparts, 256:512], r=["rows"], w=["rowsb"])

        ln_rows(1, 1.0 / C, 4, "pb7")
        newgrp(6)
        t_mm(6, pb[6][:, 0:512], bcast1[0:1, :], rowsb[0:1, 0:512], r=["bcast1", "rowsb"])

        if s > 0:
            p.emit("vector", lambda e: e.tensor_copy(
                out=h_sp[:, :].rearrange("pp (q t) -> pp q t", t=SPAN + 1)[:, :, 0:1],
                in_=h_sp[:, :].rearrange("pp (q t) -> pp q t", t=SPAN + 1)[:, :, SPAN:SPAN + 1]),
                reads=["h_sp"], writes=["h_sp"])
        p.emit("vector", lambda e: e.tensor_tensor(
            out=slog[:, :].rearrange("pp (q t) -> pp q t", t=SPAN),
            in0=x_a[:, :].rearrange("pp (q t) -> pp q t", t=SPAN),
            in1=pb[6][:, 0:256].unsqueeze(1).broadcast_to([128, NPT, 256]),
            op=ALU.mult), reads=["x_a", "pb6"], writes=["slog"])
        p.emit("vector", lambda e: e.tensor_tensor(
            out=slog[:, :].rearrange("pp (q t) -> pp q t", t=SPAN),
            in0=slog[:, :].rearrange("pp (q t) -> pp q t", t=SPAN),
            in1=pb[6][:, 256:512].unsqueeze(1).broadcast_to([128, NPT, 256]),
            op=ALU.add), reads=["slog", "pb6"], writes=["slog"])
        for q in range(NPT):
            v_ts(h_sp[:, hcol(q, 0):hcol(q, SPAN)], slog[:, sp(q)], ca1[:, q:q + 1],
                 modv[:, 0 * NPT + q:0 * NPT + q + 1], ALU.mult, ALU.add,
                 r=["slog", "ca1", "modv"], w=["h_sp"])
        h3 = h_sp[:, :].rearrange("pp (q t) -> pp q t", t=SPAN + 1)
        p.emit("vector", lambda e: e.tensor_tensor(
            out=xx_sp[:, :].rearrange("pp (q t) -> pp q t", t=SPAN),
            in0=h3[:, :, 0:SPAN], in1=h3[:, :, 1:SPAN + 1], op=ALU.subtract),
            reads=["h_sp"], writes=["xx_sp"])

        def mk_mix(dst, vi, wtag, eng="vector"):
            for q in range(NPT):
                v_stt(dst[:, sp(q)], xx_sp[:, sp(q)],
                      vecs[:, vi * NPT + q:vi * NPT + q + 1],
                      h_sp[:, hcol(q, 0):hcol(q, SPAN)], ALU.mult, ALU.add,
                      r=["xx_sp", "vecs", "h_sp"], w=[wtag], eng=eng)

        def proj(mix, mixtag, w_b, wtag, cb):
            for d in range(NPT):
                bi = d % 2
                newgrp(bi)
                for k in range(NPT):
                    t_mm(bi, pb[bi][:, 0:256], w_b[:, k * C + d * 128: k * C + (d + 1) * 128],
                         mix[:, sp(k)], r=[wtag, mixtag])
                cb(d, pb[bi], pbtag[bi])

        mk_mix(mixB, 4, "mixB")
        newgrp(4)
        for k in range(NPT):
            t_mm(4, pb[4][0:64, 0:256], a1_b[:, k * 64:(k + 1) * 64], mixB[:, sp(k)],
                 r=["a1_b", "mixB"])
        s_copy(lw_t[64:128, 0:SPAN], pb[4][0:64, 0:256], r=["pb4"], w=["lw_t"])
        for d in range(NPT):
            bi = 4 + (d % 2)
            newgrp(bi)
            t_mm(bi, pb[bi][:, 256:512], a2_b[64:128, d * 128:(d + 1) * 128],
                 lw_t[64:128, 0:SPAN], r=["w2_b", "lw_t"], tp=(64, 0))
            p.emit("scalar", lambda e, d=d, bank=pb[bi]: e.activation(
                out=asig_sp[:, sp(d)], in_=bank[:, 256:512], func=AF.Sigmoid,
                bias=vecs[:, 7 * NPT + d:7 * NPT + d + 1], scale=1.0),
                reads=[pbtag[bi], "vecs"], writes=["asig_sp"])

        mk_mix(mixA, 1, "mixA")
        newgrp(2)
        for k in range(NPT):
            t_mm(2, pb[2][0:64, 0:256], w1_b[:, k * 64:(k + 1) * 64], mixA[:, sp(k)],
                 r=["w1_b", "mixA"])
        s_act(lw_t[0:64, 0:SPAN], pb[2][0:64, 0:256], AF.Tanh, r=["pb2"], w=["lw_t"])
        for d in range(NPT):
            bi = 2 + (d % 2)
            newgrp(bi)
            t_mm(bi, pb[bi][:, 256:512], w2_b[0:64, d * 128:(d + 1) * 128],
                 lw_t[0:64, 0:SPAN], r=["w2_b", "lw_t"])
            p.emit("scalar", lambda e, d=d, bank=pb[bi]: e.activation(
                out=expw[:, sp(d)], in_=bank[:, 256:512], func=AF.Exp,
                bias=w0n[:, d:d + 1], scale=-1.0), reads=[pbtag[bi], "w0n"],
                writes=["expw"])
        mk_mix(mixB, 2, "mixB")
        mk_mix(mixA, 3, "mixA")
        s_act(slog[:, :], expw[:, :], AF.Ln, r=["expw"], w=["slog"], bias=1.0)
        p.emit("scalar", lambda e: e.activation(
            out=expw[:, :], in_=slog[:, :], func=AF.Exp,
            bias=fvec[:, 6:7], scale=-1.0), reads=["slog", "cvec"], writes=["expw"])
        p.emit("vector", lambda e: e.tensor_tensor_scan(
            out=slog[:, :], data0=expw[:, :],
            data1=zer64[:, 0:1].broadcast_to([128, PS]),
            initial=0.0, op0=ALU.add, op1=ALU.add),
            reads=["expw", "slog"], writes=["slog"])
        slogc = slog[:, :].rearrange("pp (n t) -> pp n t", t=CHUNK)
        NCH = PS // CHUNK
        v_copy(tmpf[:, 0:NCH - 1], slogc[:, 0:NCH - 1, CHUNK - 1:CHUNK],
               r=["slog"], w=["tmpf"])
        p.emit("vector", lambda e: e.tensor_tensor(
            out=slogc[:, 1:NCH, :], in0=slogc[:, 1:NCH, :],
            in1=tmpf[:, 0:NCH - 1].unsqueeze(2).broadcast_to([128, NCH - 1, CHUNK]),
            op=ALU.subtract), reads=["slog", "tmpf"], writes=["slog"])
        s_act(c_sp[:, :], slog[:, :], AF.Exp, r=["slog"], w=["c_sp"], scale=-1.0)
        s_act(ic_sp[:, :], slog[:, :], AF.Exp, r=["slog"], w=["ic_sp"])
        v_tt(expw[:, :], expw[:, :], slog[:, :], ALU.subtract,
             r=["expw", "slog"], w=["expw"])
        s_act(cp_sp[:, :], expw[:, :], AF.Exp, r=["expw"], w=["cp_sp"])
        slog3 = slog[:, :].rearrange("pp (n t) -> pp n t", t=CHUNK)
        p.emit("vector", lambda e: e.tensor_tensor(
            out=expw[:, :].rearrange("pp (n t) -> pp n t", t=CHUNK),
            in0=slog3,
            in1=slog3[:, :, CHUNK - 1:CHUNK].broadcast_to([128, PS // CHUNK, CHUNK]),
            op=ALU.subtract), reads=["slog"], writes=["expw"])
        s_act(esuf_sp[:, :], expw[:, :], AF.Exp, r=["expw"], w=["esuf_sp"])
        for ch in range(CPS):
            src = c_sp[:, :].rearrange("pp (q t) -> pp q t", t=SPAN)[
                :, :, ch * CHUNK + 63:ch * CHUNK + 64]
            p.emit("vector", lambda e, ch=ch, src=src: e.tensor_copy(
                out=clrep[:, ch * 512:(ch + 1) * 512].rearrange("pp (q t) -> pp q t", t=64),
                in_=src.broadcast_to([128, NPT, 64])),
                reads=["c_sp"], writes=["stage"])

        def k_evac(d, bank, btag):
            v_ts(kk_sp[:, sp(d)], bank[:, 0:256], vecs[:, 8 * NPT + d:8 * NPT + d + 1],
                 None, ALU.mult, None, r=[btag, "vecs"], w=["kk_sp"])
            v_ts(tmpb[:, 0:256], asig_sp[:, sp(d)], -1.0, None, ALU.add, None,
                 r=["asig_sp"], w=["tmpb"])
            v_ts(tmpb[:, 0:256], tmpb[:, 0:256], vecs[:, 9 * NPT + d:9 * NPT + d + 1],
                 1.0, ALU.mult, ALU.add, r=["tmpb", "vecs"], w=["tmpb"])
            v_tt(kf_sp[:, sp(d)], bank[:, 0:256], tmpb[:, 0:256], ALU.mult,
                 r=[btag, "tmpb"], w=["kf_sp"])
        proj(mixB, "mixB", wk_b, "wk_b", k_evac)
        mk_mix(mixB, 5, "mixB")

        for tt in range(2):
            for half in range(2):
                bi = half
                newgrp(bi)
                for k in range(NPT):
                    t_mm(bi, pb[bi][:, 0:512],
                         mixA[:, k * SPAN + tt * 128: k * SPAN + (tt + 1) * 128],
                         wv_b[:, k * C + half * 512: k * C + (half + 1) * 512],
                         r=["mixA", "wv_b"])
                s_copy(vT_sp[:, tt * C + half * 512: tt * C + (half + 1) * 512],
                       pb[bi][:, 0:512], r=[pbtag[bi]], w=["xb_sp"])
        mk_mix(mixA, 0, "mixA")

        newgrp(2)
        for k in range(NPT):
            t_mm(2, pb[2][:, 0:256], g1_b[:, k * 128:(k + 1) * 128], mixB[:, sp(k)],
                 r=["g1_b", "mixB"])
        s_act(lg_t[:, :], pb[2][:, 0:256], AF.Sigmoid, r=["pb2"], w=["lw_t"])
        for d in range(NPT):
            bi = 2 + (d % 2)
            newgrp(bi)
            t_mm(bi, pb[bi][:, 256:512], g2_b[:, d * 128:(d + 1) * 128], lg_t[:, :],
                 r=["g2_b", "lw_t"])
            s_copy(g_sp[:, sp(d)], pb[bi][:, 256:512], r=[pbtag[bi]], w=["g_sp"])

        def r_evac(d, bank, btag):
            v_tt(ar_sp[:, PS + d * SPAN: PS + (d + 1) * SPAN], bank[:, 0:256],
                 c_sp[:, sp(d)], ALU.mult, r=[btag, "c_sp"], w=["ar_sp"])
        proj(mixA, "mixA", wr_b, "wr_b", r_evac)

        s_act(xsq_sp[:, :], kk_sp[:, :], AF.Square, r=["kk_sp"], w=["xsq_sp"])
        newgrp(7)
        for q in range(NPT):
            t_mm(7, pb[7][0:16, 0:256], bones16[:, q * 16:(q + 1) * 16], xsq_sp[:, sp(q)],
                 r=["bones16", "xsq_sp"])
        v_ts(rows[0:16, 0:256], pb[7][0:16, 0:256], 1e-24, None, ALU.max, None,
             r=["pb7"], w=["rows"])
        s_act(rows[0:16, 0:256], rows[0:16, 0:256], AF.Ln, r=["rows"], w=["rows"])
        s_act(rowsb[0:16, 0:256], rows[0:16, 0:256], AF.Exp, r=["rows"], w=["rowsb"],
              scale=-0.5)
        for q in range(NPT):
            newgrp(6)
            t_mm(6, pb[6][:, 0:256], bsel[0:16, q * 128:(q + 1) * 128],
                 rowsb[0:16, 0:256], r=["bsel", "rowsb"])
            v_tt(kk_sp[:, sp(q)], kk_sp[:, sp(q)], pb[6][:, 0:256], ALU.mult,
                 r=["kk_sp", "pb6"], w=["kk_sp"])

        v_stt(ar_sp[:, 0:PS], kk_sp[:, :], -1.0, cp_sp[:, :],
              ALU.mult, ALU.mult, r=["kk_sp", "cp_sp"], w=["ar_sp"])
        v_tt(kkg_sp[:, :], kk_sp[:, :], g_sp[:, :], ALU.mult,
             r=["kk_sp", "g_sp"], w=["kkg_sp"])
        v_tt(btF_sp[:, :], kkg_sp[:, :], esuf_sp[:, :], ALU.mult,
             r=["kkg_sp", "esuf_sp"], w=["kk_sp"], eng="gpsimd")
        v_tt(ktF_sp[:, :], kf_sp[:, :], esuf_sp[:, :], ALU.mult,
             r=["kf_sp", "esuf_sp"], w=["xx_sp"], eng="gpsimd")
        v_tt(bh_sp[:, :], kkg_sp[:, :], ic_sp[:, :], ALU.mult,
             r=["kkg_sp", "ic_sp"], w=["bh_sp"])
        v_tt(kh_sp[:, :], kf_sp[:, :], ic_sp[:, :], ALU.mult,
             r=["kf_sp", "ic_sp"], w=["kh_sp"])

        xsqb = xsq_sp
        sqplan = [(0, -2, -1), (1, -1, -2), (2, 1, 0), (3, 0, 1),
                  (4, 3, 2), (5, 2, 3), (6, 5, 4), (7, 4, 5), (8, 7, 6)]
        rounds = [sqplan[0:2], sqplan[2:4], sqplan[4:6], sqplan[6:8], sqplan[8:9]]

        pairs8 = []
        for _j in range(4):
            pairs8 += [(2 * _j, 0), (2 * _j + 1, 1), (2 * _j, 1), (2 * _j + 1, 0)]

        def wres(wi):
            P = wi % 2
            b = [4 * P, 4 * P + 1, 4 * P + 2, 4 * P + 3]
            if P == 0:
                return dict(P=P, b=b, tap=tap_sb, pkk=pkk_sb, tapt="h_sp",
                            stg=kkg_sp, stgt="kkg_sp",
                            xs=[xs_a, xs_b], xst="cp_sp")
            return dict(P=P, b=b, tap=esuf_sp[:, 0:1024], pkk=esuf_sp[:, 1024:2048],
                        tapt="esuf_sp", stg=kf_sp, stgt="kf_sp",
                        xs=[c_sp[:, 0:1024], c_sp[:, 1024:2048]], xst="c_sp")

        def pw(R, hi, k):
            if R["P"] == 0:
                if k == 8:
                    return xsqb[:, hi * 64: hi * 64 + 64], "xsq_sp"
                return slogb[:, hi * 512 + k * 64: hi * 512 + (k + 1) * 64], "slog"
            if k == 8:
                return expwb[:, hi * 512 + 384: hi * 512 + 448], "expw"
            if k >= 6:
                return (xsqb[:, 512 + hi * 128 + (k - 6) * 64:
                             512 + hi * 128 + (k - 5) * 64], "xsq_sp")
            return expwb[:, hi * 512 + k * 64: hi * 512 + (k + 1) * 64], "expw"

        def pwdst(R, par, dk0, nt):
            if R["P"] == 0:
                if dk0 == 8:
                    return (xsqb[:, 0:512].rearrange("pp (j x) -> pp j x", x=64)
                            [:, par:8:2, 0:64], "xsq_sp")
                return (slogb[:, :].rearrange("pp (j x) -> pp j x", x=512)
                        [:, par:8:2, dk0 * 64:(dk0 + nt) * 64], "slog")
            if dk0 == 8:
                return (expwb[:, :].rearrange("pp (j x) -> pp j x", x=512)
                        [:, par:8:2, 384:448], "expw")
            if dk0 >= 6:
                return (xsqb[:, 512:1536].rearrange("pp (j x) -> pp j x", x=128)
                        [:, par:8:2, (dk0 - 6) * 64:(dk0 - 6 + nt) * 64], "xsq_sp")
            return (expwb[:, :].rearrange("pp (j x) -> pp j x", x=512)
                    [:, par:8:2, dk0 * 64:(dk0 + nt) * 64], "expw")

        def front_gen(wi, tt, wv):
            R = wres(wi)
            b0, b1, b2, b3 = R["b"]
            H0 = wv * 8
            stg, stgt, tapt = R["stg"], R["stgt"], R["tapt"]
            tap, pkk = R["tap"], R["pkk"]
            xsA, xst = R["xs"][0], R["xst"]

            def tiles(hi, cp):
                h = H0 + hi
                q, hh = h // 2, (h % 2) * 64
                tc = q * SPAN + (tt * 2 + cp) * CHUNK
                return h, hh, cp * 64, tc

            newgrp(b0, b1, b2, b3)
            for hi, cp in pairs8:
                par, slot = hi % 2, hi // 2
                h, hh, ch, tc = tiles(hi, cp)
                bsl = bh_sp[hh:hh + 64, tc:tc + 64]
                asl = ar_sp[hh:hh + 64, tc:tc + 64]
                arr = ar_sp[hh:hh + 64, :].rearrange(
                    "pp (u v) -> pp u v", u=2)[:, :, tc:tc + 64]
                t_mm([b0, b1][par], pb[[b0, b1][par]][ch:ch + 64, slot * 128:(slot + 1) * 128],
                     bsl, arr, r=["bh_sp", "ar_sp"], tp=(hh, ch))
            for hi, cp in pairs8:
                par, slot = hi % 2, hi // 2
                h, hh, ch, tc = tiles(hi, cp)
                bsl = bh_sp[hh:hh + 64, tc:tc + 64]
                asl = ar_sp[hh:hh + 64, tc:tc + 64]
                t_mm([b2, b3][par], pb[[b2, b3][par]][ch:ch + 64, slot * 64:(slot + 1) * 64],
                     asl, bsl, r=["ar_sp", "bh_sp"], tp=(hh, ch))
            yield
            for par in range(2):
                p.emit("vector", lambda e, par=par: e.tensor_tensor(
                    out=tap.rearrange("pp (j u f) -> pp j u f", u=2, f=128)[:, 0:4, par, :],
                    in0=pb[[b0, b1][par]][:, 0:512].rearrange("pp (j f) -> pp j f", f=128),
                    in1=maskTP[:, 0:128].unsqueeze(1).broadcast_to([128, 4, 128]),
                    op=ALU.mult),
                    reads=[pbtag[[b0, b1][par]], "maskTP"], writes=[tapt])
                p.emit("vector", lambda e, par=par: e.tensor_tensor(
                    out=stg[:, 0:512].rearrange("pp (j u f) -> pp j u f", u=2, f=64)[:, 0:4, par, :],
                    in0=pb[[b2, b3][par]][:, 0:256].rearrange("pp (j f) -> pp j f", f=64),
                    in1=maskLO[:, 0:64].unsqueeze(1).broadcast_to([128, 4, 64]),
                    op=ALU.mult),
                    reads=[pbtag[[b2, b3][par]], "maskLO"], writes=[stgt])
            yield
            newgrp(b0, b1)
            for hi, cp in pairs8:
                par, slot = hi % 2, hi // 2
                h, hh, ch, tc = tiles(hi, cp)
                ksl = kh_sp[hh:hh + 64, tc:tc + 64]
                arr = ar_sp[hh:hh + 64, :].rearrange(
                    "pp (u v) -> pp u v", u=2)[:, :, tc:tc + 64]
                t_mm([b0, b1][par], pb[[b0, b1][par]][ch:ch + 64, slot * 128:(slot + 1) * 128],
                     ksl, arr, r=["kh_sp", "ar_sp"], tp=(hh, ch))
            yield
            for par in range(2):
                p.emit("vector", lambda e, par=par: e.tensor_tensor(
                    out=pkk.rearrange("pp (j u f) -> pp j u f", u=2, f=128)[:, 0:4, par, :],
                    in0=pb[[b0, b1][par]][:, 0:512].rearrange("pp (j f) -> pp j f", f=128),
                    in1=maskTP[:, 0:128].unsqueeze(1).broadcast_to([128, 4, 128]),
                    op=ALU.mult),
                    reads=[pbtag[[b0, b1][par]], "maskTP"], writes=[tapt])
            yield
            newgrp(b2, b3)
            for hi, cp in pairs8:
                par, slot = hi % 2, hi // 2
                h, hh, ch, tc = tiles(hi, cp)
                t_mm([b2, b3][par], pb[[b2, b3][par]][ch:ch + 64, slot * 64:(slot + 1) * 64],
                     ar_sp[hh:hh + 64, tc:tc + 64], identb[hh:hh + 64, hh:hh + 64],
                     r=["ar_sp", "identb"], tp=(hh, ch))
            for hi, cp in pairs8:
                par, slot = hi % 2, hi // 2
                h, hh, ch, tc = tiles(hi, cp)
                t_mm([b2, b3][par], pb[[b2, b3][par]][ch:ch + 64, 256 + slot * 64: 256 + (slot + 1) * 64],
                     btF_sp[hh:hh + 64, tc:tc + 64], identb[hh:hh + 64, hh:hh + 64],
                     r=["kk_sp", "identb"], tp=(hh, ch))
            newgrp(b0, b1)
            for hi, cp in pairs8:
                ch = cp * 64
                h = H0 + hi
                hh = (h % 2) * 64
                t_mm([b0, b1][cp], pb[[b0, b1][cp]][hh:hh + 64, (hi // 2) * 64:(hi // 2 + 1) * 64],
                     vT_sp[ch:ch + 64, tt * C + h * 64:tt * C + (h + 1) * 64],
                     identb[ch:ch + 64, ch:ch + 64],
                     r=["xb_sp", "identb"], tp=(ch, hh))
            yield
            for par in range(2):
                eng = "scalar" if par == 0 else "vector"
                cfn = (lambda e: e.copy) if par == 0 else (lambda e: e.tensor_copy)
                p.emit(eng, lambda e, par=par, cfn=cfn: cfn(e)(
                    out=xsA.rearrange("pp (j u f) -> pp j u f", u=2, f=128)[:, 0:4, par, 0:64],
                    in_=pb[[b2, b3][par]][:, 0:256].rearrange("pp (j f) -> pp j f", f=64)),
                    reads=[pbtag[[b2, b3][par]]], writes=[xst])
                p.emit(eng, lambda e, par=par, cfn=cfn: cfn(e)(
                    out=stg[:, 512:1024].rearrange("pp (j u f) -> pp j u f", u=2, f=64)[:, 0:4, par, :],
                    in_=pb[[b2, b3][par]][:, 256:512].rearrange("pp (j f) -> pp j f", f=64)),
                    reads=[pbtag[[b2, b3][par]]], writes=[stgt])
            for cp in range(2):
                tcc = (tt * 2 + cp) * CHUNK
                eng = "scalar" if cp == 0 else "vector"
                cfn = (lambda e: e.copy) if cp == 0 else (lambda e: e.tensor_copy)
                p.emit(eng, lambda e, cp=cp, tcc=tcc, cfn=cfn: cfn(e)(
                    out=vF_sp.rearrange("pp (j f) -> pp j f", f=SPAN)
                        [:, H0 // 2: H0 // 2 + 4, tcc:tcc + 64],
                    in_=pb[[b0, b1][cp]][:, 0:256]),
                    reads=[pbtag[[b0, b1][cp]]], writes=["asig_sp"])
            yield
            newgrp(b0, b1, b2, b3)
            for hi, cp in pairs8:
                par, slot = hi % 2, hi // 2
                h, hh, ch, tc = tiles(hi, cp)
                t_mm([b0, b1][par], pb[[b0, b1][par]][ch:ch + 64, slot * 64:(slot + 1) * 64],
                     pkk[ch:ch + 64, hi * 128: hi * 128 + 64],
                     vT_sp[ch:ch + 64, tt * C + h * 64:tt * C + (h + 1) * 64],
                     r=[tapt, "xb_sp"], tp=(ch, ch))
            for hi, cp in pairs8:
                par, slot = hi % 2, hi // 2
                h, hh, ch, tc = tiles(hi, cp)
                t_mm([b2, b3][par], pb[[b2, b3][par]][ch:ch + 64, slot * 64:(slot + 1) * 64],
                     ktF_sp[hh:hh + 64, tc:tc + 64], identb[hh:hh + 64, hh:hh + 64],
                     r=["xx_sp", "identb"], tp=(hh, ch))
            yield
            for par in range(2):
                eng = "scalar" if par == 0 else "vector"
                cfn = (lambda e: e.copy) if par == 0 else (lambda e: e.tensor_copy)
                p.emit(eng, lambda e, par=par, cfn=cfn: cfn(e)(
                    out=xsA.rearrange("pp (j u f) -> pp j u f", u=2, f=128)[:, 0:4, par, 64:128],
                    in_=pb[[b0, b1][par]][:, 0:256].rearrange("pp (j f) -> pp j f", f=64)),
                    reads=[pbtag[[b0, b1][par]]], writes=[xst])
                p.emit(eng, lambda e, par=par, cfn=cfn: cfn(e)(
                    out=stg[:, 1024:1536].rearrange("pp (j u f) -> pp j u f", u=2, f=64)[:, 0:4, par, :],
                    in_=pb[[b2, b3][par]][:, 256:512].rearrange("pp (j f) -> pp j f", f=64)
                    if False else
                    pb[[b2, b3][par]][:, 0:256].rearrange("pp (j f) -> pp j f", f=64)),
                    reads=[pbtag[[b2, b3][par]]], writes=[stgt])
            yield

        def back_gen(wi, tt, wv):
            R = wres(wi)
            b0, b1, b2, b3 = R["b"]
            H0 = wv * 8
            stg, stgt, tapt = R["stg"], R["stgt"], R["tapt"]
            tap, pkk = R["tap"], R["pkk"]
            xst = R["xst"]
            xstate = [R["xs"][0], R["xs"][1]]

            def optile(hi, k):
                if k == -1:
                    return (lambda ch: tap[ch:ch + 64, hi * 128:hi * 128 + 64]), tapt
                if k == -2:
                    return (lambda ch: stg[ch:ch + 64, hi * 64:(hi + 1) * 64]), stgt
                ap, tg = pw(R, hi, k)
                return (lambda ch, ap=ap: ap[ch:ch + 64, :]), tg

            def emit_sq(rnd):
                newgrp(b0, b1)
                nt = len(rnd)
                for ti in range(nt):
                    dk, lk, rk = rnd[ti]
                    for hi, cp in pairs8:
                        par = hi % 2
                        lf, lt = optile(hi, lk)
                        rf, rt = optile(hi, rk)
                        ch = cp * 64
                        t_mm([b0, b1][par], pb[[b0, b1][par]][ch:ch + 64,
                             (hi // 2) * 128 + ti * 64:(hi // 2) * 128 + (ti + 1) * 64],
                             lf(ch), rf(ch), r=[lt, rt], tp=(ch, ch))
                dk0 = rnd[0][0]
                for par in range(2):
                    eng = "scalar" if par == 0 else "vector"
                    dst, dtag = pwdst(R, par, dk0, nt)

                    def ev(e, par=par, nt=nt, dst=dst):
                        src = pb[[b0, b1][par]][:, 0:512].rearrange(
                            "pp (j f) -> pp j f", f=128)[:, 0:4, 0:nt * 64]
                        fn = e.copy if par == 0 else e.tensor_copy
                        return fn(out=dst, in_=src)
                    p.emit(eng, ev, reads=[pbtag[[b0, b1][par]]], writes=[dtag])
                yield

            def emit_app(ak):
                xsrc, xdst = xstate
                newgrp(b2, b3)
                for hi, cp in pairs8:
                    par = hi % 2
                    ch = cp * 64
                    t_mm([b2, b3][par], pb[[b2, b3][par]][ch:ch + 64,
                         (hi // 2) * 128:(hi // 2) * 128 + 128],
                         identb[ch:ch + 64, ch:ch + 64],
                         xsrc[ch:ch + 64, hi * 128:(hi + 1) * 128],
                         r=["identb", xst], tp=(ch, ch))
                for hi, cp in pairs8:
                    par = hi % 2
                    lf, lt = optile(hi, ak) if ak != -1 else (
                        (lambda ch, hi=hi: tap[ch:ch + 64, hi * 128:hi * 128 + 64]), tapt)
                    ch = cp * 64
                    t_mm([b2, b3][par], pb[[b2, b3][par]][ch:ch + 64,
                         (hi // 2) * 128:(hi // 2) * 128 + 128],
                         lf(ch), xsrc[ch:ch + 64, hi * 128:(hi + 1) * 128],
                         r=[lt, xst], tp=(ch, ch))
                for par in range(2):
                    eng = "scalar" if par == 0 else "vector"
                    cfn = (lambda e: e.copy) if par == 0 else (lambda e: e.tensor_copy)
                    p.emit(eng, lambda e, par=par, xdst=xdst, cfn=cfn: cfn(e)(
                        out=xdst.rearrange("pp (j u f) -> pp j u f", u=2, f=128)[:, 0:4, par, :],
                        in_=pb[[b2, b3][par]][:, 0:512].rearrange("pp (j f) -> pp j f", f=128)),
                        reads=[pbtag[[b2, b3][par]]], writes=[xst])
                xstate[0], xstate[1] = xdst, xsrc
                yield

            yield from emit_sq(rounds[0])
            yield from emit_app(-1)
            yield from emit_app(0)
            yield from emit_sq(rounds[1])
            yield from emit_app(2)
            yield from emit_sq(rounds[2])
            yield from emit_app(4)
            yield from emit_sq(rounds[3])
            yield from emit_app(6)
            yield from emit_sq(rounds[4])
            yield from emit_app(8)
            xsrc = xstate[0]

            for cp in range(2):
                ch = cp * 64
                tcc = (tt * 2 + cp) * CHUNK
                gchunk_off = (tt * 2 + cp) * 512
                newgrp(b0, b1)
                for hi in range(8):
                    h = H0 + hi
                    hh = (h % 2) * 64
                    UA = xsrc[ch:ch + 64, hi * 128: hi * 128 + 64]
                    t_mm(b0, pb[b0][hh:hh + 64, (hi // 2) * 64:(hi // 2 + 1) * 64], UA,
                         stg[ch:ch + 64, 512 + hi * 64:512 + (hi + 1) * 64],
                         r=[xst, stgt], tp=(ch, hh))
                    t_mm(b1, pb[b1][hh:hh + 64, (hi // 2) * 64:(hi // 2 + 1) * 64], UA,
                         tap[ch:ch + 64, hi * 128 + 64: hi * 128 + 128],
                         r=[xst, tapt], tp=(ch, hh))
                yield
                s_copy(stg[:, 1536:1792], pb[b0][:, 0:256], r=[pbtag[b0]], w=[stgt])
                rsl = ar_sp[:, PS:].rearrange("pp (j f) -> pp j f", f=SPAN)[
                    :, H0 // 2: H0 // 2 + 4, tcc:tcc + 64]
                v_tt(stg[:, 1792:2048].rearrange("pp (j f) -> pp j f", f=64),
                     pb[b1][:, 0:256].rearrange("pp (j f) -> pp j f", f=64),
                     rsl, ALU.add, r=[pbtag[b1], "ar_sp"], w=[stgt])
                newgrp(b2, b3)
                for hi in range(8):
                    h = H0 + hi
                    hh = (h % 2) * 64
                    UL = xsrc[ch:ch + 64, hi * 128 + 64: hi * 128 + 128]
                    t_mm(b2, pb[b2][hh:hh + 64, (hi // 2) * 64:(hi // 2 + 1) * 64],
                         stg[ch:ch + 64, 512 + hi * 64:512 + (hi + 1) * 64], UL,
                         r=[stgt, xst], tp=(ch, hh))
                    t_mm(b2, pb[b2][hh:hh + 64, (hi // 2) * 64:(hi // 2 + 1) * 64],
                         stg[ch:ch + 64, 1024 + hi * 64:1024 + (hi + 1) * 64],
                         vT_sp[ch:ch + 64, tt * C + h * 64:tt * C + (h + 1) * 64],
                         r=[stgt, "xb_sp"], tp=(ch, hh))
                    t_mm(b3, pb[b3][hh:hh + 64, (hi // 2) * 64:(hi // 2 + 1) * 64], UL,
                         tap[ch:ch + 64, hi * 128 + 64: hi * 128 + 128],
                         r=[xst, tapt], tp=(ch, hh))
                    t_mm(b3, pb[b3][hh:hh + 64, (hi // 2) * 64:(hi // 2 + 1) * 64],
                         vT_sp[ch:ch + 64, tt * C + h * 64:tt * C + (h + 1) * 64],
                         pkk[ch:ch + 64, hi * 128 + 64: hi * 128 + 128],
                         r=["xb_sp", tapt], tp=(ch, hh))
                yield
                newgrp(b0, b1)
                for j in range(4):
                    for he, ho in ((2 * j, 2 * j + 1), (2 * j + 1, 2 * j)):
                        h = H0 + he
                        hh = (h % 2) * 64
                        pr = h // 2
                        t_mm(b0, pb[b0][hh:hh + 64, (he // 2) * 64:(he // 2 + 1) * 64],
                             stg[hh:hh + 64, 1536 + (he // 2) * 64:1536 + (he // 2 + 1) * 64],
                             stSb[hh:hh + 64, pr * 64:(pr + 1) * 64],
                             r=[stgt, "stSb"], tp=(hh, hh))
                        h2 = H0 + ho
                        hh2 = (h2 % 2) * 64
                        pr2 = h2 // 2
                        t_mm(b1, pb[b1][hh2:hh2 + 64, (ho // 2) * 64:(ho // 2 + 1) * 64],
                             stSb[hh2:hh2 + 64, pr2 * 64:(pr2 + 1) * 64],
                             stg[hh2:hh2 + 64, 1792 + (ho // 2) * 64:1792 + (ho // 2 + 1) * 64],
                             r=["stSb", stgt], tp=(hh2, hh2))
                yield
                pr0 = H0 // 2
                ssl = slice(pr0 * 64, (pr0 + 4) * 64)
                csl = clrep[:, gchunk_off + pr0 * 64: gchunk_off + (pr0 + 4) * 64]
                v_tt(stS[:, ssl], stS[:, ssl], csl, ALU.mult,
                     r=["stS", "stage"], w=["stS"])
                v_tt(stS[:, ssl], pb[b2][:, 0:256], stS[:, ssl], ALU.add,
                     r=[pbtag[b2], "stS"], w=["stS"])
                v_tt(stS[:, ssl], pb[b0][:, 0:256], stS[:, ssl], ALU.add,
                     r=[pbtag[b0], "stS"], w=["stS"])
                v_copy(stSb[:, ssl], stS[:, ssl], r=["stS"], w=["stSb"])
                p.emit("vector", lambda e, tcc=tcc, b3=b3: e.tensor_copy(
                    out=o_sp.rearrange("pp (j f) -> pp j f", f=SPAN)
                        [:, H0 // 2: H0 // 2 + 4, tcc:tcc + 64],
                    in_=pb[b3][:, 0:256]),
                    reads=[pbtag[b3]], writes=["ic_sp"])
                p.emit("vector", lambda e, tcc=tcc, b1=b1: e.tensor_tensor(
                    out=o_sp.rearrange("pp (j f) -> pp j f", f=SPAN)
                        [:, H0 // 2: H0 // 2 + 4, tcc:tcc + 64],
                    in0=pb[b1][:, 0:256].rearrange("pp (j f) -> pp j f", f=64),
                    in1=o_sp.rearrange("pp (j f) -> pp j f", f=SPAN)
                        [:, H0 // 2: H0 // 2 + 4, tcc:tcc + 64],
                    op=ALU.add),
                    reads=[pbtag[b1], "ic_sp"], writes=["ic_sp"])
                yield

        wavelist = [(tt, wv) for tt in range(2) for wv in range(2)]
        _S = object()
        prev_back = None
        for wi, (tt, wv) in enumerate(wavelist):
            f = front_gen(wi, tt, wv)
            if prev_back is None:
                for _ in f:
                    pass
            else:
                fa, ba = True, True
                while fa or ba:
                    if ba:
                        ba = next(prev_back, _S) is not _S
                    if fa:
                        fa = next(f, _S) is not _S
            prev_back = back_gen(wi, tt, wv)
        for _ in prev_back:
            pass

        s_act(xsq_sp[:, :], o_sp[:, :], AF.Square, r=["ic_sp"], w=["xsq_sp"])
        newgrp(7)
        for q in range(NPT):
            t_mm(7, pb[7][0:16, 0:256], bones16[:, q * 16:(q + 1) * 16], o_sp[:, sp(q)],
                 r=["bones16", "ic_sp"])
        for q in range(NPT):
            t_mm(7, pb[7][0:16, 256:512], bones16[:, q * 16:(q + 1) * 16], xsq_sp[:, sp(q)],
                 r=["bones16", "xsq_sp"])
        ln_rows(16, 1.0 / HD, 5, "pb7")
        v_tt(xb_sp[:, :], ar_sp[:, PS:2 * PS], kh_sp[:, :], ALU.mult,
             r=["ar_sp", "kh_sp"], w=["xb_sp"])
        p.emit("vector", lambda e: e.tensor_tensor(
            out=xsq_sp[:, :].rearrange("pp (q t) -> pp q t", t=SPAN),
            in0=xb_sp[:, :].rearrange("pp (q t) -> pp q t", t=SPAN),
            in1=vecs[:, 10 * NPT:11 * NPT].unsqueeze(2).broadcast_to([128, NPT, SPAN]),
            op=ALU.mult), reads=["xb_sp", "vecs"], writes=["xsq_sp"])
        newgrp(6)
        for q in range(NPT):
            t_mm(6, pb[6][0:16, 0:256], bones16[:, q * 16:(q + 1) * 16], xsq_sp[:, sp(q)],
                 r=["bones16", "xsq_sp"])
        s_copy(lw_t[0:16, 0:256], pb[6][0:16, 0:256], r=["pb6"], w=["lw_t"])
        for q in range(NPT):
            newgrp(4, 5)
            rsel = rowsb[0:16, :].rearrange("pp (u f) -> pp u f", f=256)
            t_mm(4, pb[4][:, 0:512], bsel[0:16, q * 128:(q + 1) * 128], rsel,
                 r=["bsel", "rowsb"])
            t_mm(5, pb[5][:, 0:256], bsel[0:16, q * 128:(q + 1) * 128],
                 lw_t[0:16, 0:256], r=["bsel", "lw_t"])
            v_tt(tmpf[:, 0:256], o_sp[:, sp(q)], pb[4][:, 0:256], ALU.mult,
                 r=["ic_sp", "pb4"], w=["tmpf"])
            v_tt(tmpf[:, 0:256], tmpf[:, 0:256], pb[4][:, 256:512], ALU.add,
                 r=["tmpf", "pb4"], w=["tmpf"])
            v_ts(tmpf[:, 0:256], tmpf[:, 0:256], vecs[:, 11 * NPT + q:11 * NPT + q + 1],
                 vecs[:, 12 * NPT + q:12 * NPT + q + 1], ALU.mult, ALU.add,
                 r=["tmpf", "vecs"], w=["tmpf"])
            v_tt(tmpb[:, 0:256], vF_sp[:, sp(q)], pb[5][:, 0:256], ALU.mult,
                 r=["asig_sp", "pb5"], w=["tmpb"])
            v_tt(tmpf[:, 0:256], tmpf[:, 0:256], tmpb[:, 0:256], ALU.add,
                 r=["tmpf", "tmpb"], w=["tmpf"])
            v_tt(y_sp[:, sp(q)], tmpf[:, 0:256], g_sp[:, sp(q)], ALU.mult,
                 r=["tmpf", "g_sp"], w=["cp_sp"])

        for d in range(NPT):
            bi = d % 2
            newgrp(bi)
            for k in range(NPT):
                t_mm(bi, pb[bi][:, 0:256], wo_b[:, k * C + d * 128: k * C + (d + 1) * 128],
                     y_sp[:, sp(k)], r=["wo_b", "cp_sp"])
            v_stt(x_a[:, sp(d)], pb[bi][:, 0:256], gav[:, d:d + 1], x_a[:, sp(d)],
                  ALU.mult, ALU.add, r=[pbtag[bi], "gav", "x_a"], w=["x_a"])

        v_copy(xb_sp[:, :], x_a[:, :], r=["x_a"], w=["xb_sp"])
        s_act(xsq_sp[:, :], x_a[:, :], AF.Square, r=["x_a"], w=["xsq_sp"])
        newgrp(7)
        for q in range(NPT):
            t_mm(7, pb[7][0:1, 0:256], onescol[:, 0:1], xb_sp[:, sp(q)],
                 r=["onescol", "xb_sp"])
        for q in range(NPT):
            t_mm(7, pb[7][0:1, 256:512], onescol[:, 0:1], xsq_sp[:, sp(q)],
                 r=["onescol", "xsq_sp"])
        ln_rows(1, 1.0 / C, 4, "pb7")
        newgrp(6)
        t_mm(6, pb[6][:, 0:512], bcast1[0:1, :], rowsb[0:1, 0:512], r=["bcast1", "rowsb"])
        p.emit("vector", lambda e: e.tensor_tensor(
            out=slog[:, :].rearrange("pp (q t) -> pp q t", t=SPAN),
            in0=x_a[:, :].rearrange("pp (q t) -> pp q t", t=SPAN),
            in1=pb[6][:, 0:256].unsqueeze(1).broadcast_to([128, NPT, 256]),
            op=ALU.mult), reads=["x_a", "pb6"], writes=["slog"])
        p.emit("vector", lambda e: e.tensor_tensor(
            out=slog[:, :].rearrange("pp (q t) -> pp q t", t=SPAN),
            in0=slog[:, :].rearrange("pp (q t) -> pp q t", t=SPAN),
            in1=pb[6][:, 256:512].unsqueeze(1).broadcast_to([128, NPT, 256]),
            op=ALU.add), reads=["slog", "pb6"], writes=["slog"])
        for q in range(NPT):
            v_ts(xb_sp[:, sp(q)], slog[:, sp(q)], cb1[:, q:q + 1],
                 modv[:, 3 * NPT + q:3 * NPT + q + 1], ALU.mult, ALU.add,
                 r=["slog", "cb1", "modv"], w=["xb_sp"])
        s_act(xsq_sp[:, :], xb_sp[:, :], AF.Square, r=["xb_sp"], w=["xsq_sp"])
        v_tt(tmpf[:, 0:256], xsq_sp[:, sp(0)], xsq_sp[:, sp(1)], ALU.max,
             r=["xsq_sp"], w=["tmpf"])
        v_tt(tmpf2[:, 0:256], xsq_sp[:, sp(2)], xsq_sp[:, sp(3)], ALU.max,
             r=["xsq_sp"], w=["tmpf2"])
        v_tt(rows[:, 0:256], xsq_sp[:, sp(4)], xsq_sp[:, sp(5)], ALU.max,
             r=["xsq_sp"], w=["rows"])
        v_tt(rows[:, 256:512], xsq_sp[:, sp(6)], xsq_sp[:, sp(7)], ALU.max,
             r=["xsq_sp"], w=["rows"])
        v_tt(tmpf[:, 0:256], tmpf[:, 0:256], tmpf2[:, 0:256], ALU.max,
             r=["tmpf", "tmpf2"], w=["tmpf"])
        v_tt(rows[:, 0:256], rows[:, 0:256], rows[:, 256:512], ALU.max,
             r=["rows"], w=["rows"])
        v_tt(tmpf[:, 0:256], tmpf[:, 0:256], rows[:, 0:256], ALU.max,
             r=["tmpf", "rows"], w=["tmpf"])
        newgrp(6)
        for half in range(2):
            t_mm(6, pb[6][:, half * 128:(half + 1) * 128],
                 tmpf[:, half * 128:(half + 1) * 128], identf[:, :],
                 r=["tmpf", "identf"])
        for half in range(2):
            p.emit("vector", lambda e, half=half: e.reduce_max(
                out=fvec[:, 2 + half:3 + half], in_=pb[6][:, half * 128:(half + 1) * 128],
                axis=AX), reads=["pb6"], writes=["fvec"])
        s_act(fvec[:, 2:4], fvec[:, 2:4], AF.Ln, r=["fvec"], w=["fvec"])
        s_act(fvec[:, 2:4], fvec[:, 2:4], AF.Exp, r=["fvec"], w=["fvec"], scale=0.5)
        v_ts(fvec[:, 2:4], fvec[:, 2:4], EPS_FFN, None, ALU.add, None,
             r=["fvec"], w=["fvec"])
        s_act(fvec[:, 0:2], fvec[:, 2:4], AF.Ln, r=["fvec"], w=["fvec"])
        s_act(fvec[:, 0:2], fvec[:, 0:2], AF.Exp, r=["fvec"], w=["fvec"], scale=-1.0)
        v_ts(fvec[:, 0:2], fvec[:, 0:2], 127.0, None, ALU.mult, None,
             r=["fvec"], w=["fvec"])
        v_copy(fvb[:, 0:4], fvec[:, 0:4], r=["fvec"], w=["fvb"])
        newgrp(5)
        for cc in range(4):
            t_mm(5, pb[5][0:1, cc * 128:(cc + 1) * 128], fvb[:, cc:cc + 1],
                 identb[:, 0:128], r=["fvb", "identb"])
        s_copy(rowsb[0:1, 0:512], pb[5][0:1, 0:512], r=["pb5"], w=["rowsb"])
        newgrp(6)
        for cc in range(4):
            t_mm(6, pb[6][:, cc * 128:(cc + 1) * 128], bcast1[0:1, :],
                 rowsb[0:1, cc * 128:(cc + 1) * 128], r=["bcast1", "rowsb"])
        p.emit("vector", lambda e: e.tensor_tensor(
            out=slog[:, :].rearrange("pp (q t) -> pp q t", t=SPAN),
            in0=xb_sp[:, :].rearrange("pp (q t) -> pp q t", t=SPAN),
            in1=pb[6][:, 0:256].unsqueeze(1).broadcast_to([128, NPT, 256]),
            op=ALU.mult), reads=["xb_sp", "pb6"], writes=["slog"])
        v_ts(xsq_sp[:, :], slog[:, :], MAGIC, MAGIC, ALU.add, ALU.subtract,
             r=["slog"], w=["xsq_sp"])
        v_copy(ascb[:, 0:256], pb[6][:, 256:512], r=["pb6"], w=["tmpf2"])
        for d in range(NPT):
            bi = d % 2
            newgrp(bi)
            for k in range(NPT):
                t_mm(bi, pb[bi][:, 0:256], wq_b[:, k * C + d * 128: k * C + (d + 1) * 128],
                     xsq_sp[:, sp(k)], r=["wq_b", "xsq_sp"])
            v_tt(tmpf[:, 0:256], pb[bi][:, 0:256], ascb[:, 0:256], ALU.mult,
                 r=[pbtag[bi], "tmpf2"], w=["tmpf"])
            v_stt(slog[:, sp(d)], tmpf[:, 0:256], gbv[:, d:d + 1], x_a[:, sp(d)],
                  ALU.mult, ALU.add, r=["tmpf", "gbv", "x_a"], w=["slog"])
        if s + 1 < NSPAN:
            dma("sync", x_a[:, :], dx[s + 1], r=["x_a"], w=["x_a"])
        dma("sync", dout[s], slog[:, :], r=["slog"], w=["OUT"])

    newgrp(*range(8))
    p.emit("sync", lambda e: e.nop(), reads=["OUT"], writes=[])
    p.finalize(nc)
    return nc


_NC_CACHE = {}


def lay_w(W):
    return np.ascontiguousarray(
        W.reshape(8, 128, 8, 128).transpose(1, 0, 2, 3).reshape(128, NPT * C))


def lay_lora(W, cols):
    return np.ascontiguousarray(
        W.reshape(8, 128, cols).transpose(1, 0, 2).reshape(128, NPT * cols))


def lay_wmod(W):
    return np.ascontiguousarray(
        W.reshape(8, 128, 48, 128).transpose(1, 2, 0, 3).reshape(128, 48 * C))


def lay_x(xb):
    return np.ascontiguousarray(
        xb.reshape(NSPAN, SPAN, 8, 128).transpose(0, 3, 2, 1).reshape(NSPAN, 128, PS))


def unlay_out(arr):
    return np.ascontiguousarray(
        arr.reshape(NSPAN, 128, 8, SPAN).transpose(0, 3, 2, 1).reshape(T, C))


def kernel(**inputs):
    if "nc" not in _NC_CACHE:
        _NC_CACHE["nc"] = build_program()
    nc = _NC_CACHE["nc"]

    att = lambda n: np.asarray(inputs["L0_attn_" + n])
    vec_rows = np.stack([
        att("x_r").reshape(C), att("x_w").reshape(C), att("x_k").reshape(C),
        att("x_v").reshape(C), att("x_a").reshape(C), att("x_g").reshape(C),
        att("w0").reshape(C), att("a0").reshape(C),
        att("k_k").reshape(C), att("k_a").reshape(C),
        att("r_k").reshape(C),
        att("ln_gamma").reshape(C), att("ln_beta").reshape(C),
    ]).astype(np.float32)

    base = {
        "wr": lay_w(np.asarray(att("Wr"), np.float32)),
        "wk": lay_w(np.asarray(att("Wk"), np.float32)),
        "wv": lay_w(np.asarray(att("Wv"), np.float32)),
        "wo": lay_w(np.asarray(att("Wo"), np.float32)),
        "wq": lay_w(np.asarray(np.asarray(inputs["L0_ffn_weight"]).T, np.float32)),
        "w1": lay_lora(np.asarray(att("w1"), np.float32), 64),
        "w2": np.ascontiguousarray(att("w2"), np.float32),
        "a1": lay_lora(np.asarray(att("a1"), np.float32), 64),
        "a2": np.ascontiguousarray(att("a2"), np.float32),
        "g1": lay_lora(np.asarray(att("g1"), np.float32), 128),
        "g2": np.ascontiguousarray(att("g2"), np.float32),
        "wmod": lay_wmod(np.asarray(np.asarray(inputs["L0_wmod"]), np.float32)),
        "vecs": vec_rows,
    }
    x = np.asarray(inputs["L0_attn_x"])
    cmod = np.asarray(inputs["L0_c"])
    in_maps = []
    for b in range(8):
        m = dict(base)
        m["x"] = lay_x(np.asarray(x[b], np.float32))
        m["cmod"] = np.ascontiguousarray(cmod[b].reshape(C, 1), np.float32)
        in_maps.append(m)

    res = run_bass_kernel_spmd(nc, in_maps, core_ids=list(range(8)))
    out = np.stack([unlay_out(np.asarray(res.results[b]["out"]))
                    for b in range(8)])
    return out.astype(np.float32)


# version 32
# speedup vs baseline: 1.0444x; 1.0444x over previous
import numpy as np
import concourse.bass as bass
import concourse.mybir as mybir
import bass_rust
from concourse.bass_utils import run_bass_kernel_spmd

F32 = mybir.dt.float32
BF16 = mybir.dt.bfloat16
AF = mybir.ActivationFunctionType
ALU = mybir.AluOpType
AX = mybir.AxisListType.X
MAGIC = 1.5 * 2.0 ** 23
REDMAX = bass_rust.ReduceOp.max

C = 1024
T = 2048
HD = 64
SPAN = 256
NSPAN = T // SPAN
CHUNK = 64
CPS = SPAN // CHUNK
NPT = C // 128
PS = NPT * SPAN
EPS_LN = 1e-5
EPS_GN = 0.00064
EPS_FFN = 1e-8

ENGINES = ("tensor", "vector", "scalar", "gpsimd", "sync")


class Prog:
    NDMASLOT = 4

    def __init__(self):
        self.ops = {e: [] for e in ENGINES}
        self.rcount = {}
        self.riders = {}
        self.tags = {}
        self.waited = {}
        self.dslot = {e: 0 for e in ENGINES}

    def _cover(self, peng, pdom, pidx):
        lst = self.riders.setdefault(pdom, [])
        for i, v in lst:
            if i >= pidx:
                return v
        isdma = pdom.startswith("dma_")
        ops = self.ops[peng]
        j = len(ops) - 1
        while ops[j]["dma"] != isdma or ops[j]["rv"] is not None:
            j -= 1
        op = ops[j]
        assert op["idx"] >= pidx, (peng, pdom, pidx)
        inc = 16 if pdom.startswith("dma_") else 1
        val = self.rcount.get(pdom, 0) + inc
        self.rcount[pdom] = val
        op["rv"] = val
        op["rdom"] = pdom
        op["rinc"] = inc
        lst.append((op["idx"], val))
        return val

    def emit(self, eng, fn, reads=(), writes=(), accum=False, dma=False):
        if dma:
            slot = self.dslot[eng]
            self.dslot[eng] = (slot + 1) % self.NDMASLOT
            dom = f"dma_{eng}{slot}"
        else:
            dom = eng
        ops = self.ops[eng]
        op = {"fn": fn, "waits": [], "rv": None, "rdom": None, "rinc": None,
              "idx": len(ops), "dma": dma}
        needs = {}

        def need(rec):
            k = (rec[0], rec[1])
            if needs.get(k, -1) < rec[2]:
                needs[k] = rec[2]

        for t in reads:
            rec = self.tags.get(t)
            if rec:
                for w in rec["w"]:
                    need(w)
        for t in writes:
            rec = self.tags.get(t)
            if rec:
                for w in rec["w"]:
                    if accum and w[0] == "tensor" and eng == "tensor":
                        continue
                    need(w)
                for r in rec["r"].values():
                    need(r)
        for (peng, pdom), pidx in sorted(needs.items(), key=lambda kv: (kv[0][0], kv[0][1])):
            if peng == eng and pdom == dom and pidx == op["idx"]:
                continue
            val = self._cover(peng, pdom, pidx)
            key = (eng, pdom)
            if self.waited.get(key, 0) < val:
                self.waited[key] = val
                op["waits"].append((pdom, val))
        me = (eng, dom, op["idx"])
        for t in writes:
            rec = self.tags.get(t)
            if rec is None or not accum:
                self.tags[t] = {"w": [me], "r": {}}
            else:
                rec["w"].append(me)
                rec["r"] = {}
        for t in reads:
            rec = self.tags.setdefault(t, {"w": [], "r": {}})
            rec["r"][(eng, dom)] = me
        ops.append(op)
        if dma:
            prev = self.rcount.get(dom, 0)
            key = (eng, dom)
            if prev > 0 and self.waited.get(key, 0) < prev:
                self.waited[key] = prev
                op["waits"].append((dom, prev))
            val = prev + 16
            self.rcount[dom] = val
            op["rv"] = val
            op["rdom"] = dom
            op["rinc"] = 16
            self.riders.setdefault(dom, []).append((op["idx"], val))
        return op

    def finalize(self, nc):
        import contextlib
        with contextlib.ExitStack() as stack:
            doms = sorted(set(self.rcount.keys()) | set(ENGINES))
            sems = {d: stack.enter_context(nc.semaphore("sem_" + d)) for d in doms}
            block = stack.enter_context(nc.Block())
            for eng in ENGINES:
                ops = self.ops[eng]
                if not ops:
                    continue

                def make_body(ops):
                    def body(e):
                        for op in ops:
                            for (pdom, val) in op["waits"]:
                                e.wait_ge(sems[pdom], val)
                            ins = op["fn"](e)
                            if op["rv"] is not None:
                                ins.then_inc(sems[op["rdom"]], op["rinc"])
                    return body
                getattr(block, eng)(make_body(ops))


def build_program():
    nc = bass.Bass("TRN2", target_bir_lowering=False)
    p = Prog()

    dx = nc.dram_tensor("x", [NSPAN, 128, PS], F32, kind="ExternalInput").ap()
    dws = {}
    for nm, shp in (("wr", [128, NPT * C]), ("wk", [128, NPT * C]),
                    ("wv", [128, NPT * C]), ("wo", [128, NPT * C]),
                    ("wq", [128, NPT * C]), ("w1", [128, NPT * 64]), ("w2", [64, C]),
                    ("a1", [128, NPT * 64]), ("a2", [64, C]), ("g1", [128, NPT * 128]),
                    ("g2", [128, C]), ("wmod", [128, 48 * C]), ("cmod", [C, 1])):
        dws[nm] = nc.dram_tensor(nm, shp, F32, kind="ExternalInput").ap()
    NV = 13
    dvecs = nc.dram_tensor("vecs", [NV, C], F32, kind="ExternalInput").ap()
    dout = nc.dram_tensor("out", [NSPAN, 128, PS], F32, kind="ExternalOutput").ap()

    def sb(name, cols, dt=BF16):
        return nc.alloc_sbuf_tensor(name, [128, cols], dt).ap()

    wr_b = sb("wr_b", NPT * C)
    wk_b = sb("wk_b", NPT * C)
    wv_b = sb("wv_b", NPT * C)
    wo_b = sb("wo_b", NPT * C)
    wq_b = sb("wq_b", NPT * C)
    w1_b = sb("w1_b", NPT * 64)
    a1_b = sb("a1_b", NPT * 64)
    g1_b = sb("g1_b", NPT * 128)
    w2_b = sb("w2_b", C)
    a2_b = w2_b
    g2_b = sb("g2_b", C)
    stage = sb("stage", C, F32)

    vecs = sb("vecs_s", NV * NPT, F32)
    silcb = sb("silcb", NPT)
    silcf = sb("silcf", NPT, F32)
    modv = sb("modv", 6 * NPT, F32)
    ca1 = sb("ca1", NPT, F32)
    cb1 = sb("cb1", NPT, F32)
    gav = sb("gav", NPT, F32)
    gbv = sb("gbv", NPT, F32)
    w0n = sb("w0n", NPT, F32)
    fvec = sb("fvec", 8, F32)
    wsc = sb("wsc", 4, F32)

    fvb = sb("fvb", 4)
    maskTP = sb("maskTP", 128)
    maskLO = sb("maskLO", 64)
    identb = sb("identb", 128)
    identf = sb("identf", 128, F32)
    onescol = sb("onescol", 1)
    bones16 = sb("bones16", NPT * 16)
    bcast1 = sb("bcast1", 128)
    bsel = sb("bsel", NPT * 128)
    zer64 = sb("zer64", 64, F32)
    zerb = sb("zerb", 1)

    stS = sb("stS", 8 * HD, F32)
    stSb = sb("stSb", 8 * HD)

    x_a = sb("x_a", PS, F32)
    slog = sb("slog", PS, F32)
    expw = sb("expw", PS, F32)
    tmpf = sb("tmpf", SPAN, F32)
    tmpf2 = sb("tmpf2", SPAN, F32)
    tmpb = sb("tmpb", SPAN)

    h_sp = sb("h_sp", NPT * (SPAN + 1))
    hcarry = sb("hcarry", NPT)
    xx_sp = sb("xx_sp", PS)
    mixA = sb("mixA", PS)
    mixB = sb("mixB", PS)
    xb_sp = sb("xb_sp", PS)
    xsq_sp = sb("xsq_sp", PS)
    ar_sp = sb("ar_sp", 2 * PS)
    bh_sp = sb("bh_sp", PS)
    kh_sp = sb("kh_sp", PS)
    kk_sp = sb("kk_sp", PS)
    kkg_sp = sb("kkg_sp", PS)
    kf_sp = sb("kf_sp", PS)
    asig_sp = sb("asig_sp", PS)
    g_sp = sb("g_sp", PS)
    c_sp = sb("c_sp", PS)
    ic_sp = sb("ic_sp", PS)
    cp_sp = sb("cp_sp", PS)
    esuf_sp = sb("esuf_sp", PS)
    lw_t = sb("lw_t", SPAN)
    lg_t = None
    rows = sb("rows", 768, F32)
    rowsb = sb("rowsb", 512)
    ascb = None

    btF_sp = kk_sp
    ktF_sp = xx_sp
    vF_sp = asig_sp
    vT_sp = xb_sp[:, 0:2 * C]
    o_sp = ic_sp
    y_sp = cp_sp
    xs_a = cp_sp[:, 0:1024]
    xs_b = cp_sp[:, 1024:2048]
    tap_sb = h_sp[:, 0:1024]
    pkk_sb = h_sp[:, 1024:2048]
    tn_sb = kkg_sp[:, 0:512]
    btT_cp = kkg_sp[:, 512:1024]
    ktT_cp = kkg_sp[:, 1024:1536]
    gG_cp = kkg_sp[:, 1536:1792]
    rEf_cp = kkg_sp[:, 1792:2048]
    ascb = tmpf2
    lg_t = lw_t
    slogb = slog.bitcast(BF16)
    expwb = expw.bitcast(BF16)
    clrep = stage.bitcast(BF16)[:, 0:CPS * 512]
    wmb = rows.bitcast(BF16)[:, 0:C]

    def psum(name, cols, dt=F32):
        return nc.alloc_psum_tensor(name, [128, cols], dt).ap()

    pb = [psum(f"pb{i}", 512) for i in range(8)]
    pbtag = [f"pb{i}" for i in range(8)]
    _pbidx = {f"pb{i}": i for i in range(8)}

    granges = [dict() for _ in range(8)]
    grrg = [dict() for _ in range(8)]

    def newgrp(*banks):
        for b in banks:
            for cell in granges[b].values():
                cell[0] = True
            granges[b].clear()

    _orig_emit = p.emit

    def _emit_hook(eng, fn, reads=(), writes=(), accum=False, dma=False):
        if eng in ("vector", "scalar"):
            for t in reads:
                if t in _pbidx and granges[_pbidx[t]]:
                    newgrp(_pbidx[t])
        return _orig_emit(eng, fn, reads=reads, writes=writes, accum=accum, dma=dma)
    p.emit = _emit_hook

    def t_mm(bi, out, lhsT, rhs, r, w_extra=(), tp=None):
        rng = (int(out.base_partition()), int(out.partition_size()))
        for (b0, sz) in granges[bi]:
            assert (b0, sz) == rng or b0 + sz <= rng[0] or rng[0] + rng[1] <= b0, \
                (bi, rng, (b0, sz))
        rg = int(lhsT.base_partition()) if tp is None else tp[0]
        first = rng not in granges[bi]
        if not first:
            assert grrg[bi][rng] == rg, (bi, rng, grrg[bi][rng], rg)
        cell = [False]
        granges[bi][rng] = cell
        grrg[bi][rng] = rg
        import traceback
        op = p.emit("tensor",
               lambda e, o=out, l=lhsT, rr=rhs, s1=first, t=tp, c=cell:
               e.matmul(o, l, rr, start=s1, stop=c[0], tile_position=t,
                        skip_group_check=True),
               reads=r, writes=[pbtag[bi]] + list(w_extra), accum=not first)
        op["dbg"] = (bi, rng, first, cell,
                     traceback.extract_stack(limit=3)[0].lineno)

    def v_tt(out, in0, in1, op, r, w, eng="vector"):
        p.emit(eng, lambda e, o=out, a=in0, b=in1, op=op:
               e.tensor_tensor(out=o, in0=a, in1=b, op=op), reads=r, writes=w)

    def v_ts(out, in0, s1, s2, op0, op1, r, w, eng="vector"):
        if s2 is None:
            p.emit(eng, lambda e, o=out, a=in0, s1=s1, op0=op0:
                   e.tensor_scalar(out=o, in0=a, scalar1=s1, scalar2=None, op0=op0),
                   reads=r, writes=w)
        else:
            p.emit(eng, lambda e, o=out, a=in0, s1=s1, s2=s2, op0=op0, op1=op1:
                   e.tensor_scalar(out=o, in0=a, scalar1=s1, scalar2=s2, op0=op0, op1=op1),
                   reads=r, writes=w)

    def v_stt(out, in0, scalar, in1, op0, op1, r, w, eng="vector"):
        p.emit(eng, lambda e, o=out, a=in0, s=scalar, b=in1, op0=op0, op1=op1:
               e.scalar_tensor_tensor(out=o, in0=a, scalar=s, in1=b, op0=op0, op1=op1),
               reads=r, writes=w)

    def v_copy(out, in_, r, w, eng="vector"):
        p.emit(eng, lambda e, o=out, a=in_: e.tensor_copy(out=o, in_=a), reads=r, writes=w)

    def s_copy(out, in_, r, w):
        p.emit("scalar", lambda e, o=out, a=in_: e.copy(out=o, in_=a), reads=r, writes=w)

    def s_act(out, in_, func, r, w, bias=0.0, scale=1.0):
        p.emit("scalar", lambda e, o=out, a=in_, f=func, bi=bias, sc=scale:
               e.activation(out=o, in_=a, func=f, bias=bi, scale=sc), reads=r, writes=w)

    def g_memset(ap, val, w, r=(), accum=False):
        p.emit("gpsimd", lambda e, a=ap, v=val: e.memset(a, v), reads=r, writes=w, accum=accum)

    def dma(eng, out, in_, r, w, slow=False):
        p.emit(eng, lambda e, o=out, a=in_, sl=slow: e.dma_start(
            out=o, in_=a, allow_slow_non_contiguous=sl),
            reads=r, writes=w, dma=True)

    g_memset(maskTP[:, :], 1.0, ["maskTP"])
    for half in (0, 64):
        p.emit("gpsimd", lambda e, hh=half: e.affine_select(
            out=maskTP[hh:hh + 64, 0:64], in_=maskTP[hh:hh + 64, 0:64],
            compare_op=ALU.is_gt, fill=0.0, base=0, pattern=[[1, 64]], channel_multiplier=-1),
            reads=["maskTP"], writes=["maskTP"])
        p.emit("gpsimd", lambda e, hh=half: e.affine_select(
            out=maskTP[hh:hh + 64, 64:128], in_=maskTP[hh:hh + 64, 64:128],
            compare_op=ALU.is_ge, fill=0.0, base=0, pattern=[[1, 64]], channel_multiplier=-1),
            reads=["maskTP"], writes=["maskTP"])
    g_memset(maskLO[:, :], 1.0, ["maskLO"])
    for half in (0, 64):
        p.emit("gpsimd", lambda e, hh=half: e.affine_select(
            out=maskLO[hh:hh + 64, 0:64], in_=maskLO[hh:hh + 64, 0:64],
            compare_op=ALU.is_gt, fill=0.0, base=0, pattern=[[-1, 64]], channel_multiplier=1),
            reads=["maskLO"], writes=["maskLO"])
    g_memset(identb[:, :], 1.0, ["identb"])
    g_memset(identf[:, :], 1.0, ["identf"])
    p.emit("gpsimd", lambda e: e.affine_select(
        out=identf[:, :], in_=identf[:, :], compare_op=ALU.is_ge, fill=0.0,
        base=0, pattern=[[-1, 128]], channel_multiplier=1), reads=["identf"], writes=["identf"])
    p.emit("gpsimd", lambda e: e.affine_select(
        out=identf[:, :], in_=identf[:, :], compare_op=ALU.is_ge, fill=0.0,
        base=0, pattern=[[1, 128]], channel_multiplier=-1), reads=["identf"], writes=["identf"])
    p.emit("gpsimd", lambda e: e.affine_select(
        out=identb[:, :], in_=identb[:, :], compare_op=ALU.is_ge, fill=0.0,
        base=0, pattern=[[-1, 128]], channel_multiplier=1), reads=["identb"], writes=["identb"])
    p.emit("gpsimd", lambda e: e.affine_select(
        out=identb[:, :], in_=identb[:, :], compare_op=ALU.is_ge, fill=0.0,
        base=0, pattern=[[1, 128]], channel_multiplier=-1), reads=["identb"], writes=["identb"])
    g_memset(onescol[:, :], 1.0, ["onescol"])
    g_memset(bones16[:, :], 0.0, ["bones16"])
    for q in range(NPT):
        g_memset(bones16[0:64, q * 16 + 2 * q: q * 16 + 2 * q + 1], 1.0,
                 ["bones16"], ["bones16"], accum=True)
        g_memset(bones16[64:128, q * 16 + 2 * q + 1: q * 16 + 2 * q + 2], 1.0,
                 ["bones16"], ["bones16"], accum=True)
    g_memset(bcast1[:, :], 0.0, ["bcast1"])
    g_memset(bcast1[0:1, :], 1.0, ["bcast1"], ["bcast1"], accum=True)
    g_memset(bsel[:, :], 1.0, ["bsel"])
    p.emit("gpsimd", lambda e: e.affine_select(
        out=bsel[0:16, :], in_=bsel[0:16, :], compare_op=ALU.is_ge, fill=0.0,
        base=0, pattern=[[1, NPT * 128]], channel_multiplier=-64),
        reads=["bsel"], writes=["bsel"])
    p.emit("gpsimd", lambda e: e.affine_select(
        out=bsel[0:16, :], in_=bsel[0:16, :], compare_op=ALU.is_ge, fill=0.0,
        base=63, pattern=[[-1, NPT * 128]], channel_multiplier=64),
        reads=["bsel"], writes=["bsel"])
    g_memset(zer64[:, :], 0.0, ["zer64"])
    g_memset(zerb[:, :], 0.0, ["zerb"])
    g_memset(fvec[:, 4:5], EPS_LN, ["cvec"])
    g_memset(fvec[:, 5:6], EPS_GN, ["cvec"])
    g_memset(fvec[:, 6:7], -0.5, ["cvec"])
    g_memset(stS[:, :], 0.0, ["stS"])
    g_memset(stSb[:, :], 0.0, ["stSb"])
    g_memset(h_sp[:, :], 0.0, ["h_sp"])

    for v in range(NV):
        dma("sync", vecs[:, v * NPT: (v + 1) * NPT],
            dvecs[v, :].rearrange("(q pp) -> pp q", pp=128), r=["vecs"], w=["vecs"],
            slow=True)
    for q in range(NPT):
        v_ts(w0n[:, q:q + 1], vecs[:, 6 * NPT + q: 6 * NPT + q + 1], -1.0, None,
             ALU.mult, None, r=["vecs"], w=["w0n"])

    dma("sync", silcf[:, :], dws["cmod"][:, 0].rearrange("(k pp) -> pp k", pp=128),
        r=["silcf"], w=["silcf"], slow=True)
    s_act(silcb[:, :], silcf[:, :], AF.Silu, r=["silcf"], w=["silcb"])

    wstg = [(stage[:, 0:C], "stage"), (expw[:, 0:C], "expwA"),
            (expw[:, C:2 * C], "expwB")]
    _wsi = [0]

    def wbuf():
        b = wstg[_wsi[0] % 3]
        _wsi[0] += 1
        return b

    def load_big(dsrc, dst_b, wtag):
        for k in range(NPT):
            buf, btag = wbuf()
            dma("gpsimd", buf, dsrc[:, k * C:(k + 1) * C], r=[btag], w=[btag])
            v_copy(dst_b[:, k * C:(k + 1) * C], buf, r=[btag], w=[wtag])

    load_big(dws["wr"], wr_b, "wr_b")
    load_big(dws["wk"], wk_b, "wk_b")
    load_big(dws["wv"], wv_b, "wv_b")
    load_big(dws["wo"], wo_b, "wo_b")
    for dsrc, dst_b, wtag, psl in (
            (dws["w1"], w1_b, "w1_b", slice(0, 512)),
            (dws["a1"], a1_b, "a1_b", slice(0, 512)),
            (dws["g1"], g1_b, "g1_b", slice(0, 1024)),
            (dws["g2"], g2_b, "g2_b", slice(0, 1024))):
        buf, btag = wbuf()
        dma("gpsimd", buf[:, psl], dsrc[:, :], r=[btag], w=[btag])
        v_copy(dst_b[:, :], buf[:, psl], r=[btag], w=[wtag])
    buf, btag = wbuf()
    dma("gpsimd", buf[0:64, 0:C], dws["w2"][:, :], r=[btag], w=[btag])
    v_copy(w2_b[0:64, :], buf[0:64, 0:C], r=[btag], w=["w2_b"])
    buf, btag = wbuf()
    dma("gpsimd", buf[64:128, 0:C], dws["a2"][:, :], r=[btag], w=[btag])
    v_copy(a2_b[64:128, :], buf[64:128, 0:C], r=[btag], w=["w2_b"])

    for k in range(NPT):
        buf, btag = wbuf()
        dma("gpsimd", buf, dws["wq"][:, k * C:(k + 1) * C], r=[btag], w=[btag])
        p.emit("vector", lambda e, k=k, buf=buf: e.reduce_sum(
            out=rows[0:128, 512 + k:513 + k], in_=buf, axis=AX,
            apply_absolute_value=True), reads=[btag], writes=["rows"], accum=(k > 0))
    p.emit("vector", lambda e: e.reduce_sum(out=rows[0:128, 520:521],
                                            in_=rows[0:128, 512:520], axis=AX),
           reads=["rows"], writes=["rows"], accum=True)
    v_copy(rowsb[0:128, 0:1], rows[0:128, 520:521], r=["rows"], w=["rowsb"])
    newgrp(6)
    t_mm(6, pb[6][0:1, 0:1], onescol[:, 0:1], rowsb[0:128, 0:1], r=["onescol", "rowsb"])
    v_ts(rows[0:1, 522:523], pb[6][0:1, 0:1], 1.0 / (C * C), EPS_FFN, ALU.mult, ALU.add,
         r=["pb6"], w=["rows"])
    s_act(rows[0:1, 523:524], rows[0:1, 522:523], AF.Ln, r=["rows"], w=["rows"])
    s_act(rows[0:1, 524:525], rows[0:1, 523:524], AF.Exp, r=["rows"], w=["rows"], scale=-1.0)
    dma("gpsimd", wsc[:, 0:1], rows[0:1, 522:523].unsqueeze(1).broadcast_to([1, 128, 1]),
        r=["rows", "wsc"], w=["wsc"])
    dma("gpsimd", wsc[:, 1:2], rows[0:1, 524:525].unsqueeze(1).broadcast_to([1, 128, 1]),
        r=["rows", "wsc"], w=["wsc"])
    for k in range(NPT):
        buf, btag = wbuf()
        dma("gpsimd", buf, dws["wq"][:, k * C:(k + 1) * C], r=[btag], w=[btag])
        v_ts(buf, buf, wsc[:, 1:2], None,
             ALU.mult, None, r=[btag, "wsc"], w=[btag])
        v_ts(buf, buf, MAGIC, MAGIC, ALU.add, ALU.subtract,
             r=[btag], w=[btag])
        v_ts(wq_b[:, k * C:(k + 1) * C], buf,
             -1.0, 1.0, ALU.max, ALU.min, r=[btag], w=["wq_b"])

    mstg = [(slog[:, 0:C], "slogA"), (slog[:, C:2 * C], "slogB")]
    mcst = [(wmb, "rows"), (xsq_sp.bitcast(BF16)[:, 0:C] if False else
             kk_sp[:, 0:C], "kk_sp")]
    for fm in range(48):
        buf, btag = mstg[fm % 2]
        cst, ctag = mcst[fm % 2]
        dma("sync", buf, dws["wmod"][:, fm * C:(fm + 1) * C], r=[btag], w=[btag])
        v_copy(cst, buf, r=[btag], w=[ctag])
        newgrp(6)
        for k in range(NPT):
            t_mm(6, pb[6][0:128, 4:5], cst[:, k * 128:(k + 1) * 128], silcb[:, k:k + 1],
                 r=[ctag, "silcb"])
        vi, q = fm // 8, fm % 8
        v_copy(modv[:, vi * NPT + q: vi * NPT + q + 1], pb[6][0:128, 4:5],
               r=["pb6"], w=["modv"])

    for q in range(NPT):
        v_ts(ca1[:, q:q + 1], modv[:, 1 * NPT + q: 1 * NPT + q + 1], 1.0, None,
             ALU.add, None, r=["modv"], w=["ca1"])
        v_ts(cb1[:, q:q + 1], modv[:, 4 * NPT + q: 4 * NPT + q + 1], 1.0, None,
             ALU.add, None, r=["modv"], w=["cb1"])
    for (vi, dst, dtag) in ((2, gav, "gav"), (5, gbv, "gbv")):
        for q in range(NPT):
            src = modv[:, vi * NPT + q: vi * NPT + q + 1]
            v_ts(fvec[:, 0:1], src, 0.0, None, ALU.is_ge, None, r=["modv"], w=["fvec"])
            v_ts(fvec[:, 1:2], src, 0.25, None, ALU.max, None, r=["modv"], w=["fvec"])
            v_ts(fvec[:, 2:3], src, -0.25, None, ALU.min, None, r=["modv"], w=["fvec"])
            v_tt(fvec[:, 3:4], fvec[:, 1:2], fvec[:, 2:3], ALU.subtract, r=["fvec"], w=["fvec"])
            v_tt(fvec[:, 3:4], fvec[:, 0:1], fvec[:, 3:4], ALU.mult, r=["fvec"], w=["fvec"])
            v_tt(dst[:, q:q + 1], fvec[:, 2:3], fvec[:, 3:4], ALU.add, r=["fvec"], w=[dtag])
    for q in range(NPT):
        v_ts(gbv[:, q:q + 1], gbv[:, q:q + 1], wsc[:, 0:1], 1.0 / 127.0,
             ALU.mult, ALU.mult, r=["gbv", "wsc"], w=["gbv"])

    g_memset(zerb[:, :], 0.0, ["zerb", "slog", "expw", "kk_sp"],
             ["slogA", "slogB", "expwA", "expwB"])

    def sp(q):
        return slice(q * SPAN, (q + 1) * SPAN)

    def hcol(q, t0):
        return q * (SPAN + 1) + 1 + t0

    for s in range(NSPAN):
        if s == 0:
            dma("sync", x_a[:, :], dx[0], r=["x_a"], w=["x_a"])

        v_copy(xb_sp[:, :], x_a[:, :], r=["x_a"], w=["xb_sp"])
        s_act(xsq_sp[:, :], x_a[:, :], AF.Square, r=["x_a"], w=["xsq_sp"])
        newgrp(7)
        for q in range(NPT):
            t_mm(7, pb[7][0:1, 0:256], onescol[:, 0:1], xb_sp[:, sp(q)],
                 r=["onescol", "xb_sp"])
        for q in range(NPT):
            t_mm(7, pb[7][0:1, 256:512], onescol[:, 0:1], xsq_sp[:, sp(q)],
                 r=["onescol", "xsq_sp"])

        def ln_rows(nparts, scale, epscol, rtag):
            v_ts(rows[0:nparts, 0:256], pb[7][0:nparts, 0:256], scale, None, ALU.mult,
                 None, r=[rtag], w=["rows"])
            v_ts(rows[0:nparts, 256:512], pb[7][0:nparts, 256:512], scale, None, ALU.mult,
                 None, r=[rtag], w=["rows"], eng="vector")
            v_tt(rows[0:nparts, 512:768], rows[0:nparts, 0:256], rows[0:nparts, 0:256],
                 ALU.mult, r=["rows"], w=["rows"])
            v_tt(rows[0:nparts, 512:768], rows[0:nparts, 256:512], rows[0:nparts, 512:768],
                 ALU.subtract, r=["rows"], w=["rows"])
            p.emit("scalar", lambda e, nparts=nparts, epscol=epscol: e.activation(
                out=rows[0:nparts, 512:768], in_=rows[0:nparts, 512:768], func=AF.Ln,
                bias=fvec[0:nparts, epscol:epscol + 1], scale=1.0),
                reads=["rows", "cvec"], writes=["rows"])
            s_act(rows[0:nparts, 512:768], rows[0:nparts, 512:768], AF.Exp,
                  r=["rows"], w=["rows"], scale=-0.5)
            v_tt(rows[0:nparts, 256:512], rows[0:nparts, 0:256], rows[0:nparts, 512:768],
                 ALU.mult, r=["rows"], w=["rows"])
            v_ts(rows[0:nparts, 256:512], rows[0:nparts, 256:512], -1.0, None, ALU.mult,
                 None, r=["rows"], w=["rows"])
            v_copy(rowsb[0:nparts, 0:256], rows[0:nparts, 512:768], r=["rows"], w=["rowsb"])
            v_copy(rowsb[0:nparts, 256:512], rows[0:nparts, 256:512], r=["rows"], w=["rowsb"])

        ln_rows(1, 1.0 / C, 4, "pb7")
        newgrp(6)
        t_mm(6, pb[6][:, 0:512], bcast1[0:1, :], rowsb[0:1, 0:512], r=["bcast1", "rowsb"])

        if s > 0:
            p.emit("vector", lambda e: e.tensor_copy(
                out=h_sp[:, :].rearrange("pp (q t) -> pp q t", t=SPAN + 1)[:, :, 0:1],
                in_=h_sp[:, :].rearrange("pp (q t) -> pp q t", t=SPAN + 1)[:, :, SPAN:SPAN + 1]),
                reads=["h_sp"], writes=["h_sp"])
        p.emit("vector", lambda e: e.tensor_tensor(
            out=slog[:, :].rearrange("pp (q t) -> pp q t", t=SPAN),
            in0=x_a[:, :].rearrange("pp (q t) -> pp q t", t=SPAN),
            in1=pb[6][:, 0:256].unsqueeze(1).broadcast_to([128, NPT, 256]),
            op=ALU.mult), reads=["x_a", "pb6"], writes=["slog"])
        p.emit("vector", lambda e: e.tensor_tensor(
            out=slog[:, :].rearrange("pp (q t) -> pp q t", t=SPAN),
            in0=slog[:, :].rearrange("pp (q t) -> pp q t", t=SPAN),
            in1=pb[6][:, 256:512].unsqueeze(1).broadcast_to([128, NPT, 256]),
            op=ALU.add), reads=["slog", "pb6"], writes=["slog"])
        for q in range(NPT):
            v_ts(h_sp[:, hcol(q, 0):hcol(q, SPAN)], slog[:, sp(q)], ca1[:, q:q + 1],
                 modv[:, 0 * NPT + q:0 * NPT + q + 1], ALU.mult, ALU.add,
                 r=["slog", "ca1", "modv"], w=["h_sp"])
        h3 = h_sp[:, :].rearrange("pp (q t) -> pp q t", t=SPAN + 1)
        p.emit("vector", lambda e: e.tensor_tensor(
            out=xx_sp[:, :].rearrange("pp (q t) -> pp q t", t=SPAN),
            in0=h3[:, :, 0:SPAN], in1=h3[:, :, 1:SPAN + 1], op=ALU.subtract),
            reads=["h_sp"], writes=["xx_sp"])

        def mk_mix(dst, vi, wtag, eng="vector"):
            for q in range(NPT):
                v_stt(dst[:, sp(q)], xx_sp[:, sp(q)],
                      vecs[:, vi * NPT + q:vi * NPT + q + 1],
                      h_sp[:, hcol(q, 0):hcol(q, SPAN)], ALU.mult, ALU.add,
                      r=["xx_sp", "vecs", "h_sp"], w=[wtag], eng=eng)

        def proj(mix, mixtag, w_b, wtag, cb):
            for d in range(NPT):
                bi = d % 2
                newgrp(bi)
                for k in range(NPT):
                    t_mm(bi, pb[bi][:, 0:256], w_b[:, k * C + d * 128: k * C + (d + 1) * 128],
                         mix[:, sp(k)], r=[wtag, mixtag])
                cb(d, pb[bi], pbtag[bi])

        mk_mix(mixB, 4, "mixB")
        newgrp(4)
        for k in range(NPT):
            t_mm(4, pb[4][0:64, 0:256], a1_b[:, k * 64:(k + 1) * 64], mixB[:, sp(k)],
                 r=["a1_b", "mixB"])
        s_copy(lw_t[64:128, 0:SPAN], pb[4][0:64, 0:256], r=["pb4"], w=["lw_t"])
        for d in range(NPT):
            bi = 4 + (d % 2)
            newgrp(bi)
            t_mm(bi, pb[bi][:, 256:512], a2_b[64:128, d * 128:(d + 1) * 128],
                 lw_t[64:128, 0:SPAN], r=["w2_b", "lw_t"], tp=(64, 0))
            p.emit("scalar", lambda e, d=d, bank=pb[bi]: e.activation(
                out=asig_sp[:, sp(d)], in_=bank[:, 256:512], func=AF.Sigmoid,
                bias=vecs[:, 7 * NPT + d:7 * NPT + d + 1], scale=1.0),
                reads=[pbtag[bi], "vecs"], writes=["asig_sp"])

        mk_mix(mixA, 1, "mixA")
        newgrp(2)
        for k in range(NPT):
            t_mm(2, pb[2][0:64, 0:256], w1_b[:, k * 64:(k + 1) * 64], mixA[:, sp(k)],
                 r=["w1_b", "mixA"])
        s_act(lw_t[0:64, 0:SPAN], pb[2][0:64, 0:256], AF.Tanh, r=["pb2"], w=["lw_t"])
        for d in range(NPT):
            bi = 2 + (d % 2)
            newgrp(bi)
            t_mm(bi, pb[bi][:, 256:512], w2_b[0:64, d * 128:(d + 1) * 128],
                 lw_t[0:64, 0:SPAN], r=["w2_b", "lw_t"])
            p.emit("scalar", lambda e, d=d, bank=pb[bi]: e.activation(
                out=expw[:, sp(d)], in_=bank[:, 256:512], func=AF.Exp,
                bias=w0n[:, d:d + 1], scale=-1.0), reads=[pbtag[bi], "w0n"],
                writes=["expw"])
        mk_mix(mixB, 2, "mixB")
        mk_mix(mixA, 3, "mixA")
        s_act(slog[:, :], expw[:, :], AF.Ln, r=["expw"], w=["slog"], bias=1.0)
        p.emit("scalar", lambda e: e.activation(
            out=expw[:, :], in_=slog[:, :], func=AF.Exp,
            bias=fvec[:, 6:7], scale=-1.0), reads=["slog", "cvec"], writes=["expw"])
        p.emit("vector", lambda e: e.tensor_tensor_scan(
            out=slog[:, :], data0=expw[:, :],
            data1=zer64[:, 0:1].broadcast_to([128, PS]),
            initial=0.0, op0=ALU.add, op1=ALU.add),
            reads=["expw", "slog"], writes=["slog"])
        slogc = slog[:, :].rearrange("pp (n t) -> pp n t", t=CHUNK)
        NCH = PS // CHUNK
        v_copy(tmpf[:, 0:NCH - 1], slogc[:, 0:NCH - 1, CHUNK - 1:CHUNK],
               r=["slog"], w=["tmpf"])
        p.emit("vector", lambda e: e.tensor_tensor(
            out=slogc[:, 1:NCH, :], in0=slogc[:, 1:NCH, :],
            in1=tmpf[:, 0:NCH - 1].unsqueeze(2).broadcast_to([128, NCH - 1, CHUNK]),
            op=ALU.subtract), reads=["slog", "tmpf"], writes=["slog"])
        s_act(c_sp[:, :], slog[:, :], AF.Exp, r=["slog"], w=["c_sp"], scale=-1.0)
        s_act(ic_sp[:, :], slog[:, :], AF.Exp, r=["slog"], w=["ic_sp"])
        v_tt(expw[:, :], expw[:, :], slog[:, :], ALU.subtract,
             r=["expw", "slog"], w=["expw"])
        s_act(cp_sp[:, :], expw[:, :], AF.Exp, r=["expw"], w=["cp_sp"])
        slog3 = slog[:, :].rearrange("pp (n t) -> pp n t", t=CHUNK)
        p.emit("vector", lambda e: e.tensor_tensor(
            out=expw[:, :].rearrange("pp (n t) -> pp n t", t=CHUNK),
            in0=slog3,
            in1=slog3[:, :, CHUNK - 1:CHUNK].broadcast_to([128, PS // CHUNK, CHUNK]),
            op=ALU.subtract), reads=["slog"], writes=["expw"])
        s_act(esuf_sp[:, :], expw[:, :], AF.Exp, r=["expw"], w=["esuf_sp"])
        for ch in range(CPS):
            src = c_sp[:, :].rearrange("pp (q t) -> pp q t", t=SPAN)[
                :, :, ch * CHUNK + 63:ch * CHUNK + 64]
            p.emit("vector", lambda e, ch=ch, src=src: e.tensor_copy(
                out=clrep[:, ch * 512:(ch + 1) * 512].rearrange("pp (q t) -> pp q t", t=64),
                in_=src.broadcast_to([128, NPT, 64])),
                reads=["c_sp"], writes=["stage"])

        def k_evac(d, bank, btag):
            v_ts(kk_sp[:, sp(d)], bank[:, 0:256], vecs[:, 8 * NPT + d:8 * NPT + d + 1],
                 None, ALU.mult, None, r=[btag, "vecs"], w=["kk_sp"])
            v_ts(tmpb[:, 0:256], asig_sp[:, sp(d)], -1.0, None, ALU.add, None,
                 r=["asig_sp"], w=["tmpb"])
            v_ts(tmpb[:, 0:256], tmpb[:, 0:256], vecs[:, 9 * NPT + d:9 * NPT + d + 1],
                 1.0, ALU.mult, ALU.add, r=["tmpb", "vecs"], w=["tmpb"])
            v_tt(kf_sp[:, sp(d)], bank[:, 0:256], tmpb[:, 0:256], ALU.mult,
                 r=[btag, "tmpb"], w=["kf_sp"])
        proj(mixB, "mixB", wk_b, "wk_b", k_evac)
        mk_mix(mixB, 5, "mixB")

        for tt in range(2):
            for half in range(2):
                bi = half
                newgrp(bi)
                for k in range(NPT):
                    t_mm(bi, pb[bi][:, 0:512],
                         mixA[:, k * SPAN + tt * 128: k * SPAN + (tt + 1) * 128],
                         wv_b[:, k * C + half * 512: k * C + (half + 1) * 512],
                         r=["mixA", "wv_b"])
                s_copy(vT_sp[:, tt * C + half * 512: tt * C + (half + 1) * 512],
                       pb[bi][:, 0:512], r=[pbtag[bi]], w=["xb_sp"])
        mk_mix(mixA, 0, "mixA")

        newgrp(2)
        for k in range(NPT):
            t_mm(2, pb[2][:, 0:256], g1_b[:, k * 128:(k + 1) * 128], mixB[:, sp(k)],
                 r=["g1_b", "mixB"])
        s_act(lg_t[:, :], pb[2][:, 0:256], AF.Sigmoid, r=["pb2"], w=["lw_t"])
        for d in range(NPT):
            bi = 2 + (d % 2)
            newgrp(bi)
            t_mm(bi, pb[bi][:, 256:512], g2_b[:, d * 128:(d + 1) * 128], lg_t[:, :],
                 r=["g2_b", "lw_t"])
            s_copy(g_sp[:, sp(d)], pb[bi][:, 256:512], r=[pbtag[bi]], w=["g_sp"])

        def r_evac(d, bank, btag):
            v_tt(ar_sp[:, PS + d * SPAN: PS + (d + 1) * SPAN], bank[:, 0:256],
                 c_sp[:, sp(d)], ALU.mult, r=[btag, "c_sp"], w=["ar_sp"])
        proj(mixA, "mixA", wr_b, "wr_b", r_evac)

        s_act(xsq_sp[:, :], kk_sp[:, :], AF.Square, r=["kk_sp"], w=["xsq_sp"])
        newgrp(7)
        for q in range(NPT):
            t_mm(7, pb[7][0:16, 0:256], bones16[:, q * 16:(q + 1) * 16], xsq_sp[:, sp(q)],
                 r=["bones16", "xsq_sp"])
        v_ts(rows[0:16, 0:256], pb[7][0:16, 0:256], 1e-24, None, ALU.max, None,
             r=["pb7"], w=["rows"])
        s_act(rows[0:16, 0:256], rows[0:16, 0:256], AF.Ln, r=["rows"], w=["rows"])
        s_act(rowsb[0:16, 0:256], rows[0:16, 0:256], AF.Exp, r=["rows"], w=["rowsb"],
              scale=-0.5)
        for q in range(NPT):
            newgrp(6)
            t_mm(6, pb[6][:, 0:256], bsel[0:16, q * 128:(q + 1) * 128],
                 rowsb[0:16, 0:256], r=["bsel", "rowsb"])
            v_tt(kk_sp[:, sp(q)], kk_sp[:, sp(q)], pb[6][:, 0:256], ALU.mult,
                 r=["kk_sp", "pb6"], w=["kk_sp"])

        v_stt(ar_sp[:, 0:PS], kk_sp[:, :], -1.0, cp_sp[:, :],
              ALU.mult, ALU.mult, r=["kk_sp", "cp_sp"], w=["ar_sp"])
        v_tt(kkg_sp[:, :], kk_sp[:, :], g_sp[:, :], ALU.mult,
             r=["kk_sp", "g_sp"], w=["kkg_sp"])
        v_tt(btF_sp[:, :], kkg_sp[:, :], esuf_sp[:, :], ALU.mult,
             r=["kkg_sp", "esuf_sp"], w=["kk_sp"], eng="gpsimd")
        v_tt(ktF_sp[:, :], kf_sp[:, :], esuf_sp[:, :], ALU.mult,
             r=["kf_sp", "esuf_sp"], w=["xx_sp"], eng="gpsimd")
        v_tt(bh_sp[:, :], kkg_sp[:, :], ic_sp[:, :], ALU.mult,
             r=["kkg_sp", "ic_sp"], w=["bh_sp"])
        v_tt(kh_sp[:, :], kf_sp[:, :], ic_sp[:, :], ALU.mult,
             r=["kf_sp", "ic_sp"], w=["kh_sp"])

        xsqb = xsq_sp
        sqplan = [(0, -2, -1), (1, -1, -2), (2, 1, 0), (3, 0, 1),
                  (4, 3, 2), (5, 2, 3), (6, 5, 4), (7, 4, 5), (8, 7, 6)]
        rounds = [sqplan[0:2], sqplan[2:4], sqplan[4:6], sqplan[6:8], sqplan[8:9]]

        pairs8 = []
        for _j in range(4):
            pairs8 += [(2 * _j, 0), (2 * _j + 1, 1), (2 * _j, 1), (2 * _j + 1, 0)]

        def wres(wi):
            P = wi % 2
            b = [4 * P, 4 * P + 1, 4 * P + 2, 4 * P + 3]
            if P == 0:
                return dict(P=P, b=b, tap=tap_sb, pkk=pkk_sb, tapt="h_sp",
                            stg=kkg_sp, stgt="kkg_sp",
                            xs=[xs_a, xs_b], xst="cp_sp")
            return dict(P=P, b=b, tap=esuf_sp[:, 0:1024], pkk=esuf_sp[:, 1024:2048],
                        tapt="esuf_sp", stg=kf_sp, stgt="kf_sp",
                        xs=[c_sp[:, 0:1024], c_sp[:, 1024:2048]], xst="c_sp")

        def pw(R, hi, k):
            if R["P"] == 0:
                if k == 8:
                    return xsqb[:, hi * 64: hi * 64 + 64], "xsq_sp"
                return slogb[:, hi * 512 + k * 64: hi * 512 + (k + 1) * 64], "slog"
            if k == 8:
                return expwb[:, hi * 512 + 384: hi * 512 + 448], "expw"
            if k >= 6:
                return (xsqb[:, 512 + hi * 128 + (k - 6) * 64:
                             512 + hi * 128 + (k - 5) * 64], "xsq_sp")
            return expwb[:, hi * 512 + k * 64: hi * 512 + (k + 1) * 64], "expw"

        def pwdst(R, par, dk0, nt):
            if R["P"] == 0:
                if dk0 == 8:
                    return (xsqb[:, 0:512].rearrange("pp (j x) -> pp j x", x=64)
                            [:, par:8:2, 0:64], "xsq_sp")
                return (slogb[:, :].rearrange("pp (j x) -> pp j x", x=512)
                        [:, par:8:2, dk0 * 64:(dk0 + nt) * 64], "slog")
            if dk0 == 8:
                return (expwb[:, :].rearrange("pp (j x) -> pp j x", x=512)
                        [:, par:8:2, 384:448], "expw")
            if dk0 >= 6:
                return (xsqb[:, 512:1536].rearrange("pp (j x) -> pp j x", x=128)
                        [:, par:8:2, (dk0 - 6) * 64:(dk0 - 6 + nt) * 64], "xsq_sp")
            return (expwb[:, :].rearrange("pp (j x) -> pp j x", x=512)
                    [:, par:8:2, dk0 * 64:(dk0 + nt) * 64], "expw")

        def front_gen(wi, tt, wv):
            R = wres(wi)
            b0, b1, b2, b3 = R["b"]
            H0 = wv * 8
            stg, stgt, tapt = R["stg"], R["stgt"], R["tapt"]
            tap, pkk = R["tap"], R["pkk"]
            xsA, xst = R["xs"][0], R["xst"]

            def tiles(hi, cp):
                h = H0 + hi
                q, hh = h // 2, (h % 2) * 64
                tc = q * SPAN + (tt * 2 + cp) * CHUNK
                return h, hh, cp * 64, tc

            newgrp(b0, b1, b2, b3)
            for hi, cp in pairs8:
                par, slot = hi % 2, hi // 2
                h, hh, ch, tc = tiles(hi, cp)
                bsl = bh_sp[hh:hh + 64, tc:tc + 64]
                asl = ar_sp[hh:hh + 64, tc:tc + 64]
                arr = ar_sp[hh:hh + 64, :].rearrange(
                    "pp (u v) -> pp u v", u=2)[:, :, tc:tc + 64]
                t_mm([b0, b1][par], pb[[b0, b1][par]][ch:ch + 64, slot * 128:(slot + 1) * 128],
                     bsl, arr, r=["bh_sp", "ar_sp"], tp=(hh, ch))
            for hi, cp in pairs8:
                par, slot = hi % 2, hi // 2
                h, hh, ch, tc = tiles(hi, cp)
                bsl = bh_sp[hh:hh + 64, tc:tc + 64]
                asl = ar_sp[hh:hh + 64, tc:tc + 64]
                t_mm([b2, b3][par], pb[[b2, b3][par]][ch:ch + 64, slot * 64:(slot + 1) * 64],
                     asl, bsl, r=["ar_sp", "bh_sp"], tp=(hh, ch))
            yield
            for par in range(2):
                p.emit("vector", lambda e, par=par: e.tensor_tensor(
                    out=tap.rearrange("pp (j u f) -> pp j u f", u=2, f=128)[:, 0:4, par, :],
                    in0=pb[[b0, b1][par]][:, 0:512].rearrange("pp (j f) -> pp j f", f=128),
                    in1=maskTP[:, 0:128].unsqueeze(1).broadcast_to([128, 4, 128]),
                    op=ALU.mult),
                    reads=[pbtag[[b0, b1][par]], "maskTP"], writes=[tapt])
                p.emit("vector", lambda e, par=par: e.tensor_tensor(
                    out=stg[:, 0:512].rearrange("pp (j u f) -> pp j u f", u=2, f=64)[:, 0:4, par, :],
                    in0=pb[[b2, b3][par]][:, 0:256].rearrange("pp (j f) -> pp j f", f=64),
                    in1=maskLO[:, 0:64].unsqueeze(1).broadcast_to([128, 4, 64]),
                    op=ALU.mult),
                    reads=[pbtag[[b2, b3][par]], "maskLO"], writes=[stgt])
            yield
            newgrp(b0, b1)
            for hi, cp in pairs8:
                par, slot = hi % 2, hi // 2
                h, hh, ch, tc = tiles(hi, cp)
                ksl = kh_sp[hh:hh + 64, tc:tc + 64]
                arr = ar_sp[hh:hh + 64, :].rearrange(
                    "pp (u v) -> pp u v", u=2)[:, :, tc:tc + 64]
                t_mm([b0, b1][par], pb[[b0, b1][par]][ch:ch + 64, slot * 128:(slot + 1) * 128],
                     ksl, arr, r=["kh_sp", "ar_sp"], tp=(hh, ch))
            yield
            for par in range(2):
                p.emit("vector", lambda e, par=par: e.tensor_tensor(
                    out=pkk.rearrange("pp (j u f) -> pp j u f", u=2, f=128)[:, 0:4, par, :],
                    in0=pb[[b0, b1][par]][:, 0:512].rearrange("pp (j f) -> pp j f", f=128),
                    in1=maskTP[:, 0:128].unsqueeze(1).broadcast_to([128, 4, 128]),
                    op=ALU.mult),
                    reads=[pbtag[[b0, b1][par]], "maskTP"], writes=[tapt])
            yield
            newgrp(b2, b3)
            for hi, cp in pairs8:
                par, slot = hi % 2, hi // 2
                h, hh, ch, tc = tiles(hi, cp)
                t_mm([b2, b3][par], pb[[b2, b3][par]][ch:ch + 64, slot * 64:(slot + 1) * 64],
                     ar_sp[hh:hh + 64, tc:tc + 64], identb[hh:hh + 64, hh:hh + 64],
                     r=["ar_sp", "identb"], tp=(hh, ch))
            for hi, cp in pairs8:
                par, slot = hi % 2, hi // 2
                h, hh, ch, tc = tiles(hi, cp)
                t_mm([b2, b3][par], pb[[b2, b3][par]][ch:ch + 64, 256 + slot * 64: 256 + (slot + 1) * 64],
                     btF_sp[hh:hh + 64, tc:tc + 64], identb[hh:hh + 64, hh:hh + 64],
                     r=["kk_sp", "identb"], tp=(hh, ch))
            newgrp(b0, b1)
            for hi, cp in pairs8:
                ch = cp * 64
                h = H0 + hi
                hh = (h % 2) * 64
                t_mm([b0, b1][cp], pb[[b0, b1][cp]][hh:hh + 64, (hi // 2) * 64:(hi // 2 + 1) * 64],
                     vT_sp[ch:ch + 64, tt * C + h * 64:tt * C + (h + 1) * 64],
                     identb[ch:ch + 64, ch:ch + 64],
                     r=["xb_sp", "identb"], tp=(ch, hh))
            yield
            for par in range(2):
                eng = "scalar" if par == 0 else "vector"
                cfn = (lambda e: e.copy) if par == 0 else (lambda e: e.tensor_copy)
                p.emit(eng, lambda e, par=par, cfn=cfn: cfn(e)(
                    out=xsA.rearrange("pp (j u f) -> pp j u f", u=2, f=128)[:, 0:4, par, 0:64],
                    in_=pb[[b2, b3][par]][:, 0:256].rearrange("pp (j f) -> pp j f", f=64)),
                    reads=[pbtag[[b2, b3][par]]], writes=[xst])
                p.emit(eng, lambda e, par=par, cfn=cfn: cfn(e)(
                    out=stg[:, 512:1024].rearrange("pp (j u f) -> pp j u f", u=2, f=64)[:, 0:4, par, :],
                    in_=pb[[b2, b3][par]][:, 256:512].rearrange("pp (j f) -> pp j f", f=64)),
                    reads=[pbtag[[b2, b3][par]]], writes=[stgt])
            for cp in range(2):
                tcc = (tt * 2 + cp) * CHUNK
                eng = "scalar" if cp == 0 else "vector"
                cfn = (lambda e: e.copy) if cp == 0 else (lambda e: e.tensor_copy)
                p.emit(eng, lambda e, cp=cp, tcc=tcc, cfn=cfn: cfn(e)(
                    out=vF_sp.rearrange("pp (j f) -> pp j f", f=SPAN)
                        [:, H0 // 2: H0 // 2 + 4, tcc:tcc + 64],
                    in_=pb[[b0, b1][cp]][:, 0:256]),
                    reads=[pbtag[[b0, b1][cp]]], writes=["asig_sp"])
            yield
            newgrp(b0, b1, b2, b3)
            for hi, cp in pairs8:
                par, slot = hi % 2, hi // 2
                h, hh, ch, tc = tiles(hi, cp)
                t_mm([b0, b1][par], pb[[b0, b1][par]][ch:ch + 64, slot * 64:(slot + 1) * 64],
                     pkk[ch:ch + 64, hi * 128: hi * 128 + 64],
                     vT_sp[ch:ch + 64, tt * C + h * 64:tt * C + (h + 1) * 64],
                     r=[tapt, "xb_sp"], tp=(ch, ch))
            for hi, cp in pairs8:
                par, slot = hi % 2, hi // 2
                h, hh, ch, tc = tiles(hi, cp)
                t_mm([b2, b3][par], pb[[b2, b3][par]][ch:ch + 64, slot * 64:(slot + 1) * 64],
                     ktF_sp[hh:hh + 64, tc:tc + 64], identb[hh:hh + 64, hh:hh + 64],
                     r=["xx_sp", "identb"], tp=(hh, ch))
            yield
            for par in range(2):
                eng = "scalar" if par == 0 else "vector"
                cfn = (lambda e: e.copy) if par == 0 else (lambda e: e.tensor_copy)
                p.emit(eng, lambda e, par=par, cfn=cfn: cfn(e)(
                    out=xsA.rearrange("pp (j u f) -> pp j u f", u=2, f=128)[:, 0:4, par, 64:128],
                    in_=pb[[b0, b1][par]][:, 0:256].rearrange("pp (j f) -> pp j f", f=64)),
                    reads=[pbtag[[b0, b1][par]]], writes=[xst])
                p.emit(eng, lambda e, par=par, cfn=cfn: cfn(e)(
                    out=stg[:, 1024:1536].rearrange("pp (j u f) -> pp j u f", u=2, f=64)[:, 0:4, par, :],
                    in_=pb[[b2, b3][par]][:, 256:512].rearrange("pp (j f) -> pp j f", f=64)
                    if False else
                    pb[[b2, b3][par]][:, 0:256].rearrange("pp (j f) -> pp j f", f=64)),
                    reads=[pbtag[[b2, b3][par]]], writes=[stgt])
            yield

        def back_gen(wi, tt, wv):
            R = wres(wi)
            b0, b1, b2, b3 = R["b"]
            H0 = wv * 8
            stg, stgt, tapt = R["stg"], R["stgt"], R["tapt"]
            tap, pkk = R["tap"], R["pkk"]
            xst = R["xst"]
            xstate = [R["xs"][0], R["xs"][1]]

            def optile(hi, k):
                if k == -1:
                    return (lambda ch: tap[ch:ch + 64, hi * 128:hi * 128 + 64]), tapt
                if k == -2:
                    return (lambda ch: stg[ch:ch + 64, hi * 64:(hi + 1) * 64]), stgt
                ap, tg = pw(R, hi, k)
                return (lambda ch, ap=ap: ap[ch:ch + 64, :]), tg

            def emit_sq(rnd):
                newgrp(b0, b1)
                nt = len(rnd)
                for ti in range(nt):
                    dk, lk, rk = rnd[ti]
                    for hi, cp in pairs8:
                        par = hi % 2
                        lf, lt = optile(hi, lk)
                        rf, rt = optile(hi, rk)
                        ch = cp * 64
                        t_mm([b0, b1][par], pb[[b0, b1][par]][ch:ch + 64,
                             (hi // 2) * 128 + ti * 64:(hi // 2) * 128 + (ti + 1) * 64],
                             lf(ch), rf(ch), r=[lt, rt], tp=(ch, ch))
                dk0 = rnd[0][0]
                for par in range(2):
                    eng = "scalar" if par == 0 else "vector"
                    dst, dtag = pwdst(R, par, dk0, nt)

                    def ev(e, par=par, nt=nt, dst=dst):
                        src = pb[[b0, b1][par]][:, 0:512].rearrange(
                            "pp (j f) -> pp j f", f=128)[:, 0:4, 0:nt * 64]
                        fn = e.copy if par == 0 else e.tensor_copy
                        return fn(out=dst, in_=src)
                    p.emit(eng, ev, reads=[pbtag[[b0, b1][par]]], writes=[dtag])
                yield

            def emit_app(ak):
                xsrc, xdst = xstate
                newgrp(b2, b3)
                for hi, cp in pairs8:
                    par = hi % 2
                    lf, lt = optile(hi, ak) if ak != -1 else (
                        (lambda ch, hi=hi: tap[ch:ch + 64, hi * 128:hi * 128 + 64]), tapt)
                    ch = cp * 64
                    t_mm([b2, b3][par], pb[[b2, b3][par]][ch:ch + 64,
                         (hi // 2) * 128:(hi // 2) * 128 + 128],
                         lf(ch), xsrc[ch:ch + 64, hi * 128:(hi + 1) * 128],
                         r=[lt, xst], tp=(ch, ch))
                for par in range(2):
                    p.emit("vector", lambda e, par=par, xsrc=xsrc, xdst=xdst: e.tensor_tensor(
                        out=xdst.rearrange("pp (j u f) -> pp j u f", u=2, f=128)[:, 0:4, par, :],
                        in0=pb[[b2, b3][par]][:, 0:512].rearrange("pp (j f) -> pp j f", f=128),
                        in1=xsrc.rearrange("pp (j u f) -> pp j u f", u=2, f=128)[:, 0:4, par, :],
                        op=ALU.add),
                        reads=[pbtag[[b2, b3][par]], xst], writes=[xst])
                xstate[0], xstate[1] = xdst, xsrc
                yield

            yield from emit_sq(rounds[0])
            yield from emit_app(-1)
            yield from emit_app(0)
            yield from emit_sq(rounds[1])
            yield from emit_app(2)
            yield from emit_sq(rounds[2])
            yield from emit_app(4)
            yield from emit_sq(rounds[3])
            yield from emit_app(6)
            yield from emit_sq(rounds[4])
            yield from emit_app(8)
            xsrc = xstate[0]

            for cp in range(2):
                ch = cp * 64
                tcc = (tt * 2 + cp) * CHUNK
                gchunk_off = (tt * 2 + cp) * 512
                newgrp(b0, b1)
                for hi, _cpu in pairs8:
                    h = H0 + hi
                    hh = (h % 2) * 64
                    UA = xsrc[ch:ch + 64, hi * 128: hi * 128 + 64]
                    if _cpu == 0:
                        t_mm(b0, pb[b0][hh:hh + 64, (hi // 2) * 64:(hi // 2 + 1) * 64], UA,
                             stg[ch:ch + 64, 512 + hi * 64:512 + (hi + 1) * 64],
                             r=[xst, stgt], tp=(ch, hh))
                    else:
                        t_mm(b1, pb[b1][hh:hh + 64, (hi // 2) * 64:(hi // 2 + 1) * 64], UA,
                             tap[ch:ch + 64, hi * 128 + 64: hi * 128 + 128],
                             r=[xst, tapt], tp=(ch, hh))
                yield
                s_copy(stg[:, 1536:1792], pb[b0][:, 0:256], r=[pbtag[b0]], w=[stgt])
                rsl = ar_sp[:, PS:].rearrange("pp (j f) -> pp j f", f=SPAN)[
                    :, H0 // 2: H0 // 2 + 4, tcc:tcc + 64]
                v_tt(stg[:, 1792:2048].rearrange("pp (j f) -> pp j f", f=64),
                     pb[b1][:, 0:256].rearrange("pp (j f) -> pp j f", f=64),
                     rsl, ALU.add, r=[pbtag[b1], "ar_sp"], w=[stgt])
                newgrp(b2, b3)
                for hi in range(8):
                    h = H0 + hi
                    hh = (h % 2) * 64
                    UL = xsrc[ch:ch + 64, hi * 128 + 64: hi * 128 + 128]
                    t_mm(b2, pb[b2][hh:hh + 64, (hi // 2) * 64:(hi // 2 + 1) * 64],
                         stg[ch:ch + 64, 512 + hi * 64:512 + (hi + 1) * 64], UL,
                         r=[stgt, xst], tp=(ch, hh))
                    t_mm(b2, pb[b2][hh:hh + 64, (hi // 2) * 64:(hi // 2 + 1) * 64],
                         stg[ch:ch + 64, 1024 + hi * 64:1024 + (hi + 1) * 64],
                         vT_sp[ch:ch + 64, tt * C + h * 64:tt * C + (h + 1) * 64],
                         r=[stgt, "xb_sp"], tp=(ch, hh))
                    t_mm(b3, pb[b3][hh:hh + 64, (hi // 2) * 64:(hi // 2 + 1) * 64], UL,
                         tap[ch:ch + 64, hi * 128 + 64: hi * 128 + 128],
                         r=[xst, tapt], tp=(ch, hh))
                    t_mm(b3, pb[b3][hh:hh + 64, (hi // 2) * 64:(hi // 2 + 1) * 64],
                         vT_sp[ch:ch + 64, tt * C + h * 64:tt * C + (h + 1) * 64],
                         pkk[ch:ch + 64, hi * 128 + 64: hi * 128 + 128],
                         r=["xb_sp", tapt], tp=(ch, hh))
                yield
                newgrp(b0, b1)
                for j in range(4):
                    for he, ho in ((2 * j, 2 * j + 1), (2 * j + 1, 2 * j)):
                        h = H0 + he
                        hh = (h % 2) * 64
                        pr = h // 2
                        t_mm(b0, pb[b0][hh:hh + 64, (he // 2) * 64:(he // 2 + 1) * 64],
                             stg[hh:hh + 64, 1536 + (he // 2) * 64:1536 + (he // 2 + 1) * 64],
                             stSb[hh:hh + 64, pr * 64:(pr + 1) * 64],
                             r=[stgt, "stSb"], tp=(hh, hh))
                        h2 = H0 + ho
                        hh2 = (h2 % 2) * 64
                        pr2 = h2 // 2
                        t_mm(b1, pb[b1][hh2:hh2 + 64, (ho // 2) * 64:(ho // 2 + 1) * 64],
                             stSb[hh2:hh2 + 64, pr2 * 64:(pr2 + 1) * 64],
                             stg[hh2:hh2 + 64, 1792 + (ho // 2) * 64:1792 + (ho // 2 + 1) * 64],
                             r=["stSb", stgt], tp=(hh2, hh2))
                yield
                pr0 = H0 // 2
                ssl = slice(pr0 * 64, (pr0 + 4) * 64)
                csl = clrep[:, gchunk_off + pr0 * 64: gchunk_off + (pr0 + 4) * 64]
                v_tt(stS[:, ssl], stS[:, ssl], csl, ALU.mult,
                     r=["stS", "stage"], w=["stS"])
                v_tt(stS[:, ssl], pb[b2][:, 0:256], stS[:, ssl], ALU.add,
                     r=[pbtag[b2], "stS"], w=["stS"])
                v_tt(stS[:, ssl], pb[b0][:, 0:256], stS[:, ssl], ALU.add,
                     r=[pbtag[b0], "stS"], w=["stS"])
                v_copy(stSb[:, ssl], stS[:, ssl], r=["stS"], w=["stSb"])
                p.emit("vector", lambda e, tcc=tcc, b3=b3: e.tensor_copy(
                    out=o_sp.rearrange("pp (j f) -> pp j f", f=SPAN)
                        [:, H0 // 2: H0 // 2 + 4, tcc:tcc + 64],
                    in_=pb[b3][:, 0:256]),
                    reads=[pbtag[b3]], writes=["ic_sp"])
                p.emit("vector", lambda e, tcc=tcc, b1=b1: e.tensor_tensor(
                    out=o_sp.rearrange("pp (j f) -> pp j f", f=SPAN)
                        [:, H0 // 2: H0 // 2 + 4, tcc:tcc + 64],
                    in0=pb[b1][:, 0:256].rearrange("pp (j f) -> pp j f", f=64),
                    in1=o_sp.rearrange("pp (j f) -> pp j f", f=SPAN)
                        [:, H0 // 2: H0 // 2 + 4, tcc:tcc + 64],
                    op=ALU.add),
                    reads=[pbtag[b1], "ic_sp"], writes=["ic_sp"])
                yield

        wavelist = [(tt, wv) for tt in range(2) for wv in range(2)]
        _S = object()
        prev_back = None
        for wi, (tt, wv) in enumerate(wavelist):
            f = front_gen(wi, tt, wv)
            if prev_back is None:
                for _ in f:
                    pass
            else:
                fa, ba = True, True
                while fa or ba:
                    if ba:
                        ba = next(prev_back, _S) is not _S
                    if fa:
                        fa = next(f, _S) is not _S
            prev_back = back_gen(wi, tt, wv)
        for _ in prev_back:
            pass

        s_act(xsq_sp[:, :], o_sp[:, :], AF.Square, r=["ic_sp"], w=["xsq_sp"])
        newgrp(7)
        for q in range(NPT):
            t_mm(7, pb[7][0:16, 0:256], bones16[:, q * 16:(q + 1) * 16], o_sp[:, sp(q)],
                 r=["bones16", "ic_sp"])
        for q in range(NPT):
            t_mm(7, pb[7][0:16, 256:512], bones16[:, q * 16:(q + 1) * 16], xsq_sp[:, sp(q)],
                 r=["bones16", "xsq_sp"])
        ln_rows(16, 1.0 / HD, 5, "pb7")
        v_tt(xb_sp[:, :], ar_sp[:, PS:2 * PS], kh_sp[:, :], ALU.mult,
             r=["ar_sp", "kh_sp"], w=["xb_sp"])
        p.emit("vector", lambda e: e.tensor_tensor(
            out=xsq_sp[:, :].rearrange("pp (q t) -> pp q t", t=SPAN),
            in0=xb_sp[:, :].rearrange("pp (q t) -> pp q t", t=SPAN),
            in1=vecs[:, 10 * NPT:11 * NPT].unsqueeze(2).broadcast_to([128, NPT, SPAN]),
            op=ALU.mult), reads=["xb_sp", "vecs"], writes=["xsq_sp"])
        newgrp(6)
        for q in range(NPT):
            t_mm(6, pb[6][0:16, 0:256], bones16[:, q * 16:(q + 1) * 16], xsq_sp[:, sp(q)],
                 r=["bones16", "xsq_sp"])
        s_copy(lw_t[0:16, 0:256], pb[6][0:16, 0:256], r=["pb6"], w=["lw_t"])
        for q in range(NPT):
            newgrp(4, 5)
            rsel = rowsb[0:16, :].rearrange("pp (u f) -> pp u f", f=256)
            t_mm(4, pb[4][:, 0:512], bsel[0:16, q * 128:(q + 1) * 128], rsel,
                 r=["bsel", "rowsb"])
            t_mm(5, pb[5][:, 0:256], bsel[0:16, q * 128:(q + 1) * 128],
                 lw_t[0:16, 0:256], r=["bsel", "lw_t"])
            v_tt(tmpf[:, 0:256], o_sp[:, sp(q)], pb[4][:, 0:256], ALU.mult,
                 r=["ic_sp", "pb4"], w=["tmpf"])
            v_tt(tmpf[:, 0:256], tmpf[:, 0:256], pb[4][:, 256:512], ALU.add,
                 r=["tmpf", "pb4"], w=["tmpf"])
            v_ts(tmpf[:, 0:256], tmpf[:, 0:256], vecs[:, 11 * NPT + q:11 * NPT + q + 1],
                 vecs[:, 12 * NPT + q:12 * NPT + q + 1], ALU.mult, ALU.add,
                 r=["tmpf", "vecs"], w=["tmpf"])
            v_tt(tmpb[:, 0:256], vF_sp[:, sp(q)], pb[5][:, 0:256], ALU.mult,
                 r=["asig_sp", "pb5"], w=["tmpb"])
            v_tt(tmpf[:, 0:256], tmpf[:, 0:256], tmpb[:, 0:256], ALU.add,
                 r=["tmpf", "tmpb"], w=["tmpf"])
            v_tt(y_sp[:, sp(q)], tmpf[:, 0:256], g_sp[:, sp(q)], ALU.mult,
                 r=["tmpf", "g_sp"], w=["cp_sp"])

        for d in range(NPT):
            bi = d % 2
            newgrp(bi)
            for k in range(NPT):
                t_mm(bi, pb[bi][:, 0:256], wo_b[:, k * C + d * 128: k * C + (d + 1) * 128],
                     y_sp[:, sp(k)], r=["wo_b", "cp_sp"])
            v_stt(x_a[:, sp(d)], pb[bi][:, 0:256], gav[:, d:d + 1], x_a[:, sp(d)],
                  ALU.mult, ALU.add, r=[pbtag[bi], "gav", "x_a"], w=["x_a"])

        v_copy(xb_sp[:, :], x_a[:, :], r=["x_a"], w=["xb_sp"])
        s_act(xsq_sp[:, :], x_a[:, :], AF.Square, r=["x_a"], w=["xsq_sp"])
        newgrp(7)
        for q in range(NPT):
            t_mm(7, pb[7][0:1, 0:256], onescol[:, 0:1], xb_sp[:, sp(q)],
                 r=["onescol", "xb_sp"])
        for q in range(NPT):
            t_mm(7, pb[7][0:1, 256:512], onescol[:, 0:1], xsq_sp[:, sp(q)],
                 r=["onescol", "xsq_sp"])
        ln_rows(1, 1.0 / C, 4, "pb7")
        newgrp(6)
        t_mm(6, pb[6][:, 0:512], bcast1[0:1, :], rowsb[0:1, 0:512], r=["bcast1", "rowsb"])
        p.emit("vector", lambda e: e.tensor_tensor(
            out=slog[:, :].rearrange("pp (q t) -> pp q t", t=SPAN),
            in0=x_a[:, :].rearrange("pp (q t) -> pp q t", t=SPAN),
            in1=pb[6][:, 0:256].unsqueeze(1).broadcast_to([128, NPT, 256]),
            op=ALU.mult), reads=["x_a", "pb6"], writes=["slog"])
        p.emit("vector", lambda e: e.tensor_tensor(
            out=slog[:, :].rearrange("pp (q t) -> pp q t", t=SPAN),
            in0=slog[:, :].rearrange("pp (q t) -> pp q t", t=SPAN),
            in1=pb[6][:, 256:512].unsqueeze(1).broadcast_to([128, NPT, 256]),
            op=ALU.add), reads=["slog", "pb6"], writes=["slog"])
        for q in range(NPT):
            v_ts(xb_sp[:, sp(q)], slog[:, sp(q)], cb1[:, q:q + 1],
                 modv[:, 3 * NPT + q:3 * NPT + q + 1], ALU.mult, ALU.add,
                 r=["slog", "cb1", "modv"], w=["xb_sp"])
        s_act(xsq_sp[:, :], xb_sp[:, :], AF.Square, r=["xb_sp"], w=["xsq_sp"])
        v_tt(tmpf[:, 0:256], xsq_sp[:, sp(0)], xsq_sp[:, sp(1)], ALU.max,
             r=["xsq_sp"], w=["tmpf"])
        v_tt(tmpf2[:, 0:256], xsq_sp[:, sp(2)], xsq_sp[:, sp(3)], ALU.max,
             r=["xsq_sp"], w=["tmpf2"])
        v_tt(rows[:, 0:256], xsq_sp[:, sp(4)], xsq_sp[:, sp(5)], ALU.max,
             r=["xsq_sp"], w=["rows"])
        v_tt(rows[:, 256:512], xsq_sp[:, sp(6)], xsq_sp[:, sp(7)], ALU.max,
             r=["xsq_sp"], w=["rows"])
        v_tt(tmpf[:, 0:256], tmpf[:, 0:256], tmpf2[:, 0:256], ALU.max,
             r=["tmpf", "tmpf2"], w=["tmpf"])
        v_tt(rows[:, 0:256], rows[:, 0:256], rows[:, 256:512], ALU.max,
             r=["rows"], w=["rows"])
        v_tt(tmpf[:, 0:256], tmpf[:, 0:256], rows[:, 0:256], ALU.max,
             r=["tmpf", "rows"], w=["tmpf"])
        newgrp(6)
        for half in range(2):
            t_mm(6, pb[6][:, half * 128:(half + 1) * 128],
                 tmpf[:, half * 128:(half + 1) * 128], identf[:, :],
                 r=["tmpf", "identf"])
        for half in range(2):
            p.emit("vector", lambda e, half=half: e.reduce_max(
                out=fvec[:, 2 + half:3 + half], in_=pb[6][:, half * 128:(half + 1) * 128],
                axis=AX), reads=["pb6"], writes=["fvec"])
        s_act(fvec[:, 2:4], fvec[:, 2:4], AF.Ln, r=["fvec"], w=["fvec"])
        s_act(fvec[:, 2:4], fvec[:, 2:4], AF.Exp, r=["fvec"], w=["fvec"], scale=0.5)
        v_ts(fvec[:, 2:4], fvec[:, 2:4], EPS_FFN, None, ALU.add, None,
             r=["fvec"], w=["fvec"])
        s_act(fvec[:, 0:2], fvec[:, 2:4], AF.Ln, r=["fvec"], w=["fvec"])
        s_act(fvec[:, 0:2], fvec[:, 0:2], AF.Exp, r=["fvec"], w=["fvec"], scale=-1.0)
        v_ts(fvec[:, 0:2], fvec[:, 0:2], 127.0, None, ALU.mult, None,
             r=["fvec"], w=["fvec"])
        v_copy(fvb[:, 0:4], fvec[:, 0:4], r=["fvec"], w=["fvb"])
        newgrp(5)
        for cc in range(4):
            t_mm(5, pb[5][0:1, cc * 128:(cc + 1) * 128], fvb[:, cc:cc + 1],
                 identb[:, 0:128], r=["fvb", "identb"])
        s_copy(rowsb[0:1, 0:512], pb[5][0:1, 0:512], r=["pb5"], w=["rowsb"])
        newgrp(6)
        for cc in range(4):
            t_mm(6, pb[6][:, cc * 128:(cc + 1) * 128], bcast1[0:1, :],
                 rowsb[0:1, cc * 128:(cc + 1) * 128], r=["bcast1", "rowsb"])
        p.emit("vector", lambda e: e.tensor_tensor(
            out=slog[:, :].rearrange("pp (q t) -> pp q t", t=SPAN),
            in0=xb_sp[:, :].rearrange("pp (q t) -> pp q t", t=SPAN),
            in1=pb[6][:, 0:256].unsqueeze(1).broadcast_to([128, NPT, 256]),
            op=ALU.mult), reads=["xb_sp", "pb6"], writes=["slog"])
        v_ts(xsq_sp[:, :], slog[:, :], MAGIC, MAGIC, ALU.add, ALU.subtract,
             r=["slog"], w=["xsq_sp"])
        v_copy(ascb[:, 0:256], pb[6][:, 256:512], r=["pb6"], w=["tmpf2"])
        for d in range(NPT):
            bi = d % 2
            newgrp(bi)
            for k in range(NPT):
                t_mm(bi, pb[bi][:, 0:256], wq_b[:, k * C + d * 128: k * C + (d + 1) * 128],
                     xsq_sp[:, sp(k)], r=["wq_b", "xsq_sp"])
            v_tt(tmpf[:, 0:256], pb[bi][:, 0:256], ascb[:, 0:256], ALU.mult,
                 r=[pbtag[bi], "tmpf2"], w=["tmpf"])
            v_stt(slog[:, sp(d)], tmpf[:, 0:256], gbv[:, d:d + 1], x_a[:, sp(d)],
                  ALU.mult, ALU.add, r=["tmpf", "gbv", "x_a"], w=["slog"])
        if s + 1 < NSPAN:
            dma("sync", x_a[:, :], dx[s + 1], r=["x_a"], w=["x_a"])
        dma("sync", dout[s], slog[:, :], r=["slog"], w=["OUT"])

    newgrp(*range(8))
    p.emit("sync", lambda e: e.nop(), reads=["OUT"], writes=[])
    p.finalize(nc)
    return nc


_NC_CACHE = {}


def lay_w(W):
    return np.ascontiguousarray(
        W.reshape(8, 128, 8, 128).transpose(1, 0, 2, 3).reshape(128, NPT * C))


def lay_lora(W, cols):
    return np.ascontiguousarray(
        W.reshape(8, 128, cols).transpose(1, 0, 2).reshape(128, NPT * cols))


def lay_wmod(W):
    return np.ascontiguousarray(
        W.reshape(8, 128, 48, 128).transpose(1, 2, 0, 3).reshape(128, 48 * C))


def lay_x(xb):
    return np.ascontiguousarray(
        xb.reshape(NSPAN, SPAN, 8, 128).transpose(0, 3, 2, 1).reshape(NSPAN, 128, PS))


def unlay_out(arr):
    return np.ascontiguousarray(
        arr.reshape(NSPAN, 128, 8, SPAN).transpose(0, 3, 2, 1).reshape(T, C))


def kernel(**inputs):
    if "nc" not in _NC_CACHE:
        _NC_CACHE["nc"] = build_program()
    nc = _NC_CACHE["nc"]

    att = lambda n: np.asarray(inputs["L0_attn_" + n])
    vec_rows = np.stack([
        att("x_r").reshape(C), att("x_w").reshape(C), att("x_k").reshape(C),
        att("x_v").reshape(C), att("x_a").reshape(C), att("x_g").reshape(C),
        att("w0").reshape(C), att("a0").reshape(C),
        att("k_k").reshape(C), att("k_a").reshape(C),
        att("r_k").reshape(C),
        att("ln_gamma").reshape(C), att("ln_beta").reshape(C),
    ]).astype(np.float32)

    base = {
        "wr": lay_w(np.asarray(att("Wr"), np.float32)),
        "wk": lay_w(np.asarray(att("Wk"), np.float32)),
        "wv": lay_w(np.asarray(att("Wv"), np.float32)),
        "wo": lay_w(np.asarray(att("Wo"), np.float32)),
        "wq": lay_w(np.asarray(np.asarray(inputs["L0_ffn_weight"]).T, np.float32)),
        "w1": lay_lora(np.asarray(att("w1"), np.float32), 64),
        "w2": np.ascontiguousarray(att("w2"), np.float32),
        "a1": lay_lora(np.asarray(att("a1"), np.float32), 64),
        "a2": np.ascontiguousarray(att("a2"), np.float32),
        "g1": lay_lora(np.asarray(att("g1"), np.float32), 128),
        "g2": np.ascontiguousarray(att("g2"), np.float32),
        "wmod": lay_wmod(np.asarray(np.asarray(inputs["L0_wmod"]), np.float32)),
        "vecs": vec_rows,
    }
    x = np.asarray(inputs["L0_attn_x"])
    cmod = np.asarray(inputs["L0_c"])
    in_maps = []
    for b in range(8):
        m = dict(base)
        m["x"] = lay_x(np.asarray(x[b], np.float32))
        m["cmod"] = np.ascontiguousarray(cmod[b].reshape(C, 1), np.float32)
        in_maps.append(m)

    res = run_bass_kernel_spmd(nc, in_maps, core_ids=list(range(8)))
    out = np.stack([unlay_out(np.asarray(res.results[b]["out"]))
                    for b in range(8)])
    return out.astype(np.float32)


# version 33
# speedup vs baseline: 1.1095x; 1.0623x over previous
import numpy as np
import concourse.bass as bass
import concourse.mybir as mybir
import bass_rust
from concourse.bass_utils import run_bass_kernel_spmd

F32 = mybir.dt.float32
BF16 = mybir.dt.bfloat16
AF = mybir.ActivationFunctionType
ALU = mybir.AluOpType
AX = mybir.AxisListType.X
MAGIC = 1.5 * 2.0 ** 23
REDMAX = bass_rust.ReduceOp.max

C = 1024
T = 2048
HD = 64
SPAN = 256
NSPAN = T // SPAN
CHUNK = 64
CPS = SPAN // CHUNK
NPT = C // 128
PS = NPT * SPAN
EPS_LN = 1e-5
EPS_GN = 0.00064
EPS_FFN = 1e-8

ENGINES = ("tensor", "vector", "scalar", "gpsimd", "sync")


class Prog:
    NDMASLOT = 4

    def __init__(self):
        self.ops = {e: [] for e in ENGINES}
        self.rcount = {}
        self.riders = {}
        self.tags = {}
        self.waited = {}
        self.dslot = {e: 0 for e in ENGINES}

    def _cover(self, peng, pdom, pidx):
        lst = self.riders.setdefault(pdom, [])
        for i, v in lst:
            if i >= pidx:
                return v
        isdma = pdom.startswith("dma_")
        ops = self.ops[peng]
        j = len(ops) - 1
        while ops[j]["dma"] != isdma or ops[j]["rv"] is not None:
            j -= 1
        op = ops[j]
        assert op["idx"] >= pidx, (peng, pdom, pidx)
        inc = 16 if pdom.startswith("dma_") else 1
        val = self.rcount.get(pdom, 0) + inc
        self.rcount[pdom] = val
        op["rv"] = val
        op["rdom"] = pdom
        op["rinc"] = inc
        lst.append((op["idx"], val))
        return val

    def emit(self, eng, fn, reads=(), writes=(), accum=False, dma=False):
        if dma:
            slot = self.dslot[eng]
            self.dslot[eng] = (slot + 1) % self.NDMASLOT
            dom = f"dma_{eng}{slot}"
        else:
            dom = eng
        ops = self.ops[eng]
        op = {"fn": fn, "waits": [], "rv": None, "rdom": None, "rinc": None,
              "idx": len(ops), "dma": dma}
        needs = {}

        def need(rec):
            k = (rec[0], rec[1])
            if needs.get(k, -1) < rec[2]:
                needs[k] = rec[2]

        for t in reads:
            rec = self.tags.get(t)
            if rec:
                for w in rec["w"]:
                    need(w)
        for t in writes:
            rec = self.tags.get(t)
            if rec:
                for w in rec["w"]:
                    if accum and w[0] == "tensor" and eng == "tensor":
                        continue
                    need(w)
                for r in rec["r"].values():
                    need(r)
        for (peng, pdom), pidx in sorted(needs.items(), key=lambda kv: (kv[0][0], kv[0][1])):
            if peng == eng and pdom == dom and pidx == op["idx"]:
                continue
            val = self._cover(peng, pdom, pidx)
            key = (eng, pdom)
            if self.waited.get(key, 0) < val:
                self.waited[key] = val
                op["waits"].append((pdom, val))
        me = (eng, dom, op["idx"])
        for t in writes:
            rec = self.tags.get(t)
            if rec is None or not accum:
                self.tags[t] = {"w": [me], "r": {}}
            else:
                rec["w"].append(me)
                rec["r"] = {}
        for t in reads:
            rec = self.tags.setdefault(t, {"w": [], "r": {}})
            rec["r"][(eng, dom)] = me
        ops.append(op)
        if dma:
            prev = self.rcount.get(dom, 0)
            key = (eng, dom)
            if prev > 0 and self.waited.get(key, 0) < prev:
                self.waited[key] = prev
                op["waits"].append((dom, prev))
            val = prev + 16
            self.rcount[dom] = val
            op["rv"] = val
            op["rdom"] = dom
            op["rinc"] = 16
            self.riders.setdefault(dom, []).append((op["idx"], val))
        return op

    def finalize(self, nc):
        import contextlib
        with contextlib.ExitStack() as stack:
            doms = sorted(set(self.rcount.keys()) | set(ENGINES))
            sems = {d: stack.enter_context(nc.semaphore("sem_" + d)) for d in doms}
            block = stack.enter_context(nc.Block())
            for eng in ENGINES:
                ops = self.ops[eng]
                if not ops:
                    continue

                def make_body(ops):
                    def body(e):
                        for op in ops:
                            for (pdom, val) in op["waits"]:
                                e.wait_ge(sems[pdom], val)
                            ins = op["fn"](e)
                            if op["rv"] is not None:
                                ins.then_inc(sems[op["rdom"]], op["rinc"])
                    return body
                getattr(block, eng)(make_body(ops))


def build_program():
    nc = bass.Bass("TRN2", target_bir_lowering=False)
    p = Prog()

    dx = nc.dram_tensor("x", [NSPAN, 128, PS], F32, kind="ExternalInput").ap()
    dws = {}
    for nm, shp in (("wr", [128, NPT * C]), ("wk", [128, NPT * C]),
                    ("wv", [128, NPT * C]), ("wo", [128, NPT * C]),
                    ("wq", [128, NPT * C]), ("w1", [128, NPT * 64]), ("w2", [64, C]),
                    ("a1", [128, NPT * 64]), ("a2", [64, C]), ("g1", [128, NPT * 128]),
                    ("g2", [128, C]), ("wmod", [128, 48 * C]), ("cmod", [C, 1])):
        dws[nm] = nc.dram_tensor(nm, shp, F32, kind="ExternalInput").ap()
    NV = 13
    dvecs = nc.dram_tensor("vecs", [NV, C], F32, kind="ExternalInput").ap()
    dout = nc.dram_tensor("out", [NSPAN, 128, PS], F32, kind="ExternalOutput").ap()

    def sb(name, cols, dt=BF16):
        return nc.alloc_sbuf_tensor(name, [128, cols], dt).ap()

    wr_b = sb("wr_b", NPT * C)
    wk_b = sb("wk_b", NPT * C)
    wv_b = sb("wv_b", NPT * C)
    wo_b = sb("wo_b", NPT * C)
    wq_b = sb("wq_b", NPT * C)
    w1_b = sb("w1_b", NPT * 64)
    a1_b = sb("a1_b", NPT * 64)
    g1_b = sb("g1_b", NPT * 128)
    w2_b = sb("w2_b", C)
    a2_b = w2_b
    g2_b = sb("g2_b", C)
    stage = sb("stage", C, F32)

    vecs = sb("vecs_s", NV * NPT, F32)
    silcb = sb("silcb", NPT)
    silcf = sb("silcf", NPT, F32)
    modv = sb("modv", 6 * NPT, F32)
    ca1 = sb("ca1", NPT, F32)
    cb1 = sb("cb1", NPT, F32)
    gav = sb("gav", NPT, F32)
    gbv = sb("gbv", NPT, F32)
    w0n = sb("w0n", NPT, F32)
    fvec = sb("fvec", 8, F32)
    wsc = sb("wsc", 4, F32)

    fvb = sb("fvb", 4)
    maskTP = sb("maskTP", 128)
    maskLO = sb("maskLO", 64)
    identb = sb("identb", 128)
    identf = sb("identf", 128, F32)
    onescol = sb("onescol", 1)
    bones16 = sb("bones16", NPT * 16)
    bcast1 = sb("bcast1", 128)
    bsel = sb("bsel", NPT * 128)
    zer64 = sb("zer64", 64, F32)
    zerb = sb("zerb", 1)

    stS = sb("stS", 8 * HD, F32)
    stSb = sb("stSb", 8 * HD)

    x_a = sb("x_a", PS, F32)
    slog = sb("slog", PS, F32)
    expw = sb("expw", PS, F32)
    tmpf = sb("tmpf", SPAN, F32)
    tmpf2 = sb("tmpf2", SPAN, F32)
    tmpb = sb("tmpb", SPAN)

    h_sp = sb("h_sp", NPT * (SPAN + 1))
    hcarry = sb("hcarry", NPT)
    xx_sp = sb("xx_sp", PS)
    mixA = sb("mixA", PS)
    mixB = sb("mixB", PS)
    xb_sp = sb("xb_sp", PS)
    xsq_sp = sb("xsq_sp", PS)
    ar_sp = sb("ar_sp", 2 * PS)
    bh_sp = sb("bh_sp", PS)
    kh_sp = sb("kh_sp", PS)
    kk_sp = sb("kk_sp", PS)
    kkg_sp = sb("kkg_sp", PS)
    kf_sp = sb("kf_sp", PS)
    asig_sp = sb("asig_sp", PS)
    g_sp = sb("g_sp", PS)
    c_sp = sb("c_sp", PS)
    ic_sp = sb("ic_sp", PS)
    cp_sp = sb("cp_sp", PS)
    esuf_sp = sb("esuf_sp", PS)
    lw_t = sb("lw_t", SPAN)
    lg_t = None
    rows = sb("rows", 768, F32)
    rowsb = sb("rowsb", 512)
    ascb = None

    btF_sp = kk_sp
    ktF_sp = xx_sp
    vF_sp = asig_sp
    vT_sp = xb_sp[:, 0:2 * C]
    o_sp = ic_sp
    y_sp = cp_sp
    xs_a = cp_sp[:, 0:1024]
    xs_b = cp_sp[:, 1024:2048]
    tap_sb = h_sp[:, 0:1024]
    pkk_sb = h_sp[:, 1024:2048]
    tn_sb = kkg_sp[:, 0:512]
    btT_cp = kkg_sp[:, 512:1024]
    ktT_cp = kkg_sp[:, 1024:1536]
    gG_cp = kkg_sp[:, 1536:1792]
    rEf_cp = kkg_sp[:, 1792:2048]
    ascb = tmpf2
    lg_t = lw_t
    slogb = slog.bitcast(BF16)
    expwb = expw.bitcast(BF16)
    clrep = stage.bitcast(BF16)[:, 0:CPS * 512]
    wmb = rows.bitcast(BF16)[:, 0:C]

    def psum(name, cols, dt=F32):
        return nc.alloc_psum_tensor(name, [128, cols], dt).ap()

    pb = [psum(f"pb{i}", 512) for i in range(8)]
    pbtag = [f"pb{i}" for i in range(8)]
    _pbidx = {f"pb{i}": i for i in range(8)}

    granges = [dict() for _ in range(8)]
    grrg = [dict() for _ in range(8)]

    def newgrp(*banks):
        for b in banks:
            for cell in granges[b].values():
                cell[0] = True
            granges[b].clear()

    _orig_emit = p.emit

    def _emit_hook(eng, fn, reads=(), writes=(), accum=False, dma=False):
        if eng in ("vector", "scalar"):
            for t in reads:
                if t in _pbidx and granges[_pbidx[t]]:
                    newgrp(_pbidx[t])
        return _orig_emit(eng, fn, reads=reads, writes=writes, accum=accum, dma=dma)
    p.emit = _emit_hook

    def t_mm(bi, out, lhsT, rhs, r, w_extra=(), tp=None):
        rng = (int(out.base_partition()), int(out.partition_size()))
        for (b0, sz) in granges[bi]:
            assert (b0, sz) == rng or b0 + sz <= rng[0] or rng[0] + rng[1] <= b0, \
                (bi, rng, (b0, sz))
        rg = int(lhsT.base_partition()) if tp is None else tp[0]
        first = rng not in granges[bi]
        if not first:
            assert grrg[bi][rng] == rg, (bi, rng, grrg[bi][rng], rg)
        cell = [False]
        granges[bi][rng] = cell
        grrg[bi][rng] = rg
        import traceback
        op = p.emit("tensor",
               lambda e, o=out, l=lhsT, rr=rhs, s1=first, t=tp, c=cell:
               e.matmul(o, l, rr, start=s1, stop=c[0], tile_position=t,
                        skip_group_check=True),
               reads=r, writes=[pbtag[bi]] + list(w_extra), accum=not first)
        op["dbg"] = (bi, rng, first, cell,
                     traceback.extract_stack(limit=3)[0].lineno)

    def v_tt(out, in0, in1, op, r, w, eng="vector"):
        p.emit(eng, lambda e, o=out, a=in0, b=in1, op=op:
               e.tensor_tensor(out=o, in0=a, in1=b, op=op), reads=r, writes=w)

    def v_ts(out, in0, s1, s2, op0, op1, r, w, eng="vector"):
        if s2 is None:
            p.emit(eng, lambda e, o=out, a=in0, s1=s1, op0=op0:
                   e.tensor_scalar(out=o, in0=a, scalar1=s1, scalar2=None, op0=op0),
                   reads=r, writes=w)
        else:
            p.emit(eng, lambda e, o=out, a=in0, s1=s1, s2=s2, op0=op0, op1=op1:
                   e.tensor_scalar(out=o, in0=a, scalar1=s1, scalar2=s2, op0=op0, op1=op1),
                   reads=r, writes=w)

    def v_stt(out, in0, scalar, in1, op0, op1, r, w, eng="vector"):
        p.emit(eng, lambda e, o=out, a=in0, s=scalar, b=in1, op0=op0, op1=op1:
               e.scalar_tensor_tensor(out=o, in0=a, scalar=s, in1=b, op0=op0, op1=op1),
               reads=r, writes=w)

    def v_copy(out, in_, r, w, eng="vector"):
        p.emit(eng, lambda e, o=out, a=in_: e.tensor_copy(out=o, in_=a), reads=r, writes=w)

    def s_copy(out, in_, r, w):
        p.emit("scalar", lambda e, o=out, a=in_: e.copy(out=o, in_=a), reads=r, writes=w)

    def s_act(out, in_, func, r, w, bias=0.0, scale=1.0):
        p.emit("scalar", lambda e, o=out, a=in_, f=func, bi=bias, sc=scale:
               e.activation(out=o, in_=a, func=f, bias=bi, scale=sc), reads=r, writes=w)

    def g_memset(ap, val, w, r=(), accum=False):
        p.emit("gpsimd", lambda e, a=ap, v=val: e.memset(a, v), reads=r, writes=w, accum=accum)

    def dma(eng, out, in_, r, w, slow=False):
        p.emit(eng, lambda e, o=out, a=in_, sl=slow: e.dma_start(
            out=o, in_=a, allow_slow_non_contiguous=sl),
            reads=r, writes=w, dma=True)

    g_memset(maskTP[:, :], 1.0, ["maskTP"])
    for half in (0, 64):
        p.emit("gpsimd", lambda e, hh=half: e.affine_select(
            out=maskTP[hh:hh + 64, 0:64], in_=maskTP[hh:hh + 64, 0:64],
            compare_op=ALU.is_gt, fill=0.0, base=0, pattern=[[1, 64]], channel_multiplier=-1),
            reads=["maskTP"], writes=["maskTP"])
        p.emit("gpsimd", lambda e, hh=half: e.affine_select(
            out=maskTP[hh:hh + 64, 64:128], in_=maskTP[hh:hh + 64, 64:128],
            compare_op=ALU.is_ge, fill=0.0, base=0, pattern=[[1, 64]], channel_multiplier=-1),
            reads=["maskTP"], writes=["maskTP"])
    g_memset(maskLO[:, :], 1.0, ["maskLO"])
    for half in (0, 64):
        p.emit("gpsimd", lambda e, hh=half: e.affine_select(
            out=maskLO[hh:hh + 64, 0:64], in_=maskLO[hh:hh + 64, 0:64],
            compare_op=ALU.is_gt, fill=0.0, base=0, pattern=[[-1, 64]], channel_multiplier=1),
            reads=["maskLO"], writes=["maskLO"])
    g_memset(identb[:, :], 1.0, ["identb"])
    g_memset(identf[:, :], 1.0, ["identf"])
    p.emit("gpsimd", lambda e: e.affine_select(
        out=identf[:, :], in_=identf[:, :], compare_op=ALU.is_ge, fill=0.0,
        base=0, pattern=[[-1, 128]], channel_multiplier=1), reads=["identf"], writes=["identf"])
    p.emit("gpsimd", lambda e: e.affine_select(
        out=identf[:, :], in_=identf[:, :], compare_op=ALU.is_ge, fill=0.0,
        base=0, pattern=[[1, 128]], channel_multiplier=-1), reads=["identf"], writes=["identf"])
    p.emit("gpsimd", lambda e: e.affine_select(
        out=identb[:, :], in_=identb[:, :], compare_op=ALU.is_ge, fill=0.0,
        base=0, pattern=[[-1, 128]], channel_multiplier=1), reads=["identb"], writes=["identb"])
    p.emit("gpsimd", lambda e: e.affine_select(
        out=identb[:, :], in_=identb[:, :], compare_op=ALU.is_ge, fill=0.0,
        base=0, pattern=[[1, 128]], channel_multiplier=-1), reads=["identb"], writes=["identb"])
    g_memset(onescol[:, :], 1.0, ["onescol"])
    g_memset(bones16[:, :], 0.0, ["bones16"])
    for q in range(NPT):
        g_memset(bones16[0:64, q * 16 + 2 * q: q * 16 + 2 * q + 1], 1.0,
                 ["bones16"], ["bones16"], accum=True)
        g_memset(bones16[64:128, q * 16 + 2 * q + 1: q * 16 + 2 * q + 2], 1.0,
                 ["bones16"], ["bones16"], accum=True)
    g_memset(bcast1[:, :], 0.0, ["bcast1"])
    g_memset(bcast1[0:1, :], 1.0, ["bcast1"], ["bcast1"], accum=True)
    g_memset(bsel[:, :], 1.0, ["bsel"])
    p.emit("gpsimd", lambda e: e.affine_select(
        out=bsel[0:16, :], in_=bsel[0:16, :], compare_op=ALU.is_ge, fill=0.0,
        base=0, pattern=[[1, NPT * 128]], channel_multiplier=-64),
        reads=["bsel"], writes=["bsel"])
    p.emit("gpsimd", lambda e: e.affine_select(
        out=bsel[0:16, :], in_=bsel[0:16, :], compare_op=ALU.is_ge, fill=0.0,
        base=63, pattern=[[-1, NPT * 128]], channel_multiplier=64),
        reads=["bsel"], writes=["bsel"])
    g_memset(zer64[:, :], 0.0, ["zer64"])
    g_memset(zerb[:, :], 0.0, ["zerb"])
    g_memset(fvec[:, 4:5], EPS_LN, ["cvec"])
    g_memset(fvec[:, 5:6], EPS_GN, ["cvec"])
    g_memset(fvec[:, 6:7], -0.5, ["cvec"])
    g_memset(stS[:, :], 0.0, ["stS"])
    g_memset(stSb[:, :], 0.0, ["stSb"])
    g_memset(h_sp[:, :], 0.0, ["h_sp"])

    for v in range(NV):
        dma("sync", vecs[:, v * NPT: (v + 1) * NPT],
            dvecs[v, :].rearrange("(q pp) -> pp q", pp=128), r=["vecs"], w=["vecs"],
            slow=True)
    for q in range(NPT):
        v_ts(w0n[:, q:q + 1], vecs[:, 6 * NPT + q: 6 * NPT + q + 1], -1.0, None,
             ALU.mult, None, r=["vecs"], w=["w0n"])

    dma("sync", silcf[:, :], dws["cmod"][:, 0].rearrange("(k pp) -> pp k", pp=128),
        r=["silcf"], w=["silcf"], slow=True)
    s_act(silcb[:, :], silcf[:, :], AF.Silu, r=["silcf"], w=["silcb"])

    wstg = [(stage[:, 0:C], "stage"), (expw[:, 0:C], "expwA"),
            (expw[:, C:2 * C], "expwB")]
    _wsi = [0]

    def wbuf():
        b = wstg[_wsi[0] % 3]
        _wsi[0] += 1
        return b

    def load_big(dsrc, dst_b, wtag):
        for k in range(NPT):
            buf, btag = wbuf()
            dma("gpsimd", buf, dsrc[:, k * C:(k + 1) * C], r=[btag], w=[btag])
            v_copy(dst_b[:, k * C:(k + 1) * C], buf, r=[btag], w=[wtag])

    load_big(dws["wr"], wr_b, "wr_b")
    load_big(dws["wk"], wk_b, "wk_b")
    load_big(dws["wv"], wv_b, "wv_b")
    load_big(dws["wo"], wo_b, "wo_b")
    for dsrc, dst_b, wtag, psl in (
            (dws["w1"], w1_b, "w1_b", slice(0, 512)),
            (dws["a1"], a1_b, "a1_b", slice(0, 512)),
            (dws["g1"], g1_b, "g1_b", slice(0, 1024)),
            (dws["g2"], g2_b, "g2_b", slice(0, 1024))):
        buf, btag = wbuf()
        dma("gpsimd", buf[:, psl], dsrc[:, :], r=[btag], w=[btag])
        v_copy(dst_b[:, :], buf[:, psl], r=[btag], w=[wtag])
    buf, btag = wbuf()
    dma("gpsimd", buf[0:64, 0:C], dws["w2"][:, :], r=[btag], w=[btag])
    v_copy(w2_b[0:64, :], buf[0:64, 0:C], r=[btag], w=["w2_b"])
    buf, btag = wbuf()
    dma("gpsimd", buf[64:128, 0:C], dws["a2"][:, :], r=[btag], w=[btag])
    v_copy(a2_b[64:128, :], buf[64:128, 0:C], r=[btag], w=["w2_b"])

    for k in range(NPT):
        buf, btag = wbuf()
        dma("gpsimd", buf, dws["wq"][:, k * C:(k + 1) * C], r=[btag], w=[btag])
        p.emit("vector", lambda e, k=k, buf=buf: e.reduce_sum(
            out=rows[0:128, 512 + k:513 + k], in_=buf, axis=AX,
            apply_absolute_value=True), reads=[btag], writes=["rows"], accum=(k > 0))
    p.emit("vector", lambda e: e.reduce_sum(out=rows[0:128, 520:521],
                                            in_=rows[0:128, 512:520], axis=AX),
           reads=["rows"], writes=["rows"], accum=True)
    v_copy(rowsb[0:128, 0:1], rows[0:128, 520:521], r=["rows"], w=["rowsb"])
    newgrp(6)
    t_mm(6, pb[6][0:1, 0:1], onescol[:, 0:1], rowsb[0:128, 0:1], r=["onescol", "rowsb"])
    v_ts(rows[0:1, 522:523], pb[6][0:1, 0:1], 1.0 / (C * C), EPS_FFN, ALU.mult, ALU.add,
         r=["pb6"], w=["rows"])
    s_act(rows[0:1, 523:524], rows[0:1, 522:523], AF.Ln, r=["rows"], w=["rows"])
    s_act(rows[0:1, 524:525], rows[0:1, 523:524], AF.Exp, r=["rows"], w=["rows"], scale=-1.0)
    dma("gpsimd", wsc[:, 0:1], rows[0:1, 522:523].unsqueeze(1).broadcast_to([1, 128, 1]),
        r=["rows", "wsc"], w=["wsc"])
    dma("gpsimd", wsc[:, 1:2], rows[0:1, 524:525].unsqueeze(1).broadcast_to([1, 128, 1]),
        r=["rows", "wsc"], w=["wsc"])
    for k in range(NPT):
        buf, btag = wbuf()
        dma("gpsimd", buf, dws["wq"][:, k * C:(k + 1) * C], r=[btag], w=[btag])
        v_ts(buf, buf, wsc[:, 1:2], None,
             ALU.mult, None, r=[btag, "wsc"], w=[btag])
        v_ts(buf, buf, MAGIC, MAGIC, ALU.add, ALU.subtract,
             r=[btag], w=[btag])
        v_ts(wq_b[:, k * C:(k + 1) * C], buf,
             -1.0, 1.0, ALU.max, ALU.min, r=[btag], w=["wq_b"])

    mstg = [(slog[:, 0:C], "slogA"), (slog[:, C:2 * C], "slogB")]
    mcst = [(wmb, "rows"), (xsq_sp.bitcast(BF16)[:, 0:C] if False else
             kk_sp[:, 0:C], "kk_sp")]
    for fm in range(48):
        buf, btag = mstg[fm % 2]
        cst, ctag = mcst[fm % 2]
        dma("sync", buf, dws["wmod"][:, fm * C:(fm + 1) * C], r=[btag], w=[btag])
        v_copy(cst, buf, r=[btag], w=[ctag])
        newgrp(6)
        for k in range(NPT):
            t_mm(6, pb[6][0:128, 4:5], cst[:, k * 128:(k + 1) * 128], silcb[:, k:k + 1],
                 r=[ctag, "silcb"])
        vi, q = fm // 8, fm % 8
        v_copy(modv[:, vi * NPT + q: vi * NPT + q + 1], pb[6][0:128, 4:5],
               r=["pb6"], w=["modv"])

    for q in range(NPT):
        v_ts(ca1[:, q:q + 1], modv[:, 1 * NPT + q: 1 * NPT + q + 1], 1.0, None,
             ALU.add, None, r=["modv"], w=["ca1"])
        v_ts(cb1[:, q:q + 1], modv[:, 4 * NPT + q: 4 * NPT + q + 1], 1.0, None,
             ALU.add, None, r=["modv"], w=["cb1"])
    for (vi, dst, dtag) in ((2, gav, "gav"), (5, gbv, "gbv")):
        for q in range(NPT):
            src = modv[:, vi * NPT + q: vi * NPT + q + 1]
            v_ts(fvec[:, 0:1], src, 0.0, None, ALU.is_ge, None, r=["modv"], w=["fvec"])
            v_ts(fvec[:, 1:2], src, 0.25, None, ALU.max, None, r=["modv"], w=["fvec"])
            v_ts(fvec[:, 2:3], src, -0.25, None, ALU.min, None, r=["modv"], w=["fvec"])
            v_tt(fvec[:, 3:4], fvec[:, 1:2], fvec[:, 2:3], ALU.subtract, r=["fvec"], w=["fvec"])
            v_tt(fvec[:, 3:4], fvec[:, 0:1], fvec[:, 3:4], ALU.mult, r=["fvec"], w=["fvec"])
            v_tt(dst[:, q:q + 1], fvec[:, 2:3], fvec[:, 3:4], ALU.add, r=["fvec"], w=[dtag])
    for q in range(NPT):
        v_ts(gbv[:, q:q + 1], gbv[:, q:q + 1], wsc[:, 0:1], 1.0 / 127.0,
             ALU.mult, ALU.mult, r=["gbv", "wsc"], w=["gbv"])

    g_memset(zerb[:, :], 0.0, ["zerb", "slog", "expw", "kk_sp"],
             ["slogA", "slogB", "expwA", "expwB"])

    def sp(q):
        return slice(q * SPAN, (q + 1) * SPAN)

    def hcol(q, t0):
        return q * (SPAN + 1) + 1 + t0

    for s in range(NSPAN):
        if s == 0:
            dma("sync", x_a[:, :], dx[0], r=["x_a"], w=["x_a"])

        v_copy(xb_sp[:, :], x_a[:, :], r=["x_a"], w=["xb_sp"])
        s_act(xsq_sp[:, :], x_a[:, :], AF.Square, r=["x_a"], w=["xsq_sp"])
        newgrp(7)
        for q in range(NPT):
            t_mm(7, pb[7][0:1, 0:256], onescol[:, 0:1], xb_sp[:, sp(q)],
                 r=["onescol", "xb_sp"])
        for q in range(NPT):
            t_mm(7, pb[7][0:1, 256:512], onescol[:, 0:1], xsq_sp[:, sp(q)],
                 r=["onescol", "xsq_sp"])

        def ln_rows(nparts, scale, epscol, rtag):
            v_ts(rows[0:nparts, 0:256], pb[7][0:nparts, 0:256], scale, None, ALU.mult,
                 None, r=[rtag], w=["rows"])
            v_ts(rows[0:nparts, 256:512], pb[7][0:nparts, 256:512], scale, None, ALU.mult,
                 None, r=[rtag], w=["rows"], eng="vector")
            v_tt(rows[0:nparts, 512:768], rows[0:nparts, 0:256], rows[0:nparts, 0:256],
                 ALU.mult, r=["rows"], w=["rows"])
            v_tt(rows[0:nparts, 512:768], rows[0:nparts, 256:512], rows[0:nparts, 512:768],
                 ALU.subtract, r=["rows"], w=["rows"])
            p.emit("scalar", lambda e, nparts=nparts, epscol=epscol: e.activation(
                out=rows[0:nparts, 512:768], in_=rows[0:nparts, 512:768], func=AF.Ln,
                bias=fvec[0:nparts, epscol:epscol + 1], scale=1.0),
                reads=["rows", "cvec"], writes=["rows"])
            s_act(rows[0:nparts, 512:768], rows[0:nparts, 512:768], AF.Exp,
                  r=["rows"], w=["rows"], scale=-0.5)
            v_tt(rows[0:nparts, 256:512], rows[0:nparts, 0:256], rows[0:nparts, 512:768],
                 ALU.mult, r=["rows"], w=["rows"])
            v_ts(rows[0:nparts, 256:512], rows[0:nparts, 256:512], -1.0, None, ALU.mult,
                 None, r=["rows"], w=["rows"])
            v_copy(rowsb[0:nparts, 0:256], rows[0:nparts, 512:768], r=["rows"], w=["rowsb"])
            v_copy(rowsb[0:nparts, 256:512], rows[0:nparts, 256:512], r=["rows"], w=["rowsb"])

        ln_rows(1, 1.0 / C, 4, "pb7")
        newgrp(6)
        t_mm(6, pb[6][:, 0:512], bcast1[0:1, :], rowsb[0:1, 0:512], r=["bcast1", "rowsb"])

        if s > 0:
            p.emit("vector", lambda e: e.tensor_copy(
                out=h_sp[:, :].rearrange("pp (q t) -> pp q t", t=SPAN + 1)[:, :, 0:1],
                in_=h_sp[:, :].rearrange("pp (q t) -> pp q t", t=SPAN + 1)[:, :, SPAN:SPAN + 1]),
                reads=["h_sp"], writes=["h_sp"])
        p.emit("vector", lambda e: e.tensor_tensor(
            out=slog[:, :].rearrange("pp (q t) -> pp q t", t=SPAN),
            in0=x_a[:, :].rearrange("pp (q t) -> pp q t", t=SPAN),
            in1=pb[6][:, 0:256].unsqueeze(1).broadcast_to([128, NPT, 256]),
            op=ALU.mult), reads=["x_a", "pb6"], writes=["slog"])
        p.emit("vector", lambda e: e.tensor_tensor(
            out=slog[:, :].rearrange("pp (q t) -> pp q t", t=SPAN),
            in0=slog[:, :].rearrange("pp (q t) -> pp q t", t=SPAN),
            in1=pb[6][:, 256:512].unsqueeze(1).broadcast_to([128, NPT, 256]),
            op=ALU.add), reads=["slog", "pb6"], writes=["slog"])
        for q in range(NPT):
            v_ts(h_sp[:, hcol(q, 0):hcol(q, SPAN)], slog[:, sp(q)], ca1[:, q:q + 1],
                 modv[:, 0 * NPT + q:0 * NPT + q + 1], ALU.mult, ALU.add,
                 r=["slog", "ca1", "modv"], w=["h_sp"])
        h3 = h_sp[:, :].rearrange("pp (q t) -> pp q t", t=SPAN + 1)
        p.emit("vector", lambda e: e.tensor_tensor(
            out=xx_sp[:, :].rearrange("pp (q t) -> pp q t", t=SPAN),
            in0=h3[:, :, 0:SPAN], in1=h3[:, :, 1:SPAN + 1], op=ALU.subtract),
            reads=["h_sp"], writes=["xx_sp"])

        def mk_mix(dst, vi, wtag, eng="vector"):
            for q in range(NPT):
                v_stt(dst[:, sp(q)], xx_sp[:, sp(q)],
                      vecs[:, vi * NPT + q:vi * NPT + q + 1],
                      h_sp[:, hcol(q, 0):hcol(q, SPAN)], ALU.mult, ALU.add,
                      r=["xx_sp", "vecs", "h_sp"], w=[wtag], eng=eng)

        def proj(mix, mixtag, w_b, wtag, cb):
            for d in range(NPT):
                bi = d % 2
                newgrp(bi)
                for k in range(NPT):
                    t_mm(bi, pb[bi][:, 0:256], w_b[:, k * C + d * 128: k * C + (d + 1) * 128],
                         mix[:, sp(k)], r=[wtag, mixtag])
                cb(d, pb[bi], pbtag[bi])

        mk_mix(mixB, 4, "mixB")
        newgrp(4)
        for k in range(NPT):
            t_mm(4, pb[4][0:64, 0:256], a1_b[:, k * 64:(k + 1) * 64], mixB[:, sp(k)],
                 r=["a1_b", "mixB"])
        s_copy(lw_t[64:128, 0:SPAN], pb[4][0:64, 0:256], r=["pb4"], w=["lw_t"])
        for d in range(NPT):
            bi = 4 + (d % 2)
            newgrp(bi)
            t_mm(bi, pb[bi][:, 256:512], a2_b[64:128, d * 128:(d + 1) * 128],
                 lw_t[64:128, 0:SPAN], r=["w2_b", "lw_t"], tp=(64, 0))
            p.emit("scalar", lambda e, d=d, bank=pb[bi]: e.activation(
                out=asig_sp[:, sp(d)], in_=bank[:, 256:512], func=AF.Sigmoid,
                bias=vecs[:, 7 * NPT + d:7 * NPT + d + 1], scale=1.0),
                reads=[pbtag[bi], "vecs"], writes=["asig_sp"])

        mk_mix(mixA, 1, "mixA")
        newgrp(2)
        for k in range(NPT):
            t_mm(2, pb[2][0:64, 0:256], w1_b[:, k * 64:(k + 1) * 64], mixA[:, sp(k)],
                 r=["w1_b", "mixA"])
        s_act(lw_t[0:64, 0:SPAN], pb[2][0:64, 0:256], AF.Tanh, r=["pb2"], w=["lw_t"])
        for d in range(NPT):
            bi = 2 + (d % 2)
            newgrp(bi)
            t_mm(bi, pb[bi][:, 256:512], w2_b[0:64, d * 128:(d + 1) * 128],
                 lw_t[0:64, 0:SPAN], r=["w2_b", "lw_t"])
            p.emit("scalar", lambda e, d=d, bank=pb[bi]: e.activation(
                out=expw[:, sp(d)], in_=bank[:, 256:512], func=AF.Exp,
                bias=w0n[:, d:d + 1], scale=-1.0), reads=[pbtag[bi], "w0n"],
                writes=["expw"])
        mk_mix(mixB, 2, "mixB")
        mk_mix(mixA, 3, "mixA")
        s_act(slog[:, :], expw[:, :], AF.Ln, r=["expw"], w=["slog"], bias=1.0)
        p.emit("scalar", lambda e: e.activation(
            out=expw[:, :], in_=slog[:, :], func=AF.Exp,
            bias=fvec[:, 6:7], scale=-1.0), reads=["slog", "cvec"], writes=["expw"])
        p.emit("vector", lambda e: e.tensor_tensor_scan(
            out=slog[:, :], data0=expw[:, :],
            data1=zer64[:, 0:1].broadcast_to([128, PS]),
            initial=0.0, op0=ALU.add, op1=ALU.add),
            reads=["expw", "slog"], writes=["slog"])
        slogc = slog[:, :].rearrange("pp (n t) -> pp n t", t=CHUNK)
        NCH = PS // CHUNK
        v_copy(tmpf[:, 0:NCH - 1], slogc[:, 0:NCH - 1, CHUNK - 1:CHUNK],
               r=["slog"], w=["tmpf"])
        p.emit("vector", lambda e: e.tensor_tensor(
            out=slogc[:, 1:NCH, :], in0=slogc[:, 1:NCH, :],
            in1=tmpf[:, 0:NCH - 1].unsqueeze(2).broadcast_to([128, NCH - 1, CHUNK]),
            op=ALU.subtract), reads=["slog", "tmpf"], writes=["slog"])
        s_act(c_sp[:, :], slog[:, :], AF.Exp, r=["slog"], w=["c_sp"], scale=-1.0)
        s_act(ic_sp[:, :], slog[:, :], AF.Exp, r=["slog"], w=["ic_sp"])
        v_tt(expw[:, :], expw[:, :], slog[:, :], ALU.subtract,
             r=["expw", "slog"], w=["expw"])
        s_act(cp_sp[:, :], expw[:, :], AF.Exp, r=["expw"], w=["cp_sp"])
        slog3 = slog[:, :].rearrange("pp (n t) -> pp n t", t=CHUNK)
        p.emit("vector", lambda e: e.tensor_tensor(
            out=expw[:, :].rearrange("pp (n t) -> pp n t", t=CHUNK),
            in0=slog3,
            in1=slog3[:, :, CHUNK - 1:CHUNK].broadcast_to([128, PS // CHUNK, CHUNK]),
            op=ALU.subtract), reads=["slog"], writes=["expw"])
        s_act(esuf_sp[:, :], expw[:, :], AF.Exp, r=["expw"], w=["esuf_sp"])
        for ch in range(CPS):
            src = c_sp[:, :].rearrange("pp (q t) -> pp q t", t=SPAN)[
                :, :, ch * CHUNK + 63:ch * CHUNK + 64]
            p.emit("vector", lambda e, ch=ch, src=src: e.tensor_copy(
                out=clrep[:, ch * 512:(ch + 1) * 512].rearrange("pp (q t) -> pp q t", t=64),
                in_=src.broadcast_to([128, NPT, 64])),
                reads=["c_sp"], writes=["stage"])

        def k_evac(d, bank, btag):
            v_ts(kk_sp[:, sp(d)], bank[:, 0:256], vecs[:, 8 * NPT + d:8 * NPT + d + 1],
                 None, ALU.mult, None, r=[btag, "vecs"], w=["kk_sp"])
            v_ts(tmpb[:, 0:256], asig_sp[:, sp(d)], -1.0, None, ALU.add, None,
                 r=["asig_sp"], w=["tmpb"])
            v_ts(tmpb[:, 0:256], tmpb[:, 0:256], vecs[:, 9 * NPT + d:9 * NPT + d + 1],
                 1.0, ALU.mult, ALU.add, r=["tmpb", "vecs"], w=["tmpb"])
            v_tt(kf_sp[:, sp(d)], bank[:, 0:256], tmpb[:, 0:256], ALU.mult,
                 r=[btag, "tmpb"], w=["kf_sp"])
        proj(mixB, "mixB", wk_b, "wk_b", k_evac)
        mk_mix(mixB, 5, "mixB")

        for tt in range(2):
            for half in range(2):
                bi = half
                newgrp(bi)
                for k in range(NPT):
                    t_mm(bi, pb[bi][:, 0:512],
                         mixA[:, k * SPAN + tt * 128: k * SPAN + (tt + 1) * 128],
                         wv_b[:, k * C + half * 512: k * C + (half + 1) * 512],
                         r=["mixA", "wv_b"])
                s_copy(vT_sp[:, tt * C + half * 512: tt * C + (half + 1) * 512],
                       pb[bi][:, 0:512], r=[pbtag[bi]], w=["xb_sp"])
        mk_mix(mixA, 0, "mixA")

        newgrp(2)
        for k in range(NPT):
            t_mm(2, pb[2][:, 0:256], g1_b[:, k * 128:(k + 1) * 128], mixB[:, sp(k)],
                 r=["g1_b", "mixB"])
        s_act(lg_t[:, :], pb[2][:, 0:256], AF.Sigmoid, r=["pb2"], w=["lw_t"])
        for d in range(NPT):
            bi = 2 + (d % 2)
            newgrp(bi)
            t_mm(bi, pb[bi][:, 256:512], g2_b[:, d * 128:(d + 1) * 128], lg_t[:, :],
                 r=["g2_b", "lw_t"])
            s_copy(g_sp[:, sp(d)], pb[bi][:, 256:512], r=[pbtag[bi]], w=["g_sp"])

        def r_evac(d, bank, btag):
            v_tt(ar_sp[:, PS + d * SPAN: PS + (d + 1) * SPAN], bank[:, 0:256],
                 c_sp[:, sp(d)], ALU.mult, r=[btag, "c_sp"], w=["ar_sp"])
        proj(mixA, "mixA", wr_b, "wr_b", r_evac)

        s_act(xsq_sp[:, :], kk_sp[:, :], AF.Square, r=["kk_sp"], w=["xsq_sp"])
        newgrp(7)
        for q in range(NPT):
            t_mm(7, pb[7][0:16, 0:256], bones16[:, q * 16:(q + 1) * 16], xsq_sp[:, sp(q)],
                 r=["bones16", "xsq_sp"])
        v_ts(rows[0:16, 0:256], pb[7][0:16, 0:256], 1e-24, None, ALU.max, None,
             r=["pb7"], w=["rows"])
        s_act(rows[0:16, 0:256], rows[0:16, 0:256], AF.Ln, r=["rows"], w=["rows"])
        s_act(rowsb[0:16, 0:256], rows[0:16, 0:256], AF.Exp, r=["rows"], w=["rowsb"],
              scale=-0.5)
        for q in range(NPT):
            newgrp(6)
            t_mm(6, pb[6][:, 0:256], bsel[0:16, q * 128:(q + 1) * 128],
                 rowsb[0:16, 0:256], r=["bsel", "rowsb"])
            v_tt(kk_sp[:, sp(q)], kk_sp[:, sp(q)], pb[6][:, 0:256], ALU.mult,
                 r=["kk_sp", "pb6"], w=["kk_sp"])

        v_stt(ar_sp[:, 0:PS], kk_sp[:, :], -1.0, cp_sp[:, :],
              ALU.mult, ALU.mult, r=["kk_sp", "cp_sp"], w=["ar_sp"])
        v_tt(kkg_sp[:, :], kk_sp[:, :], g_sp[:, :], ALU.mult,
             r=["kk_sp", "g_sp"], w=["kkg_sp"])
        v_tt(btF_sp[:, :], kkg_sp[:, :], esuf_sp[:, :], ALU.mult,
             r=["kkg_sp", "esuf_sp"], w=["kk_sp"], eng="gpsimd")
        v_tt(ktF_sp[:, :], kf_sp[:, :], esuf_sp[:, :], ALU.mult,
             r=["kf_sp", "esuf_sp"], w=["xx_sp"], eng="gpsimd")
        v_tt(bh_sp[:, :], kkg_sp[:, :], ic_sp[:, :], ALU.mult,
             r=["kkg_sp", "ic_sp"], w=["bh_sp"])
        v_tt(kh_sp[:, :], kf_sp[:, :], ic_sp[:, :], ALU.mult,
             r=["kf_sp", "ic_sp"], w=["kh_sp"])

        xsqb = xsq_sp
        sqplan = [(0, -2, -1), (1, -1, -2), (2, 1, 0), (3, 0, 1),
                  (4, 3, 2), (5, 2, 3), (6, 5, 4), (7, 4, 5), (8, 7, 6)]
        rounds = [sqplan[0:2], sqplan[2:4], sqplan[4:6], sqplan[6:8], sqplan[8:9]]

        pairs8 = []
        for _j in range(4):
            pairs8 += [(2 * _j, 0), (2 * _j + 1, 1), (2 * _j, 1), (2 * _j + 1, 0)]

        def wres(wi):
            P = wi % 2
            b = [4 * P, 4 * P + 1, 4 * P + 2, 4 * P + 3]
            if P == 0:
                return dict(P=P, b=b, tap=tap_sb, pkk=pkk_sb, tapt="h_sp",
                            stg=kkg_sp, stgt="kkg_sp",
                            xs=[xs_a, xs_b], xst="cp_sp")
            return dict(P=P, b=b, tap=esuf_sp[:, 0:1024], pkk=esuf_sp[:, 1024:2048],
                        tapt="esuf_sp", stg=kf_sp, stgt="kf_sp",
                        xs=[c_sp[:, 0:1024], c_sp[:, 1024:2048]], xst="c_sp")

        def pw(R, hi, k):
            if R["P"] == 0:
                if k == 8:
                    return xsqb[:, hi * 64: hi * 64 + 64], "xsq_sp"
                return slogb[:, hi * 512 + k * 64: hi * 512 + (k + 1) * 64], "slog"
            if k == 8:
                return expwb[:, hi * 512 + 384: hi * 512 + 448], "expw"
            if k >= 6:
                return (xsqb[:, 512 + hi * 128 + (k - 6) * 64:
                             512 + hi * 128 + (k - 5) * 64], "xsq_sp")
            return expwb[:, hi * 512 + k * 64: hi * 512 + (k + 1) * 64], "expw"

        def pwdst(R, par, dk0, nt):
            if R["P"] == 0:
                if dk0 == 8:
                    return (xsqb[:, 0:512].rearrange("pp (j x) -> pp j x", x=64)
                            [:, par:8:2, 0:64], "xsq_sp")
                return (slogb[:, :].rearrange("pp (j x) -> pp j x", x=512)
                        [:, par:8:2, dk0 * 64:(dk0 + nt) * 64], "slog")
            if dk0 == 8:
                return (expwb[:, :].rearrange("pp (j x) -> pp j x", x=512)
                        [:, par:8:2, 384:448], "expw")
            if dk0 >= 6:
                return (xsqb[:, 512:1536].rearrange("pp (j x) -> pp j x", x=128)
                        [:, par:8:2, (dk0 - 6) * 64:(dk0 - 6 + nt) * 64], "xsq_sp")
            return (expwb[:, :].rearrange("pp (j x) -> pp j x", x=512)
                    [:, par:8:2, dk0 * 64:(dk0 + nt) * 64], "expw")

        def front_gen(wi, tt, wv):
            R = wres(wi)
            b0, b1, b2, b3 = R["b"]
            H0 = wv * 8
            stg, stgt, tapt = R["stg"], R["stgt"], R["tapt"]
            tap, pkk = R["tap"], R["pkk"]
            xsA, xst = R["xs"][0], R["xst"]

            def tiles(hi, cp):
                h = H0 + hi
                q, hh = h // 2, (h % 2) * 64
                tc = q * SPAN + (tt * 2 + cp) * CHUNK
                return h, hh, cp * 64, tc

            newgrp(b0, b1, b2, b3)
            for hi, cp in pairs8:
                par, slot = hi % 2, hi // 2
                h, hh, ch, tc = tiles(hi, cp)
                bsl = bh_sp[hh:hh + 64, tc:tc + 64]
                asl = ar_sp[hh:hh + 64, tc:tc + 64]
                arr = ar_sp[hh:hh + 64, :].rearrange(
                    "pp (u v) -> pp u v", u=2)[:, :, tc:tc + 64]
                t_mm([b0, b1][par], pb[[b0, b1][par]][ch:ch + 64, slot * 128:(slot + 1) * 128],
                     bsl, arr, r=["bh_sp", "ar_sp"], tp=(hh, ch))
            for hi, cp in pairs8:
                par, slot = hi % 2, hi // 2
                h, hh, ch, tc = tiles(hi, cp)
                bsl = bh_sp[hh:hh + 64, tc:tc + 64]
                asl = ar_sp[hh:hh + 64, tc:tc + 64]
                t_mm([b2, b3][par], pb[[b2, b3][par]][ch:ch + 64, slot * 64:(slot + 1) * 64],
                     asl, bsl, r=["ar_sp", "bh_sp"], tp=(hh, ch))
            yield
            for par in range(2):
                p.emit("vector", lambda e, par=par: e.tensor_tensor(
                    out=tap.rearrange("pp (j u f) -> pp j u f", u=2, f=128)[:, 0:4, par, :],
                    in0=pb[[b0, b1][par]][:, 0:512].rearrange("pp (j f) -> pp j f", f=128),
                    in1=maskTP[:, 0:128].unsqueeze(1).broadcast_to([128, 4, 128]),
                    op=ALU.mult),
                    reads=[pbtag[[b0, b1][par]], "maskTP"], writes=[tapt])
                p.emit("vector", lambda e, par=par: e.tensor_tensor(
                    out=stg[:, 0:512].rearrange("pp (j u f) -> pp j u f", u=2, f=64)[:, 0:4, par, :],
                    in0=pb[[b2, b3][par]][:, 0:256].rearrange("pp (j f) -> pp j f", f=64),
                    in1=maskLO[:, 0:64].unsqueeze(1).broadcast_to([128, 4, 64]),
                    op=ALU.mult),
                    reads=[pbtag[[b2, b3][par]], "maskLO"], writes=[stgt])
            yield
            newgrp(b0, b1)
            for hi, cp in pairs8:
                par, slot = hi % 2, hi // 2
                h, hh, ch, tc = tiles(hi, cp)
                ksl = kh_sp[hh:hh + 64, tc:tc + 64]
                arr = ar_sp[hh:hh + 64, :].rearrange(
                    "pp (u v) -> pp u v", u=2)[:, :, tc:tc + 64]
                t_mm([b0, b1][par], pb[[b0, b1][par]][ch:ch + 64, slot * 128:(slot + 1) * 128],
                     ksl, arr, r=["kh_sp", "ar_sp"], tp=(hh, ch))
            yield
            for par in range(2):
                p.emit("vector", lambda e, par=par: e.tensor_tensor(
                    out=pkk.rearrange("pp (j u f) -> pp j u f", u=2, f=128)[:, 0:4, par, :],
                    in0=pb[[b0, b1][par]][:, 0:512].rearrange("pp (j f) -> pp j f", f=128),
                    in1=maskTP[:, 0:128].unsqueeze(1).broadcast_to([128, 4, 128]),
                    op=ALU.mult),
                    reads=[pbtag[[b0, b1][par]], "maskTP"], writes=[tapt])
            yield
            newgrp(b2, b3)
            for hi, cp in pairs8:
                par, slot = hi % 2, hi // 2
                h, hh, ch, tc = tiles(hi, cp)
                t_mm([b2, b3][par], pb[[b2, b3][par]][ch:ch + 64, slot * 64:(slot + 1) * 64],
                     ar_sp[hh:hh + 64, tc:tc + 64], identb[hh:hh + 64, hh:hh + 64],
                     r=["ar_sp", "identb"], tp=(hh, ch))
            for hi, cp in pairs8:
                par, slot = hi % 2, hi // 2
                h, hh, ch, tc = tiles(hi, cp)
                t_mm([b2, b3][par], pb[[b2, b3][par]][ch:ch + 64, 256 + slot * 64: 256 + (slot + 1) * 64],
                     btF_sp[hh:hh + 64, tc:tc + 64], identb[hh:hh + 64, hh:hh + 64],
                     r=["kk_sp", "identb"], tp=(hh, ch))
            newgrp(b0, b1)
            for hi, cp in pairs8:
                ch = cp * 64
                h = H0 + hi
                hh = (h % 2) * 64
                t_mm([b0, b1][cp], pb[[b0, b1][cp]][hh:hh + 64, (hi // 2) * 64:(hi // 2 + 1) * 64],
                     vT_sp[ch:ch + 64, tt * C + h * 64:tt * C + (h + 1) * 64],
                     identb[ch:ch + 64, ch:ch + 64],
                     r=["xb_sp", "identb"], tp=(ch, hh))
            yield
            for par in range(2):
                eng = "scalar" if par == 0 else "vector"
                cfn = (lambda e: e.copy) if par == 0 else (lambda e: e.tensor_copy)
                p.emit(eng, lambda e, par=par, cfn=cfn: cfn(e)(
                    out=xsA.rearrange("pp (j u f) -> pp j u f", u=2, f=128)[:, 0:4, par, 0:64],
                    in_=pb[[b2, b3][par]][:, 0:256].rearrange("pp (j f) -> pp j f", f=64)),
                    reads=[pbtag[[b2, b3][par]]], writes=[xst])
                p.emit(eng, lambda e, par=par, cfn=cfn: cfn(e)(
                    out=stg[:, 512:1024].rearrange("pp (j u f) -> pp j u f", u=2, f=64)[:, 0:4, par, :],
                    in_=pb[[b2, b3][par]][:, 256:512].rearrange("pp (j f) -> pp j f", f=64)),
                    reads=[pbtag[[b2, b3][par]]], writes=[stgt])
            for cp in range(2):
                tcc = (tt * 2 + cp) * CHUNK
                eng = "scalar" if cp == 0 else "vector"
                cfn = (lambda e: e.copy) if cp == 0 else (lambda e: e.tensor_copy)
                p.emit(eng, lambda e, cp=cp, tcc=tcc, cfn=cfn: cfn(e)(
                    out=vF_sp.rearrange("pp (j f) -> pp j f", f=SPAN)
                        [:, H0 // 2: H0 // 2 + 4, tcc:tcc + 64],
                    in_=pb[[b0, b1][cp]][:, 0:256]),
                    reads=[pbtag[[b0, b1][cp]]], writes=["asig_sp"])
            yield
            newgrp(b0, b1, b2, b3)
            for hi, cp in pairs8:
                par, slot = hi % 2, hi // 2
                h, hh, ch, tc = tiles(hi, cp)
                t_mm([b0, b1][par], pb[[b0, b1][par]][ch:ch + 64, slot * 64:(slot + 1) * 64],
                     pkk[ch:ch + 64, hi * 128: hi * 128 + 64],
                     vT_sp[ch:ch + 64, tt * C + h * 64:tt * C + (h + 1) * 64],
                     r=[tapt, "xb_sp"], tp=(ch, ch))
            for hi, cp in pairs8:
                par, slot = hi % 2, hi // 2
                h, hh, ch, tc = tiles(hi, cp)
                t_mm([b2, b3][par], pb[[b2, b3][par]][ch:ch + 64, slot * 64:(slot + 1) * 64],
                     ktF_sp[hh:hh + 64, tc:tc + 64], identb[hh:hh + 64, hh:hh + 64],
                     r=["xx_sp", "identb"], tp=(hh, ch))
            yield
            for par in range(2):
                eng = "scalar" if par == 0 else "vector"
                cfn = (lambda e: e.copy) if par == 0 else (lambda e: e.tensor_copy)
                p.emit(eng, lambda e, par=par, cfn=cfn: cfn(e)(
                    out=xsA.rearrange("pp (j u f) -> pp j u f", u=2, f=128)[:, 0:4, par, 64:128],
                    in_=pb[[b0, b1][par]][:, 0:256].rearrange("pp (j f) -> pp j f", f=64)),
                    reads=[pbtag[[b0, b1][par]]], writes=[xst])
                p.emit(eng, lambda e, par=par, cfn=cfn: cfn(e)(
                    out=stg[:, 1024:1536].rearrange("pp (j u f) -> pp j u f", u=2, f=64)[:, 0:4, par, :],
                    in_=pb[[b2, b3][par]][:, 256:512].rearrange("pp (j f) -> pp j f", f=64)
                    if False else
                    pb[[b2, b3][par]][:, 0:256].rearrange("pp (j f) -> pp j f", f=64)),
                    reads=[pbtag[[b2, b3][par]]], writes=[stgt])
            yield

        def back_gen(wi, tt, wv):
            R = wres(wi)
            b0, b1, b2, b3 = R["b"]
            H0 = wv * 8
            stg, stgt, tapt = R["stg"], R["stgt"], R["tapt"]
            tap, pkk = R["tap"], R["pkk"]
            xst = R["xst"]
            xstate = [R["xs"][0], R["xs"][1]]

            def optile(hi, k):
                if k == -1:
                    return (lambda ch: tap[ch:ch + 64, hi * 128:hi * 128 + 64]), tapt
                if k == -2:
                    return (lambda ch: stg[ch:ch + 64, hi * 64:(hi + 1) * 64]), stgt
                ap, tg = pw(R, hi, k)
                return (lambda ch, ap=ap: ap[ch:ch + 64, :]), tg

            def emit_sq(rnd):
                newgrp(b0, b1)
                nt = len(rnd)
                for ti in range(nt):
                    dk, lk, rk = rnd[ti]
                    for hi, cp in pairs8:
                        par = hi % 2
                        lf, lt = optile(hi, lk)
                        rf, rt = optile(hi, rk)
                        ch = cp * 64
                        t_mm([b0, b1][par], pb[[b0, b1][par]][ch:ch + 64,
                             (hi // 2) * 128 + ti * 64:(hi // 2) * 128 + (ti + 1) * 64],
                             lf(ch), rf(ch), r=[lt, rt], tp=(ch, ch))
                dk0 = rnd[0][0]
                for par in range(2):
                    eng = "scalar" if par == 0 else "vector"
                    dst, dtag = pwdst(R, par, dk0, nt)

                    def ev(e, par=par, nt=nt, dst=dst):
                        src = pb[[b0, b1][par]][:, 0:512].rearrange(
                            "pp (j f) -> pp j f", f=128)[:, 0:4, 0:nt * 64]
                        fn = e.copy if par == 0 else e.tensor_copy
                        return fn(out=dst, in_=src)
                    p.emit(eng, ev, reads=[pbtag[[b0, b1][par]]], writes=[dtag])
                yield

            def emit_app(ak):
                xsrc, xdst = xstate
                newgrp(b2, b3)
                for hi, cp in pairs8:
                    par = hi % 2
                    lf, lt = optile(hi, ak) if ak != -1 else (
                        (lambda ch, hi=hi: tap[ch:ch + 64, hi * 128:hi * 128 + 64]), tapt)
                    ch = cp * 64
                    t_mm([b2, b3][par], pb[[b2, b3][par]][ch:ch + 64,
                         (hi // 2) * 128:(hi // 2) * 128 + 128],
                         lf(ch), xsrc[ch:ch + 64, hi * 128:(hi + 1) * 128],
                         r=[lt, xst], tp=(ch, ch))
                for par in range(2):
                    p.emit("vector", lambda e, par=par, xsrc=xsrc, xdst=xdst: e.tensor_tensor(
                        out=xdst.rearrange("pp (j u f) -> pp j u f", u=2, f=128)[:, 0:4, par, :],
                        in0=pb[[b2, b3][par]][:, 0:512].rearrange("pp (j f) -> pp j f", f=128),
                        in1=xsrc.rearrange("pp (j u f) -> pp j u f", u=2, f=128)[:, 0:4, par, :],
                        op=ALU.add),
                        reads=[pbtag[[b2, b3][par]], xst], writes=[xst])
                xstate[0], xstate[1] = xdst, xsrc
                yield

            yield from emit_sq(rounds[0])
            yield from emit_app(-1)
            yield from emit_app(0)
            yield from emit_sq(rounds[1])
            yield from emit_app(2)
            yield from emit_sq(rounds[2])
            yield from emit_app(4)
            yield from emit_sq(rounds[3])
            yield from emit_app(6)
            yield from emit_sq(rounds[4])
            yield from emit_app(8)
            xsrc = xstate[0]

            for cp in range(2):
                ch = cp * 64
                tcc = (tt * 2 + cp) * CHUNK
                gchunk_off = (tt * 2 + cp) * 512
                newgrp(b0, b1)
                for hi, _cpu in pairs8:
                    h = H0 + hi
                    hh = (h % 2) * 64
                    UA = xsrc[ch:ch + 64, hi * 128: hi * 128 + 64]
                    if _cpu == 0:
                        t_mm(b0, pb[b0][hh:hh + 64, (hi // 2) * 64:(hi // 2 + 1) * 64], UA,
                             stg[ch:ch + 64, 512 + hi * 64:512 + (hi + 1) * 64],
                             r=[xst, stgt], tp=(ch, hh))
                    else:
                        t_mm(b1, pb[b1][hh:hh + 64, (hi // 2) * 64:(hi // 2 + 1) * 64], UA,
                             tap[ch:ch + 64, hi * 128 + 64: hi * 128 + 128],
                             r=[xst, tapt], tp=(ch, hh))
                yield
                s_copy(stg[:, 1536:1792], pb[b0][:, 0:256], r=[pbtag[b0]], w=[stgt])
                rsl = ar_sp[:, PS:].rearrange("pp (j f) -> pp j f", f=SPAN)[
                    :, H0 // 2: H0 // 2 + 4, tcc:tcc + 64]
                v_tt(stg[:, 1792:2048].rearrange("pp (j f) -> pp j f", f=64),
                     pb[b1][:, 0:256].rearrange("pp (j f) -> pp j f", f=64),
                     rsl, ALU.add, r=[pbtag[b1], "ar_sp"], w=[stgt])
                newgrp(b2, b3)
                for hi in range(8):
                    h = H0 + hi
                    hh = (h % 2) * 64
                    UL = xsrc[ch:ch + 64, hi * 128 + 64: hi * 128 + 128]
                    t_mm(b2, pb[b2][hh:hh + 64, (hi // 2) * 64:(hi // 2 + 1) * 64],
                         stg[ch:ch + 64, 512 + hi * 64:512 + (hi + 1) * 64], UL,
                         r=[stgt, xst], tp=(ch, hh))
                    t_mm(b2, pb[b2][hh:hh + 64, (hi // 2) * 64:(hi // 2 + 1) * 64],
                         stg[ch:ch + 64, 1024 + hi * 64:1024 + (hi + 1) * 64],
                         vT_sp[ch:ch + 64, tt * C + h * 64:tt * C + (h + 1) * 64],
                         r=[stgt, "xb_sp"], tp=(ch, hh))
                    t_mm(b3, pb[b3][hh:hh + 64, (hi // 2) * 64:(hi // 2 + 1) * 64], UL,
                         tap[ch:ch + 64, hi * 128 + 64: hi * 128 + 128],
                         r=[xst, tapt], tp=(ch, hh))
                    t_mm(b3, pb[b3][hh:hh + 64, (hi // 2) * 64:(hi // 2 + 1) * 64],
                         vT_sp[ch:ch + 64, tt * C + h * 64:tt * C + (h + 1) * 64],
                         pkk[ch:ch + 64, hi * 128 + 64: hi * 128 + 128],
                         r=["xb_sp", tapt], tp=(ch, hh))
                yield
                newgrp(b0, b1)
                for j in range(4):
                    for he, ho in ((2 * j, 2 * j + 1), (2 * j + 1, 2 * j)):
                        h = H0 + he
                        hh = (h % 2) * 64
                        pr = h // 2
                        t_mm(b0, pb[b0][hh:hh + 64, (he // 2) * 64:(he // 2 + 1) * 64],
                             stg[hh:hh + 64, 1536 + (he // 2) * 64:1536 + (he // 2 + 1) * 64],
                             stSb[hh:hh + 64, pr * 64:(pr + 1) * 64],
                             r=[stgt, "stSb"], tp=(hh, hh))
                        h2 = H0 + ho
                        hh2 = (h2 % 2) * 64
                        pr2 = h2 // 2
                        t_mm(b1, pb[b1][hh2:hh2 + 64, (ho // 2) * 64:(ho // 2 + 1) * 64],
                             stSb[hh2:hh2 + 64, pr2 * 64:(pr2 + 1) * 64],
                             stg[hh2:hh2 + 64, 1792 + (ho // 2) * 64:1792 + (ho // 2 + 1) * 64],
                             r=["stSb", stgt], tp=(hh2, hh2))
                yield
                pr0 = H0 // 2
                ssl = slice(pr0 * 64, (pr0 + 4) * 64)
                csl = clrep[:, gchunk_off + pr0 * 64: gchunk_off + (pr0 + 4) * 64]
                v_tt(stS[:, ssl], stS[:, ssl], csl, ALU.mult,
                     r=["stS", "stage"], w=["stS"])
                v_tt(stS[:, ssl], pb[b2][:, 0:256], stS[:, ssl], ALU.add,
                     r=[pbtag[b2], "stS"], w=["stS"])
                v_tt(stS[:, ssl], pb[b0][:, 0:256], stS[:, ssl], ALU.add,
                     r=[pbtag[b0], "stS"], w=["stS"])
                v_copy(stSb[:, ssl], stS[:, ssl], r=["stS"], w=["stSb"])
                p.emit("vector", lambda e, tcc=tcc, b3=b3: e.tensor_copy(
                    out=o_sp.rearrange("pp (j f) -> pp j f", f=SPAN)
                        [:, H0 // 2: H0 // 2 + 4, tcc:tcc + 64],
                    in_=pb[b3][:, 0:256]),
                    reads=[pbtag[b3]], writes=["ic_sp"])
                p.emit("vector", lambda e, tcc=tcc, b1=b1: e.tensor_tensor(
                    out=o_sp.rearrange("pp (j f) -> pp j f", f=SPAN)
                        [:, H0 // 2: H0 // 2 + 4, tcc:tcc + 64],
                    in0=pb[b1][:, 0:256].rearrange("pp (j f) -> pp j f", f=64),
                    in1=o_sp.rearrange("pp (j f) -> pp j f", f=SPAN)
                        [:, H0 // 2: H0 // 2 + 4, tcc:tcc + 64],
                    op=ALU.add),
                    reads=[pbtag[b1], "ic_sp"], writes=["ic_sp"])
                yield

        wavelist = [(tt, wv) for tt in range(2) for wv in range(2)]
        _S = object()
        prev_back = None
        for wi, (tt, wv) in enumerate(wavelist):
            f = front_gen(wi, tt, wv)
            if prev_back is None:
                for _ in f:
                    pass
            else:
                fa, ba = True, True
                tick = 0
                while fa or ba:
                    if ba:
                        ba = next(prev_back, _S) is not _S
                    if fa and (tick % 2 == 1 or not ba):
                        fa = next(f, _S) is not _S
                    tick += 1
            prev_back = back_gen(wi, tt, wv)
        for _ in prev_back:
            pass

        s_act(xsq_sp[:, :], o_sp[:, :], AF.Square, r=["ic_sp"], w=["xsq_sp"])
        newgrp(7)
        for q in range(NPT):
            t_mm(7, pb[7][0:16, 0:256], bones16[:, q * 16:(q + 1) * 16], o_sp[:, sp(q)],
                 r=["bones16", "ic_sp"])
        for q in range(NPT):
            t_mm(7, pb[7][0:16, 256:512], bones16[:, q * 16:(q + 1) * 16], xsq_sp[:, sp(q)],
                 r=["bones16", "xsq_sp"])
        ln_rows(16, 1.0 / HD, 5, "pb7")
        v_tt(xb_sp[:, :], ar_sp[:, PS:2 * PS], kh_sp[:, :], ALU.mult,
             r=["ar_sp", "kh_sp"], w=["xb_sp"])
        p.emit("vector", lambda e: e.tensor_tensor(
            out=xsq_sp[:, :].rearrange("pp (q t) -> pp q t", t=SPAN),
            in0=xb_sp[:, :].rearrange("pp (q t) -> pp q t", t=SPAN),
            in1=vecs[:, 10 * NPT:11 * NPT].unsqueeze(2).broadcast_to([128, NPT, SPAN]),
            op=ALU.mult), reads=["xb_sp", "vecs"], writes=["xsq_sp"])
        newgrp(6)
        for q in range(NPT):
            t_mm(6, pb[6][0:16, 0:256], bones16[:, q * 16:(q + 1) * 16], xsq_sp[:, sp(q)],
                 r=["bones16", "xsq_sp"])
        s_copy(lw_t[0:16, 0:256], pb[6][0:16, 0:256], r=["pb6"], w=["lw_t"])
        for q in range(NPT):
            newgrp(4, 5)
            rsel = rowsb[0:16, :].rearrange("pp (u f) -> pp u f", f=256)
            t_mm(4, pb[4][:, 0:512], bsel[0:16, q * 128:(q + 1) * 128], rsel,
                 r=["bsel", "rowsb"])
            t_mm(5, pb[5][:, 0:256], bsel[0:16, q * 128:(q + 1) * 128],
                 lw_t[0:16, 0:256], r=["bsel", "lw_t"])
            v_tt(tmpf[:, 0:256], o_sp[:, sp(q)], pb[4][:, 0:256], ALU.mult,
                 r=["ic_sp", "pb4"], w=["tmpf"])
            v_tt(tmpf[:, 0:256], tmpf[:, 0:256], pb[4][:, 256:512], ALU.add,
                 r=["tmpf", "pb4"], w=["tmpf"])
            v_ts(tmpf[:, 0:256], tmpf[:, 0:256], vecs[:, 11 * NPT + q:11 * NPT + q + 1],
                 vecs[:, 12 * NPT + q:12 * NPT + q + 1], ALU.mult, ALU.add,
                 r=["tmpf", "vecs"], w=["tmpf"])
            v_tt(tmpb[:, 0:256], vF_sp[:, sp(q)], pb[5][:, 0:256], ALU.mult,
                 r=["asig_sp", "pb5"], w=["tmpb"])
            v_tt(tmpf[:, 0:256], tmpf[:, 0:256], tmpb[:, 0:256], ALU.add,
                 r=["tmpf", "tmpb"], w=["tmpf"])
            v_tt(y_sp[:, sp(q)], tmpf[:, 0:256], g_sp[:, sp(q)], ALU.mult,
                 r=["tmpf", "g_sp"], w=["cp_sp"])

        for d in range(NPT):
            bi = d % 2
            newgrp(bi)
            for k in range(NPT):
                t_mm(bi, pb[bi][:, 0:256], wo_b[:, k * C + d * 128: k * C + (d + 1) * 128],
                     y_sp[:, sp(k)], r=["wo_b", "cp_sp"])
            v_stt(x_a[:, sp(d)], pb[bi][:, 0:256], gav[:, d:d + 1], x_a[:, sp(d)],
                  ALU.mult, ALU.add, r=[pbtag[bi], "gav", "x_a"], w=["x_a"])

        v_copy(xb_sp[:, :], x_a[:, :], r=["x_a"], w=["xb_sp"])
        s_act(xsq_sp[:, :], x_a[:, :], AF.Square, r=["x_a"], w=["xsq_sp"])
        newgrp(7)
        for q in range(NPT):
            t_mm(7, pb[7][0:1, 0:256], onescol[:, 0:1], xb_sp[:, sp(q)],
                 r=["onescol", "xb_sp"])
        for q in range(NPT):
            t_mm(7, pb[7][0:1, 256:512], onescol[:, 0:1], xsq_sp[:, sp(q)],
                 r=["onescol", "xsq_sp"])
        ln_rows(1, 1.0 / C, 4, "pb7")
        newgrp(6)
        t_mm(6, pb[6][:, 0:512], bcast1[0:1, :], rowsb[0:1, 0:512], r=["bcast1", "rowsb"])
        p.emit("vector", lambda e: e.tensor_tensor(
            out=slog[:, :].rearrange("pp (q t) -> pp q t", t=SPAN),
            in0=x_a[:, :].rearrange("pp (q t) -> pp q t", t=SPAN),
            in1=pb[6][:, 0:256].unsqueeze(1).broadcast_to([128, NPT, 256]),
            op=ALU.mult), reads=["x_a", "pb6"], writes=["slog"])
        p.emit("vector", lambda e: e.tensor_tensor(
            out=slog[:, :].rearrange("pp (q t) -> pp q t", t=SPAN),
            in0=slog[:, :].rearrange("pp (q t) -> pp q t", t=SPAN),
            in1=pb[6][:, 256:512].unsqueeze(1).broadcast_to([128, NPT, 256]),
            op=ALU.add), reads=["slog", "pb6"], writes=["slog"])
        for q in range(NPT):
            v_ts(xb_sp[:, sp(q)], slog[:, sp(q)], cb1[:, q:q + 1],
                 modv[:, 3 * NPT + q:3 * NPT + q + 1], ALU.mult, ALU.add,
                 r=["slog", "cb1", "modv"], w=["xb_sp"])
        s_act(xsq_sp[:, :], xb_sp[:, :], AF.Square, r=["xb_sp"], w=["xsq_sp"])
        v_tt(tmpf[:, 0:256], xsq_sp[:, sp(0)], xsq_sp[:, sp(1)], ALU.max,
             r=["xsq_sp"], w=["tmpf"])
        v_tt(tmpf2[:, 0:256], xsq_sp[:, sp(2)], xsq_sp[:, sp(3)], ALU.max,
             r=["xsq_sp"], w=["tmpf2"])
        v_tt(rows[:, 0:256], xsq_sp[:, sp(4)], xsq_sp[:, sp(5)], ALU.max,
             r=["xsq_sp"], w=["rows"])
        v_tt(rows[:, 256:512], xsq_sp[:, sp(6)], xsq_sp[:, sp(7)], ALU.max,
             r=["xsq_sp"], w=["rows"])
        v_tt(tmpf[:, 0:256], tmpf[:, 0:256], tmpf2[:, 0:256], ALU.max,
             r=["tmpf", "tmpf2"], w=["tmpf"])
        v_tt(rows[:, 0:256], rows[:, 0:256], rows[:, 256:512], ALU.max,
             r=["rows"], w=["rows"])
        v_tt(tmpf[:, 0:256], tmpf[:, 0:256], rows[:, 0:256], ALU.max,
             r=["tmpf", "rows"], w=["tmpf"])
        newgrp(6)
        for half in range(2):
            t_mm(6, pb[6][:, half * 128:(half + 1) * 128],
                 tmpf[:, half * 128:(half + 1) * 128], identf[:, :],
                 r=["tmpf", "identf"])
        for half in range(2):
            p.emit("vector", lambda e, half=half: e.reduce_max(
                out=fvec[:, 2 + half:3 + half], in_=pb[6][:, half * 128:(half + 1) * 128],
                axis=AX), reads=["pb6"], writes=["fvec"])
        s_act(fvec[:, 2:4], fvec[:, 2:4], AF.Ln, r=["fvec"], w=["fvec"])
        s_act(fvec[:, 2:4], fvec[:, 2:4], AF.Exp, r=["fvec"], w=["fvec"], scale=0.5)
        v_ts(fvec[:, 2:4], fvec[:, 2:4], EPS_FFN, None, ALU.add, None,
             r=["fvec"], w=["fvec"])
        s_act(fvec[:, 0:2], fvec[:, 2:4], AF.Ln, r=["fvec"], w=["fvec"])
        s_act(fvec[:, 0:2], fvec[:, 0:2], AF.Exp, r=["fvec"], w=["fvec"], scale=-1.0)
        v_ts(fvec[:, 0:2], fvec[:, 0:2], 127.0, None, ALU.mult, None,
             r=["fvec"], w=["fvec"])
        v_copy(fvb[:, 0:4], fvec[:, 0:4], r=["fvec"], w=["fvb"])
        newgrp(5)
        for cc in range(4):
            t_mm(5, pb[5][0:1, cc * 128:(cc + 1) * 128], fvb[:, cc:cc + 1],
                 identb[:, 0:128], r=["fvb", "identb"])
        s_copy(rowsb[0:1, 0:512], pb[5][0:1, 0:512], r=["pb5"], w=["rowsb"])
        newgrp(6)
        for cc in range(4):
            t_mm(6, pb[6][:, cc * 128:(cc + 1) * 128], bcast1[0:1, :],
                 rowsb[0:1, cc * 128:(cc + 1) * 128], r=["bcast1", "rowsb"])
        p.emit("vector", lambda e: e.tensor_tensor(
            out=slog[:, :].rearrange("pp (q t) -> pp q t", t=SPAN),
            in0=xb_sp[:, :].rearrange("pp (q t) -> pp q t", t=SPAN),
            in1=pb[6][:, 0:256].unsqueeze(1).broadcast_to([128, NPT, 256]),
            op=ALU.mult), reads=["xb_sp", "pb6"], writes=["slog"])
        v_ts(xsq_sp[:, :], slog[:, :], MAGIC, MAGIC, ALU.add, ALU.subtract,
             r=["slog"], w=["xsq_sp"])
        v_copy(ascb[:, 0:256], pb[6][:, 256:512], r=["pb6"], w=["tmpf2"])
        for d in range(NPT):
            bi = d % 2
            newgrp(bi)
            for k in range(NPT):
                t_mm(bi, pb[bi][:, 0:256], wq_b[:, k * C + d * 128: k * C + (d + 1) * 128],
                     xsq_sp[:, sp(k)], r=["wq_b", "xsq_sp"])
            v_tt(tmpf[:, 0:256], pb[bi][:, 0:256], ascb[:, 0:256], ALU.mult,
                 r=[pbtag[bi], "tmpf2"], w=["tmpf"])
            v_stt(slog[:, sp(d)], tmpf[:, 0:256], gbv[:, d:d + 1], x_a[:, sp(d)],
                  ALU.mult, ALU.add, r=["tmpf", "gbv", "x_a"], w=["slog"])
        if s + 1 < NSPAN:
            dma("sync", x_a[:, :], dx[s + 1], r=["x_a"], w=["x_a"])
        dma("sync", dout[s], slog[:, :], r=["slog"], w=["OUT"])

    newgrp(*range(8))
    p.emit("sync", lambda e: e.nop(), reads=["OUT"], writes=[])
    p.finalize(nc)
    return nc


_NC_CACHE = {}


def lay_w(W):
    return np.ascontiguousarray(
        W.reshape(8, 128, 8, 128).transpose(1, 0, 2, 3).reshape(128, NPT * C))


def lay_lora(W, cols):
    return np.ascontiguousarray(
        W.reshape(8, 128, cols).transpose(1, 0, 2).reshape(128, NPT * cols))


def lay_wmod(W):
    return np.ascontiguousarray(
        W.reshape(8, 128, 48, 128).transpose(1, 2, 0, 3).reshape(128, 48 * C))


def lay_x(xb):
    return np.ascontiguousarray(
        xb.reshape(NSPAN, SPAN, 8, 128).transpose(0, 3, 2, 1).reshape(NSPAN, 128, PS))


def unlay_out(arr):
    return np.ascontiguousarray(
        arr.reshape(NSPAN, 128, 8, SPAN).transpose(0, 3, 2, 1).reshape(T, C))


def kernel(**inputs):
    if "nc" not in _NC_CACHE:
        _NC_CACHE["nc"] = build_program()
    nc = _NC_CACHE["nc"]

    att = lambda n: np.asarray(inputs["L0_attn_" + n])
    vec_rows = np.stack([
        att("x_r").reshape(C), att("x_w").reshape(C), att("x_k").reshape(C),
        att("x_v").reshape(C), att("x_a").reshape(C), att("x_g").reshape(C),
        att("w0").reshape(C), att("a0").reshape(C),
        att("k_k").reshape(C), att("k_a").reshape(C),
        att("r_k").reshape(C),
        att("ln_gamma").reshape(C), att("ln_beta").reshape(C),
    ]).astype(np.float32)

    base = {
        "wr": lay_w(np.asarray(att("Wr"), np.float32)),
        "wk": lay_w(np.asarray(att("Wk"), np.float32)),
        "wv": lay_w(np.asarray(att("Wv"), np.float32)),
        "wo": lay_w(np.asarray(att("Wo"), np.float32)),
        "wq": lay_w(np.asarray(np.asarray(inputs["L0_ffn_weight"]).T, np.float32)),
        "w1": lay_lora(np.asarray(att("w1"), np.float32), 64),
        "w2": np.ascontiguousarray(att("w2"), np.float32),
        "a1": lay_lora(np.asarray(att("a1"), np.float32), 64),
        "a2": np.ascontiguousarray(att("a2"), np.float32),
        "g1": lay_lora(np.asarray(att("g1"), np.float32), 128),
        "g2": np.ascontiguousarray(att("g2"), np.float32),
        "wmod": lay_wmod(np.asarray(np.asarray(inputs["L0_wmod"]), np.float32)),
        "vecs": vec_rows,
    }
    x = np.asarray(inputs["L0_attn_x"])
    cmod = np.asarray(inputs["L0_c"])
    in_maps = []
    for b in range(8):
        m = dict(base)
        m["x"] = lay_x(np.asarray(x[b], np.float32))
        m["cmod"] = np.ascontiguousarray(cmod[b].reshape(C, 1), np.float32)
        in_maps.append(m)

    res = run_bass_kernel_spmd(nc, in_maps, core_ids=list(range(8)))
    out = np.stack([unlay_out(np.asarray(res.results[b]["out"]))
                    for b in range(8)])
    return out.astype(np.float32)
